# version 11
# baseline (speedup 1.0000x reference)
import numpy as np
import concourse.bass as bass
import concourse.mybir as mybir
from concourse.bass_utils import run_bass_kernel_spmd

F32 = mybir.dt.float32
BF16 = mybir.dt.bfloat16
AOP = mybir.AluOpType
ACT = mybir.ActivationFunctionType
AXX = mybir.AxisListType.X

L = 4096
LH = 2048
DM = 1024
DL = 1024
KT = 8
NST = 16
RK = 64
T = 512
NCH = L // T
SL = T + 1
CW = KT * SL
NL = 4
EPS = 1e-5

SEMS = ["S", "G", "T", "V", "A", "AD"]

_NC_CACHE = {}


class Prog:
    def __init__(self):
        self.steps = []
        self.cnt = dict.fromkeys(SEMS, 0)

    def add(self, eng, fn, waits=None, inc=None, amt=1):
        w = [(s, v) for (s, v) in (waits or []) if v > 0]
        if inc is not None:
            self.cnt[inc] += amt
        self.steps.append((eng, fn, w, inc, amt))

    def c(self, s):
        return self.cnt[s]


def build_nc(dbg=False, nl=NL):
    nc = bass.Bass("TRN2", target_bir_lowering=False, debug=False)
    KIND = "ExternalOutput" if dbg else "Internal"

    x_full = nc.dram_tensor("x_full", [L, DM], F32, kind="ExternalInput").ap()
    x_half = nc.dram_tensor("x_half", [LH, DM], F32, kind="ExternalInput").ap()
    winT, wxT, wdtT, woutT = [], [], [], []
    convw, convb, bdt, alog, dvec, normw = [], [], [], [], [], []
    for i in range(NL):
        winT.append(nc.dram_tensor(f"winT{i}", [DM, 2 * DL], F32, kind="ExternalInput").ap())
        wxT.append(nc.dram_tensor(f"wxT{i}", [DL, 96], F32, kind="ExternalInput").ap())
        wdtT.append(nc.dram_tensor(f"wdtT{i}", [RK, DL], F32, kind="ExternalInput").ap())
        woutT.append(nc.dram_tensor(f"woutT{i}", [DL, DM], F32, kind="ExternalInput").ap())
        convw.append(nc.dram_tensor(f"convw{i}", [128, 32], F32, kind="ExternalInput").ap())
        convb.append(nc.dram_tensor(f"convb{i}", [128, 8], F32, kind="ExternalInput").ap())
        bdt.append(nc.dram_tensor(f"bdt{i}", [128, 8], F32, kind="ExternalInput").ap())
        alog.append(nc.dram_tensor(f"alog{i}", [128, 128], F32, kind="ExternalInput").ap())
        dvec.append(nc.dram_tensor(f"dvec{i}", [128, 8], F32, kind="ExternalInput").ap())
        normw.append(nc.dram_tensor(f"normw{i}", [128, 8], F32, kind="ExternalInput").ap())
    w1T = nc.dram_tensor("w1T", [DM, 4096], F32, kind="ExternalInput").ap()
    w2T = nc.dram_tensor("w2T", [4096, DM], F32, kind="ExternalInput").ap()
    b1c = nc.dram_tensor("b1c", [128, 32], F32, kind="ExternalInput").ap()
    b2c = nc.dram_tensor("b2c", [128, 8], F32, kind="ExternalInput").ap()
    ones1 = nc.dram_tensor("ones1", [128, 128], F32, kind="ExternalInput").ap()
    onescol = nc.dram_tensor("onescol", [128, 1], F32, kind="ExternalInput").ap()
    eye_in = nc.dram_tensor("eye", [128, 128], F32, kind="ExternalInput").ap()
    m0_in = nc.dram_tensor("m0c", [128, 1], F32, kind="ExternalInput").ap()
    m1_in = nc.dram_tensor("m1c", [128, 1], F32, kind="ExternalInput").ap()
    cv_in = nc.dram_tensor("cvals", [128, 2], F32, kind="ExternalInput").ap()

    out_t = nc.dram_tensor("out", [LH, DM], F32, kind="ExternalOutput").ap()

    xm_h = nc.dram_tensor("xm_h", [DM, L], F32, kind=KIND).ap()
    xc_h = nc.dram_tensor("xc_h", [DL, L], F32, kind=KIND).ap()
    zs_h = nc.dram_tensor("zs_h", [DL, L], F32, kind=KIND).ap()
    h1_h = nc.dram_tensor("h1_h", [4096, 2 * LH], BF16, kind="Internal").ap()
    ccd_i = nc.dram_tensor("ccd_i", [96, L], F32, kind="Internal").ap()
    ccd_o = nc.dram_tensor("ccd_o", [96, L], F32, kind="Internal").ap()
    ccw_i = nc.dram_tensor("ccw_i", [DM, L], F32, kind="Internal").ap()
    ccw_o = nc.dram_tensor("ccw_o", [DM, L], F32, kind="Internal").ap()
    if dbg:
        dbg_d = nc.dram_tensor("dbg_d", [96, L], F32, kind="ExternalOutput").ap()
        dbg_w = nc.dram_tensor("dbg_w", [DM, L], F32, kind="ExternalOutput").ap()
        dbg_s = nc.dram_tensor("dbg_s", [128, 512], F32, kind="ExternalOutput").ap()
    RG = [[0, 1], [2, 3], [4, 5], [6, 7]]

    import contextlib
    ctx = contextlib.ExitStack()
    with ctx:
        AR1 = ctx.enter_context(nc.sbuf_tensor("AR1", [128, CW], F32))
        AR2 = ctx.enter_context(nc.sbuf_tensor("AR2", [128, 4096], F32))
        AR3 = ctx.enter_context(nc.sbuf_tensor("AR3", [128, KT * 516], F32))
        AR4 = ctx.enter_context(nc.sbuf_tensor("AR4", [128, 4 * CW], F32))
        AR5 = ctx.enter_context(nc.sbuf_tensor("AR5", [128, 8192], F32))
        AR6 = ctx.enter_context(nc.sbuf_tensor("AR6", [128, 7680], F32))
        AR7 = ctx.enter_context(nc.sbuf_tensor("AR7", [128, CW], F32))
        state = ctx.enter_context(nc.sbuf_tensor("state", [128, NST * KT], F32))
        vecs = ctx.enter_context(nc.sbuf_tensor("vecs", [128, 64], F32))
        acol = ctx.enter_context(nc.sbuf_tensor("acol", [128, 128], F32))
        atmp = ctx.enter_context(nc.sbuf_tensor("atmp", [128, 128], F32))
        wdt_sb = ctx.enter_context(nc.sbuf_tensor("wdt_sb", [64, DL], F32))
        wx_sb = ctx.enter_context(nc.sbuf_tensor("wx_sb", [128, KT * 96], F32))
        b1_sb = ctx.enter_context(nc.sbuf_tensor("b1_sb", [128, 32], F32))
        b2_sb = ctx.enter_context(nc.sbuf_tensor("b2_sb", [128, 8], F32))
        ones1_sb = ctx.enter_context(nc.sbuf_tensor("ones1_sb", [128, 128], F32))
        onesc_sb = ctx.enter_context(nc.sbuf_tensor("onesc_sb", [128, 1], F32))
        eye_sb = ctx.enter_context(nc.sbuf_tensor("eye_sb", [128, 128], F32))
        m0_sb = ctx.enter_context(nc.sbuf_tensor("m0_sb", [128, 1], F32))
        m1_sb = ctx.enter_context(nc.sbuf_tensor("m1_sb", [128, 1], F32))
        cvals = ctx.enter_context(nc.sbuf_tensor("cvals_sb", [128, 2], F32))
        nc.const_aps.register(F32, EPS, cvals[:, 0:1])
        nc.const_aps.register(F32, 1.0, cvals[:, 1:2])
        rs1 = ctx.enter_context(nc.sbuf_tensor("rs1", [1, T], F32))
        rs2 = rs1
        sc1 = ctx.enter_context(nc.sbuf_tensor("sc1", [128, 8], F32))
        gq = ctx.enter_context(nc.sbuf_tensor("gq", [128, 4], F32))

        wq = AR1

        pA0 = ctx.enter_context(nc.psum_tensor("pA0", [128, T], F32))
        pA1 = ctx.enter_context(nc.psum_tensor("pA1", [128, T], F32))
        pdbc = ctx.enter_context(nc.psum_tensor("pdbc", [128, T], F32))
        pbc = ctx.enter_context(nc.psum_tensor("pbc", [128, T], F32))
        pdel = ctx.enter_context(nc.psum_tensor("pdel", [128, T], F32))
        pwo0 = ctx.enter_context(nc.psum_tensor("pwo0", [128, T], F32))
        pwo1 = ctx.enter_context(nc.psum_tensor("pwo1", [128, T], F32))

        sems = {s: ctx.enter_context(nc.semaphore(s)) for s in SEMS}
        blk = ctx.enter_context(nc.Block())

        scarr = AR6[:, 0:768]
        bcst = AR7[:, 0:2048]

        pr = Prog()
        W = pr.c

        def ln_pass_s(xsl, tt, w0):
            pr.add("V", lambda e, xsl=xsl, tt=tt: e.tensor_reduce(
                scarr[:, tt:tt + 1], xsl, AXX, AOP.add), waits=w0, inc="V")
            pr.add("V", lambda e, xsl=xsl: e.tensor_tensor(AR2[:, 0:1024], xsl, xsl, AOP.mult),
                   waits=[("T", W("T")), ("A", W("A"))], inc="V")
            pr.add("V", lambda e, tt=tt: e.tensor_reduce(
                scarr[:, 128 + tt:129 + tt], AR2[:, 0:1024], AXX, AOP.add), inc="V")

        def ln_pass_m():
            pr.add("V", lambda e: e.tensor_scalar_mul(scarr[:, 256:384], scarr[:, 0:128], 1.0 / DM), inc="V")
            pr.add("V", lambda e: e.tensor_tensor(scarr[:, 384:512], scarr[:, 256:384],
                                                  scarr[:, 256:384], AOP.mult), inc="V")
            pr.add("V", lambda e: e.tensor_scalar_mul(scarr[:, 512:640], scarr[:, 128:256], 1.0 / DM), inc="V")
            pr.add("V", lambda e: e.tensor_sub(scarr[:, 512:640], scarr[:, 512:640],
                                               scarr[:, 384:512]), inc="V")
            vW = W("V")
            pr.add("A", lambda e: e.activation(scarr[:, 640:768], scarr[:, 512:640], ACT.Ln, bias=EPS),
                   waits=[("V", vW)], inc="A")
            pr.add("A", lambda e: e.activation(scarr[:, 640:768], scarr[:, 640:768], ACT.Exp, scale=-0.5),
                   inc="A")
            return W("A")

        def ln_pass_n(xsl, xnsl, tt, w0):
            pr.add("V", lambda e, xsl=xsl, xnsl=xnsl, tt=tt: e.tensor_scalar(
                xnsl, xsl, scarr[:, 256 + tt:257 + tt], scarr[:, 640 + tt:641 + tt],
                AOP.subtract, AOP.mult), waits=w0, inc="V")
            return W("V")

        for dst, src in [(cvals[:], cv_in), (ones1_sb[:], ones1), (onesc_sb[:], onescol),
                         (eye_sb[:], eye_in), (b1_sb[:], b1c), (b2_sb[:], b2c),
                         (m0_sb[:], m0_in), (m1_sb[:], m1_in)]:
            pr.add("S", lambda e, dst=dst, src=src: e.dma_start(out=dst, in_=src), inc="S", amt=16)

        for tt in range(L // 128):
            xsl = AR1[:, (tt % 4) * 1024:(tt % 4) * 1024 + 1024]
            pr.add("S", lambda e, xsl=xsl, tt=tt: e.dma_start(out=xsl, in_=x_full[tt * 128:(tt + 1) * 128, :]),
                   waits=[("V", W("V"))], inc="S", amt=16)
            sW = W("S")
            ln_pass_s(xsl, tt, [("S", sW)])
        aW_ln = ln_pass_m()
        for tt in range(L // 128):
            xsl = AR1[:, (tt % 4) * 1024:(tt % 4) * 1024 + 1024]
            pr.add("S", lambda e, xsl=xsl, tt=tt: e.dma_start(out=xsl, in_=x_full[tt * 128:(tt + 1) * 128, :]),
                   waits=[("V", W("V"))], inc="S", amt=16)
            sW = W("S")
            xnsl = AR2[:, 0:1024]
            ln_pass_n(xsl, xnsl, tt, [("S", sW), ("A", aW_ln), ("T", W("T"))])
            vW = W("V")
            prevG = W("G")
            for a in range(8):
                bank = pdel if a % 2 == 0 else pbc
                pr.add("T", lambda e, bank=bank, xnsl=xnsl, a=a: e.transpose(
                    bank[:, 0:128], xnsl[:, a * 128:(a + 1) * 128], eye_sb[:]),
                    waits=[("V", vW), ("A", W("A")), ("G", prevG)], inc="T")
                tW = W("T")
                pr.add("A", lambda e, a=a, bank=bank: e.copy(AR3[:, a * 128:(a + 1) * 128], bank[:, 0:128]),
                       waits=[("T", tW)], inc="A")
            aW = W("A")
            pr.add("G", lambda e, tt=tt: e.dma_start(
                out=xm_h[:, tt * 128:(tt + 1) * 128].rearrange("(a p) t -> p a t", p=128),
                in_=AR3[:, 0:1024].rearrange("p (a t) -> p a t", a=8)),
                waits=[("A", aW)], inc="G", amt=16)

        for li in range(nl):
            pr.add("S", lambda e, li=li: e.dma_start(out=vecs[:, 0:32], in_=convw[li]),
                   waits=[("V", W("V")), ("A", W("A"))], inc="S", amt=16)
            pr.add("S", lambda e, li=li: e.dma_start(out=vecs[:, 32:40], in_=convb[li]), inc="S", amt=16)
            pr.add("S", lambda e, li=li: e.dma_start(out=vecs[:, 40:48], in_=bdt[li]), inc="S", amt=16)
            pr.add("S", lambda e, li=li: e.dma_start(out=vecs[:, 48:56], in_=dvec[li]), inc="S", amt=16)
            pr.add("S", lambda e, li=li: e.dma_start(out=vecs[:, 56:64], in_=normw[li]), inc="S", amt=16)
            pr.add("S", lambda e, li=li: e.dma_start(out=atmp[:], in_=alog[li]), inc="S", amt=16)
            pr.add("S", lambda e, li=li: e.dma_start(out=wdt_sb[:], in_=wdtT[li]),
                   waits=[("T", W("T"))], inc="S", amt=16)
            pr.add("S", lambda e, li=li: e.dma_start(
                out=wx_sb[:].rearrange("p (a m) -> p a m", a=KT),
                in_=wxT[li].rearrange("(a p) m -> p a m", p=128)), inc="S", amt=16)
            pr.add("S", lambda e, li=li: e.dma_start(
                out=AR4[:, 0:16384].rearrange("p (a m) -> p a m", a=8),
                in_=winT[li].rearrange("(a p) m -> p a m", p=128)),
                waits=[("V", W("V"))], inc="S", amt=16)
            sWL = W("S")
            pr.add("A", lambda e: e.activation(acol[:], atmp[:], ACT.Exp), waits=[("S", sWL)], inc="A")
            pr.add("V", lambda e: e.tensor_scalar_mul(acol[:], acol[:], -1.0),
                   waits=[("A", W("A"))], inc="V")
            pr.add("V", lambda e: e.memset(state[:], 0.0), inc="V")
            pr.add("V", lambda e: e.memset(
                AR3[:].rearrange("p (a q) -> p a q", a=KT)[:, :, 0:3], 0.0),
                waits=[("A", W("A")), ("G", W("G"))], inc="V")

            evac_cnt = {0: 0, 1: 0}
            for c in range(NCH):
                cs = c * T
                pr.add("S", lambda e, cs=cs: e.dma_start(
                    out=AR1[:, 0:4096].rearrange("p (a t) -> p a t", a=8),
                    in_=xm_h[:, cs:cs + T].rearrange("(a p) t -> p a t", p=128)),
                    waits=[("V", W("V")), ("T", W("T")), ("G", W("G"))], inc="S", amt=16)
                sW = W("S")
                pr.add("A", lambda e: e.activation(AR2[:, 0:4096], AR1[:, 0:4096], ACT.Square),
                       waits=[("S", sW), ("T", W("T")), ("V", W("V"))], inc="A")
                aW = W("A")
                for j in range(8):
                    pr.add("T", lambda e, j=j: e.matmul(pbc[0:1, :], onesc_sb[:], AR2[:, j * T:(j + 1) * T],
                                                        start=(j == 0), stop=(j == 7)),
                           waits=([("A", aW)] if j == 0 else []), inc="T")
                tW = W("T")
                pr.add("A", lambda e: e.activation(rs1[:], pbc[0:1, :], ACT.Ln, scale=1.0 / DM, bias=EPS),
                       waits=[("T", tW)], inc="A")
                pr.add("A", lambda e: e.activation(rs2[:], rs1[:], ACT.Exp, scale=-0.5), inc="A")
                aW = W("A")
                pr.add("T", lambda e: e.matmul(pdel[:], ones1_sb[0:1, :], rs2[:], start=True, stop=True),
                       waits=[("A", aW)], inc="T")
                tW = W("T")
                for j in range(8):
                    pr.add("V", lambda e, j=j: e.scalar_tensor_tensor(
                        AR2[:, j * T:(j + 1) * T], AR1[:, j * T:(j + 1) * T], vecs[:, 56 + j:57 + j],
                        pdel[:], AOP.mult, AOP.mult), waits=([("T", tW)] if j == 0 else []), inc="V")
                hW = W("V")
                silu_done = {}
                for m in range(16):
                    bank = pA0 if m % 2 == 0 else pA1
                    for k in range(8):
                        pr.add("T", lambda e, m=m, k=k, bank=bank: e.matmul(
                            bank[:], AR4[:, k * 2048 + m * 128:k * 2048 + (m + 1) * 128],
                            AR2[:, k * T:(k + 1) * T], start=(k == 0), stop=(k == 7)),
                            waits=([("V", hW), ("A", evac_cnt[m % 2])] if k == 0 else []), inc="T")
                    tW = W("T")
                    if m < 8:
                        base = m * 516
                        xsl = AR3[:, base + 3:base + 515]
                        w_extra = [("G", W("G"))] if (li == 0 and c == 0 and m == 0) else []
                        pr.add("A", lambda e, xsl=xsl, bank=bank: e.copy(xsl, bank[:]),
                               waits=[("T", tW), ("V", W("V"))] + w_extra, inc="A")
                        evac_cnt[m % 2] = W("A")
                        aW = W("A")
                        xcm = AR6[:, 4096 + (m % 2) * T:4096 + (m % 2) * T + T]
                        pr.add("V", lambda e, xcm=xcm, base=base, m=m: e.tensor_scalar_mul(
                            xcm, AR3[:, base:base + T], vecs[:, m * 4:m * 4 + 1]),
                            waits=[("A", aW)], inc="V")
                        for s in range(1, 4):
                            pr.add("V", lambda e, xcm=xcm, base=base, m=m, s=s: e.scalar_tensor_tensor(
                                xcm, AR3[:, base + s:base + s + T], vecs[:, m * 4 + s:m * 4 + s + 1],
                                xcm, AOP.mult, AOP.add), inc="V")
                        vW = W("V")
                        xcs = AR6[:, m * T:(m + 1) * T]
                        pr.add("A", lambda e, xcs=xcs, xcm=xcm, m=m: e.activation(
                            xcs, xcm, ACT.Silu, bias=vecs[:, 32 + m:33 + m]),
                            waits=[("V", vW), ("G", W("G")), ("T", W("T"))], inc="A")
                        silu_done[m] = W("A")
                        aW = W("A")
                        pr.add("G", lambda e, xcs=xcs, m=m, cs=cs: e.dma_start(
                            out=xc_h[m * 128:(m + 1) * 128, cs:cs + T], in_=xcs),
                            waits=[("A", aW)], inc="G", amt=16)
                        pr.add("V", lambda e, base=base: e.tensor_copy(
                            AR3[:, base:base + 3], AR3[:, base + 512:base + 515]), inc="V")
                    else:
                        mm = m - 8
                        szs = AR6[:, 4096 + (m % 2) * T:4096 + (m % 2) * T + T]
                        pr.add("A", lambda e, szs=szs, bank=bank: e.activation(szs, bank[:], ACT.Silu),
                               waits=[("T", tW), ("G", W("G"))], inc="A")
                        evac_cnt[m % 2] = W("A")
                        aW = W("A")
                        pr.add("G", lambda e, szs=szs, mm=mm, cs=cs: e.dma_start(
                            out=zs_h[mm * 128:(mm + 1) * 128, cs:cs + T], in_=szs),
                            waits=[("A", aW)], inc="G", amt=16)
                for m in range(8):
                    pr.add("T", lambda e, m=m: e.matmul(
                        pdbc[0:96, :], wx_sb[:, m * 96:(m + 1) * 96], AR6[:, m * T:(m + 1) * T],
                        start=(m == 0), stop=(m == 7)),
                        waits=([("A", silu_done[7])] if m == 0 else []), inc="T")
                tW = W("T")
                pr.add("A", lambda e, cs=cs: e.copy(AR7[0:96, cs:cs + T], pdbc[0:96, :]),
                       waits=[("T", tW)], inc="A")
            aW = W("A")
            pr.add("G", lambda e: e.dma_start(out=ccd_i, in_=AR7[0:96, 0:4096]),
                   waits=[("A", aW)], inc="G", amt=16)
            gW = W("G")
            pr.add("G", lambda e: e.collective_compute("AllReduce", AOP.add, replica_groups=RG,
                                                       ins=[ccd_i], outs=[ccd_o]),
                   waits=[("G", gW)], inc="G", amt=1)
            gW = W("G")
            pr.add("G", lambda e: e.dma_start(out=AR2[0:96, 0:4096], in_=ccd_o),
                   waits=[("G", gW), ("V", W("V")), ("T", W("T")), ("A", W("A"))], inc="G", amt=16)
            dbcG = W("G")
            pr.add("S", lambda e, li=li: e.dma_start(
                out=AR5[:].rearrange("p (a m) -> p a m", a=KT),
                in_=woutT[li].rearrange("(a p) m -> p a m", p=128)),
                waits=[("A", W("A")), ("T", W("T"))], inc="S", amt=16)
            woutS = W("S")

            U8 = AR4[:, 0 * CW:1 * CW]
            DA8 = AR4[:, 1 * CW:2 * CW]
            BX8 = AR4[:, 2 * CW:3 * CW]
            H8 = AR4[:, 3 * CW:4 * CW]
            pr.add("V", lambda e: e.memset(DA8.rearrange("p (a q) -> p a q", a=KT)[:, :, 0:1], 0.0),
                   waits=[("T", W("T"))], inc="V")
            bx_done = W("V")
            sc_done = W("V")
            ch_done = W("V")
            bc_consumed = {0: 0, 1: 0}
            cc_consumed = {0: 0, 1: 0}
            for c in range(NCH):
                cs = c * T
                for k in range(KT):
                    pr.add("T", lambda e, k=k, cs=cs: e.matmul(
                        pdel[:], wdt_sb[:, k * 128:(k + 1) * 128], AR2[0:64, cs:cs + T],
                        start=True, stop=True),
                        waits=[("G", dbcG), ("A", W("A")), ("V", W("V"))], inc="T")
                    tW = W("T")
                    tmp = AR6[:, 7168:7680]
                    pr.add("A", lambda e, tmp=tmp, k=k: e.activation(
                        tmp, pdel[:], ACT.Exp, bias=vecs[:, 40 + k:41 + k]),
                        waits=[("T", tW)], inc="A")
                    dsl = AR6[:, k * T:(k + 1) * T]
                    pr.add("A", lambda e, dsl=dsl, tmp=tmp: e.activation(dsl, tmp, ACT.Ln, bias=1.0), inc="A")
                    aW = W("A")
                    xck = AR6[:, 4096 + (k % 2) * T:4096 + (k % 2) * T + T]
                    pr.add("A", lambda e, xck=xck, k=k, cs=cs: e.dma_start(
                        out=xck, in_=xc_h[k * 128:(k + 1) * 128, cs:cs + T]),
                        waits=[("V", W("V"))], inc="AD", amt=16)
                    adW = W("AD")
                    pr.add("V", lambda e, k=k, dsl=dsl, xck=xck: e.tensor_tensor(
                        U8[:, k * SL + 1:k * SL + 1 + T], dsl, xck, AOP.mult),
                        waits=[("A", aW), ("AD", adW), ("T", W("T"))], inc="V")
                    pr.add("V", lambda e, k=k, xck=xck: e.tensor_scalar_mul(
                        AR1[:, k * SL + 1:k * SL + 1 + T], xck, vecs[:, 48 + k:49 + k]), inc="V")
                for n in range(NST):
                    bslot = bcst[:, (n % 2) * T:(n % 2) * T + T]
                    cslot = bcst[:, 1024 + (n % 2) * T:1024 + (n % 2) * T + T]
                    pr.add("S", lambda e, n=n, cs=cs, bslot=bslot: e.dma_start(
                        out=bslot, in_=ccd_o[64 + n:65 + n, cs:cs + T].to_broadcast([128, T])),
                        waits=[("V", bc_consumed[n % 2]), ("G", dbcG)], inc="S", amt=16)
                    sWb = W("S")
                    pr.add("G", lambda e, n=n, cs=cs, cslot=cslot: e.dma_start(
                        out=cslot, in_=ccd_o[80 + n:81 + n, cs:cs + T].to_broadcast([128, T])),
                        waits=[("V", cc_consumed[n % 2])], inc="G", amt=16)
                    gWc = W("G")
                    for k in range(KT):
                        pr.add("A", lambda e, k=k, n=n: e.activation(
                            DA8[:, k * SL + 1:k * SL + 1 + T], AR6[:, k * T:(k + 1) * T], ACT.Exp,
                            scale=acol[:, k * NST + n:k * NST + n + 1]),
                            waits=([("V", sc_done)] if k == 0 else []), inc="A")
                    aW = W("A")
                    pr.add("V", lambda e, n=n: e.tensor_copy(
                        BX8.rearrange("p (a q) -> p a q", a=KT)[:, :, 0],
                        state[:, n * KT:(n + 1) * KT]), waits=[("A", aW), ("S", sWb)], inc="V")
                    for k in range(KT):
                        pr.add("V", lambda e, k=k, bslot=bslot: e.tensor_tensor(
                            BX8[:, k * SL + 1:k * SL + 1 + T], U8[:, k * SL + 1:k * SL + 1 + T],
                            bslot, AOP.mult), inc="V")
                    bc_consumed[n % 2] = W("V")
                    bx_done = W("V")
                    pr.add("V", lambda e: e.tensor_tensor_scan(H8[:], DA8[:], BX8[:], 0.0, AOP.mult, AOP.add),
                           inc="V")
                    sc_done = W("V")
                    for k in range(KT):
                        pr.add("V", lambda e, k=k, cslot=cslot: e.tensor_tensor(
                            BX8[:, k * SL + 1:k * SL + 1 + T], H8[:, k * SL + 1:k * SL + 1 + T],
                            cslot, AOP.mult), waits=([("G", gWc)] if k == 0 else []), inc="V")
                    cc_consumed[n % 2] = W("V")
                    ch_done = W("V")
                    pr.add("V", lambda e: e.tensor_tensor(
                        AR1.rearrange("p (a q) -> p a q", a=KT)[:, :, 1:SL],
                        AR1.rearrange("p (a q) -> p a q", a=KT)[:, :, 1:SL],
                        BX8.rearrange("p (a q) -> p a q", a=KT)[:, :, 1:SL], AOP.add), inc="V")
                    pr.add("V", lambda e, n=n: e.tensor_copy(
                        state[:, n * KT:(n + 1) * KT],
                        H8.rearrange("p (a q) -> p a q", a=KT)[:, :, SL - 1]), inc="V")
                for k in range(KT):
                    szk = AR6[:, 5120 + (k % 2) * T:5120 + (k % 2) * T + T]
                    pr.add("S", lambda e, szk=szk, k=k, cs=cs: e.dma_start(
                        out=szk, in_=zs_h[k * 128:(k + 1) * 128, cs:cs + T]),
                        waits=[("V", W("V"))], inc="S", amt=16)
                    sW = W("S")
                    pr.add("V", lambda e, k=k, szk=szk: e.tensor_tensor(
                        AR1[:, k * SL + 1:k * SL + 1 + T], AR1[:, k * SL + 1:k * SL + 1 + T],
                        szk, AOP.mult), waits=[("S", sW)], inc="V")
                gV = W("V")
                for m2 in range(8):
                    bank = pwo0 if m2 % 2 == 0 else pwo1
                    for k in range(KT):
                        pr.add("T", lambda e, m2=m2, k=k, bank=bank: e.matmul(
                            bank[:], AR5[:, k * 1024 + m2 * 128:k * 1024 + (m2 + 1) * 128],
                            AR1[:, k * SL + 1:k * SL + 1 + T], start=(k == 0), stop=(k == KT - 1)),
                            waits=([("V", gV), ("S", woutS), ("A", W("A"))] if k == 0 else []), inc="T")
                    tW = W("T")
                    osl = AR6[:, 6144 + (m2 % 2) * T:6144 + (m2 % 2) * T + T]
                    pr.add("A", lambda e, osl=osl, bank=bank: e.copy(osl, bank[:]),
                           waits=[("T", tW), ("G", W("G"))], inc="A")
                    aW = W("A")
                    pr.add("G", lambda e, osl=osl, m2=m2, cs=cs: e.dma_start(
                        out=ccw_i[m2 * 128:(m2 + 1) * 128, cs:cs + T], in_=osl),
                        waits=[("A", aW)], inc="G", amt=16)
            gW = W("G")
            pr.add("G", lambda e: e.collective_compute("AllReduce", AOP.add, replica_groups=RG,
                                                       ins=[ccw_i], outs=[ccw_o]),
                   waits=[("G", gW)], inc="G", amt=1)
            woG = W("G")
            if dbg:
                pr.add("G", lambda e: e.dma_start(out=dbg_d, in_=ccd_o), inc="G", amt=16)
                pr.add("G", lambda e: e.dma_start(out=dbg_w, in_=ccw_o), inc="G", amt=16)
            for c in range(NCH):
                cs = c * T
                pr.add("S", lambda e, cs=cs: e.dma_start(
                    out=AR1[:, 0:4096].rearrange("p (a t) -> p a t", a=8),
                    in_=xm_h[:, cs:cs + T].rearrange("(a p) t -> p a t", p=128)),
                    waits=[("V", W("V")), ("T", W("T")), ("A", W("A")), ("G", W("G"))], inc="S", amt=16)
                pr.add("S", lambda e, cs=cs: e.dma_start(
                    out=AR2[:, 0:4096].rearrange("p (a t) -> p a t", a=8),
                    in_=ccw_o[:, cs:cs + T].rearrange("(a p) t -> p a t", p=128)),
                    waits=[("G", woG)], inc="S", amt=16)
                sW = W("S")
                pr.add("V", lambda e: e.tensor_tensor(AR1[:, 0:4096], AR1[:, 0:4096], AR2[:, 0:4096], AOP.add),
                       waits=[("S", sW)], inc="V")
                vW = W("V")
                pr.add("G", lambda e, cs=cs: e.dma_start(
                    out=xm_h[:, cs:cs + T].rearrange("(a p) t -> p a t", p=128),
                    in_=AR1[:, 0:4096].rearrange("p (a t) -> p a t", a=8)),
                    waits=[("V", vW)], inc="G", amt=16)

        xnh = AR4[:, 0:8192].bitcast(BF16)
        xnl = AR4[:, 8208:16400].bitcast(BF16)
        xmG = W("G")
        for tt in range(LH // 128):
            xsl = AR1[:, (tt % 2) * 1024:(tt % 2) * 1024 + 1024]
            pr.add("S", lambda e, xsl=xsl, tt=tt: e.dma_start(out=xsl, in_=x_half[tt * 128:(tt + 1) * 128, :]),
                   waits=[("V", W("V")), ("G", xmG), ("T", W("T")), ("A", W("A"))], inc="S", amt=16)
            sW = W("S")
            ln_pass_s(xsl, tt, [("S", sW)])
        aW_ln = ln_pass_m()
        for tt in range(LH // 128):
            xsl = AR1[:, (tt % 2) * 1024:(tt % 2) * 1024 + 1024]
            pr.add("S", lambda e, xsl=xsl, tt=tt: e.dma_start(out=xsl, in_=x_half[tt * 128:(tt + 1) * 128, :]),
                   waits=[("V", W("V"))], inc="S", amt=16)
            sW = W("S")
            xnsl = AR2[:, 0:1024]
            ln_pass_n(xsl, xnsl, tt, [("S", sW), ("A", aW_ln), ("T", W("T"))])
            vW = W("V")
            for a in range(8):
                bank = pdel if a % 2 == 0 else pbc
                pr.add("T", lambda e, a=a, bank=bank, xnsl=xnsl: e.transpose(
                    bank[:, 0:128], xnsl[:, a * 128:(a + 1) * 128], eye_sb[:]),
                    waits=[("V", vW), ("A", W("A"))], inc="T")
                tW = W("T")
                pr.add("A", lambda e, a=a, tt=tt, bank=bank: e.copy(
                    xnh[:, a * LH + tt * 128:a * LH + (tt + 1) * 128], bank[:, 0:128]),
                    waits=[("T", tW)], inc="A")
                aW2 = W("A")
                pr.add("V", lambda e, a=a, tt=tt, bank=bank: e.tensor_tensor(
                    xnl[:, a * LH + tt * 128:a * LH + (tt + 1) * 128], bank[:, 0:128],
                    xnh[:, a * LH + tt * 128:a * LH + (tt + 1) * 128], AOP.subtract),
                    waits=[("A", aW2)], inc="V")

        def spacer():
            pr.add("V", lambda e: e.tensor_copy(AR2[:, 1024:1536], AR2[:, 1536:2048]), inc="V")

        for gi, (wt, n_t, ldfn) in enumerate([
            (w1T, 8, lambda e, t, wsl: e.dma_start(out=wsl, in_=w1T[t * 128:(t + 1) * 128, :])),
            (w2T, 8, lambda e, t, wsl: e.dma_start(
                out=wsl.rearrange("p (a m) -> p a m", a=4),
                in_=w2T[t * 512:(t + 1) * 512, :].rearrange("(a p) m -> p a m", p=128))),
        ]):
            for t in range(n_t):
                wsl = AR5[:, (t % 2) * 4096:(t % 2) * 4096 + 4096]
                pr.add("S", lambda e, t=t, wsl=wsl, ldfn=ldfn: ldfn(e, t, wsl),
                       waits=[("A", W("A")), ("T", W("T")), ("V", W("V"))], inc="S", amt=16)
                sW = W("S")
                pr.add("A", lambda e, wsl=wsl: e.activation(AR2[:, 0:4096], wsl, ACT.Abs),
                       waits=[("S", sW), ("V", W("V"))], inc="A")
                aW = W("A")
                pr.add("V", lambda e, t=t, gi=gi: e.tensor_reduce(
                    scarr[:, gi * 8 + t:gi * 8 + t + 1], AR2[:, 0:4096], AXX, AOP.add),
                    waits=[("A", aW)], inc="V")
            spacer()
            pr.add("V", lambda e, gi=gi: e.tensor_reduce(
                gq[:, gi:gi + 1], scarr[:, gi * 8:gi * 8 + 8], AXX, AOP.add), inc="V")
            spacer()
            vW = W("V")
            pr.add("T", lambda e, gi=gi: e.matmul(pbc[0:1, 0:1], onesc_sb[:], gq[:, gi:gi + 1],
                                                  start=True, stop=True), waits=[("V", vW)], inc="T")
            tW = W("T")
            pr.add("V", lambda e: e.tensor_scalar(gq[0:1, 3:4], pbc[0:1, 0:1], 1.0 / (4096.0 * DM), EPS,
                                                  AOP.mult, AOP.add), waits=[("T", tW)], inc="V")
            spacer()
            pr.add("V", lambda e: e.reciprocal(gq[0:1, 2:3], gq[0:1, 3:4]), inc="V")
            spacer()
            vW = W("V")
            pr.add("T", lambda e: e.matmul(pbc[:, 0:1], ones1_sb[0:1, :], gq[0:1, 2:3],
                                           start=True, stop=True), waits=[("V", vW)], inc="T")
            tW = W("T")
            pr.add("A", lambda e, gi=gi: e.copy(gq[:, gi:gi + 1], pbc[:, 0:1]), waits=[("T", tW)], inc="A")
        g1A = g2A = W("A")

        h1hi = AR2[:, 2048:3072].bitcast(BF16)
        h1lo = AR2[:, 3072:4096].bitcast(BF16)
        for m in range(32):
            slot = m % 2
            wsl = AR5[:, slot * 4096:slot * 4096 + 1024]
            pr.add("S", lambda e, wsl=wsl, m=m: e.dma_start(
                out=wsl.rearrange("p (a q) -> p a q", a=8),
                in_=w1T[:, m * 128:(m + 1) * 128].rearrange("(a p) q -> p a q", p=128)),
                waits=[("V", W("V"))], inc="S", amt=16)
            sW = W("S")
            wqs = wq[:, 2048 + slot * 1024:2048 + slot * 1024 + 1024]
            wqb = wq[:, slot * 512:slot * 512 + 512].bitcast(BF16)
            pr.add("V", lambda e, wsl=wsl, wqs=wqs: e.tensor_scalar_mul(wqs, wsl, gq[:, 0:1]),
                   waits=[("S", sW), ("A", g1A), ("T", W("T"))], inc="V")
            pr.add("V", lambda e, wqs=wqs, wsl=wsl: e.tensor_single_scalar(wsl, wqs, 0.5, AOP.is_gt), inc="V")
            pr.add("V", lambda e, wqs=wqs: e.tensor_single_scalar(wqs, wqs, -0.5, AOP.is_lt), inc="V")
            pr.add("V", lambda e, wqs=wqs, wsl=wsl, wqb=wqb: e.tensor_sub(wqb, wsl, wqs), inc="V")
            vW = W("V")
            for cg in range(4):
                bank = pA0 if cg % 2 == 0 else pA1
                for k in range(8):
                    for hl, xsrc in ((0, xnh), (1, xnl)):
                        pr.add("T", lambda e, k=k, cg=cg, bank=bank, wqb=wqb, xsrc=xsrc: e.matmul(
                            bank[:], wqb[:, k * 128:(k + 1) * 128],
                            xsrc[:, k * LH + cg * T:k * LH + (cg + 1) * T],
                            start=(k == 0 and xsrc is xnh), stop=(k == 7 and xsrc is xnl)),
                            waits=([("V", vW), ("A", W("A"))] if (k == 0 and hl == 0) else []), inc="T")
                tW = W("T")
                pr.add("A", lambda e, cg=cg, bank=bank, m=m: e.activation(
                    AR2[:, cg * T:(cg + 1) * T], bank[:], ACT.Relu, bias=b1_sb[:, m:m + 1]),
                    waits=[("T", tW), ("G", W("G"))], inc="A")
            aW = W("A")
            for cg in range(4):
                pr.add("A", lambda e, cg=cg: e.copy(h1hi[:, cg * T:(cg + 1) * T], AR2[:, cg * T:(cg + 1) * T]),
                       inc="A")
                aW2 = W("A")
                pr.add("V", lambda e, cg=cg: e.tensor_tensor(
                    h1lo[:, cg * T:(cg + 1) * T], AR2[:, cg * T:(cg + 1) * T],
                    h1hi[:, cg * T:(cg + 1) * T], AOP.subtract), waits=[("A", aW2)], inc="V")
            vW2 = W("V")
            pr.add("G", lambda e, m=m: e.dma_start(out=h1_h[m * 128:(m + 1) * 128, 0:LH], in_=h1hi[:]),
                   waits=[("V", vW2)], inc="G", amt=16)
            pr.add("G", lambda e, m=m: e.dma_start(out=h1_h[m * 128:(m + 1) * 128, LH:2 * LH], in_=h1lo[:]),
                   inc="G", amt=16)

        for m2 in range(8):
            w2sl = AR5[:, 4096:8192]
            pr.add("S", lambda e, w2sl=w2sl, m2=m2: e.dma_start(
                out=w2sl.rearrange("p (a q) -> p a q", a=32),
                in_=w2T[:, m2 * 128:(m2 + 1) * 128].rearrange("(a p) q -> p a q", p=128)),
                waits=[("V", W("V")), ("T", W("T"))], inc="S", amt=16)
            sW = W("S")
            wqs = AR3[:, 0:4096]
            wq2b = wq[:, 0:2048].bitcast(BF16)
            pr.add("V", lambda e, w2sl=w2sl, wqs=wqs: e.tensor_scalar_mul(wqs, w2sl, gq[:, 1:2]),
                   waits=[("S", sW), ("A", g2A), ("T", W("T"))], inc="V")
            pr.add("V", lambda e, wqs=wqs, w2sl=w2sl: e.tensor_single_scalar(w2sl[:, 0:4096], wqs, 0.5, AOP.is_gt), inc="V")
            pr.add("V", lambda e, wqs=wqs: e.tensor_single_scalar(wqs, wqs, -0.5, AOP.is_lt), inc="V")
            pr.add("V", lambda e, wqs=wqs, w2sl=w2sl, wq2b=wq2b: e.tensor_sub(wq2b, w2sl[:, 0:4096], wqs), inc="V")
            vW = W("V")
            hread = {0: 0, 1: 0}
            for kf in range(32):
                hslot = kf % 2
                hhi = AR5[:, hslot * 2048:hslot * 2048 + 1024].bitcast(BF16)
                hlo = AR5[:, hslot * 2048 + 1024:hslot * 2048 + 2048].bitcast(BF16)
                pr.add("S", lambda e, hhi=hhi, kf=kf: e.dma_start(
                    out=hhi, in_=h1_h[kf * 128:(kf + 1) * 128, 0:LH]),
                    waits=[("T", hread[hslot]), ("G", W("G")), ("V", vW)], inc="S", amt=16)
                pr.add("S", lambda e, hlo=hlo, kf=kf: e.dma_start(
                    out=hlo, in_=h1_h[kf * 128:(kf + 1) * 128, LH:2 * LH]),
                    inc="S", amt=16)
                sW = W("S")
                for cg in range(4):
                    bank = [pA0, pA1, pwo0, pwo1][cg]
                    for hl, hsrc in ((0, hhi), (1, hlo)):
                        pr.add("T", lambda e, kf=kf, cg=cg, bank=bank, hsrc=hsrc, wq2b=wq2b, hl=hl: e.matmul(
                            bank[:], wq2b[:, kf * 128:(kf + 1) * 128], hsrc[:, cg * T:(cg + 1) * T],
                            start=(kf == 0 and hl == 0), stop=(kf == 31 and hl == 1)),
                            waits=([("S", sW)] + ([("V", vW), ("A", W("A"))] if (kf == 0 and hl == 0) else [])), inc="T")
                hread[hslot] = W("T")
            tW = W("T")
            for cg in range(4):
                bank = [pA0, pA1, pwo0, pwo1][cg]
                pr.add("A", lambda e, m2=m2, cg=cg, bank=bank: e.activation(
                    AR4[:, m2 * LH + cg * T:m2 * LH + (cg + 1) * T], bank[:], ACT.Silu,
                    bias=b2_sb[:, m2:m2 + 1]), waits=([("T", tW)] if cg == 0 else []), inc="A")
        gA = W("A")

        outG = []
        for tt in range(LH // 128):
            xsl = AR1[:, (tt % 2) * 1024:(tt % 2) * 1024 + 1024]
            pr.add("S", lambda e, xsl=xsl, tt=tt: e.dma_start(out=xsl, in_=x_half[tt * 128:(tt + 1) * 128, :]),
                   waits=[("V", W("V")), ("T", W("T"))], inc="S", amt=16)
            pr.add("S", lambda e, tt=tt: e.dma_start(
                out=AR2[:, 0:1024].rearrange("p (a t) -> p a t", a=8),
                in_=xm_h[:, tt * 128:(tt + 1) * 128].rearrange("(a p) t -> p a t", p=128)),
                waits=[("V", W("V")), ("G", W("G"))], inc="S", amt=16)
            pr.add("S", lambda e, tt=tt: e.dma_start(
                out=AR2[:, 1024:2048].rearrange("p (a t) -> p a t", a=8),
                in_=xm_h[:, LH + tt * 128:LH + (tt + 1) * 128].rearrange("(a p) t -> p a t", p=128)),
                inc="S", amt=16)
            sW = W("S")
            pr.add("V", lambda e: e.tensor_scalar_mul(AR2[:, 2048:3072], AR2[:, 1024:2048], m1_sb[:, 0:1]),
                   waits=[("S", sW), ("A", gA)], inc="V")
            pr.add("V", lambda e: e.scalar_tensor_tensor(
                AR2[:, 3072:4096], AR2[:, 0:1024], m0_sb[:, 0:1], AR2[:, 2048:3072],
                AOP.mult, AOP.add), inc="V")
            for a in range(8):
                pr.add("V", lambda e, a=a, tt=tt: e.tensor_tensor(
                    AR2[:, a * 128:(a + 1) * 128], AR2[:, 3072 + a * 128:3072 + (a + 1) * 128],
                    AR4[:, a * LH + tt * 128:a * LH + (tt + 1) * 128], AOP.mult), inc="V")
            vW = W("V")
            for a in range(8):
                bank = pdel if a % 2 == 0 else pbc
                pr.add("T", lambda e, a=a, bank=bank: e.transpose(
                    bank[:, 0:128], AR2[:, a * 128:(a + 1) * 128], eye_sb[:]),
                    waits=[("V", vW), ("A", W("A"))], inc="T")
                tW = W("T")
                pr.add("A", lambda e, a=a, bank=bank, tt=tt: e.copy(
                    AR3[:, 2048 + (tt % 2) * 1024 + a * 128:2048 + (tt % 2) * 1024 + (a + 1) * 128],
                    bank[:, 0:128]),
                    waits=[("T", tW)] + ([("G", outG[tt - 2])] if tt >= 2 and a == 0 else []), inc="A")
            aW = W("A")
            osl = AR3[:, 2048 + (tt % 2) * 1024:2048 + (tt % 2) * 1024 + 1024]
            pr.add("V", lambda e, osl=osl, xsl=xsl: e.tensor_tensor(osl, osl, xsl, AOP.add),
                   waits=[("A", aW)], inc="V")
            vW = W("V")
            pr.add("G", lambda e, osl=osl, tt=tt: e.dma_start(out=out_t[tt * 128:(tt + 1) * 128, :], in_=osl),
                   waits=[("V", vW)], inc="G", amt=16)
            outG.append(W("G"))

        pr.add("A", lambda e: e.copy(sc1[:, 0:1], sc1[:, 0:1]),
               waits=[("G", W("G")), ("S", W("S"))], inc="A")

        cross_waits = {s: set() for s in SEMS}
        for (eng, fn, waits, inc, amt) in pr.steps:
            for (s, v) in waits:
                cross_waits[s].add((v, eng))
        cnt_replay = dict.fromkeys(SEMS, 0)
        needs_drain = []
        for (eng, fn, waits, inc, amt) in pr.steps:
            nd = False
            if inc is not None:
                lo = cnt_replay[inc]
                cnt_replay[inc] += amt
                hi = cnt_replay[inc]
                if eng in ("V", "A"):
                    for (v, weng) in cross_waits[inc]:
                        if weng != eng and lo < v <= hi:
                            nd = True
                            break
            needs_drain.append(nd)

        def emitter(name):
            def go(E):
                for i, (eng, fn, waits, inc, amt) in enumerate(pr.steps):
                    if eng != name:
                        continue
                    for (s, v) in waits:
                        E.wait_ge(sems[s], v)
                    inst = fn(E)
                    if inc is not None:
                        inst.then_inc(sems[inc], amt)
            return go

        blk.sync(emitter("S"))
        blk.vector(emitter("V"))
        blk.scalar(emitter("A"))
        blk.tensor(emitter("T"))
        blk.gpsimd(emitter("G"))

    return nc


def kernel(L0_attn_x, L0_attn_norm_w, L0_attn_W_in, L0_attn_conv_w, L0_attn_conv_b,
           L0_attn_W_x, L0_attn_W_dt, L0_attn_b_dt, L0_attn_A_log, L0_attn_D,
           L0_attn_W_out, L0_ffn_W1, L0_ffn_b1, L0_ffn_W2, L0_ffn_b2):
    f32 = np.float32
    x = np.ascontiguousarray(np.asarray(L0_attn_x, dtype=f32))

    def tile128(a):
        dl, k = a.shape
        return np.ascontiguousarray(a.reshape(KT, 128, k).transpose(1, 0, 2).reshape(128, KT * k))

    if "prog" not in _NC_CACHE:
        _NC_CACHE["prog"] = build_nc()
    nc = _NC_CACHE["prog"]

    in_maps = []
    for c in range(8):
        b, half = c // 2, c % 2
        r0 = half * DL
        rows = slice(r0, r0 + DL)
        m = {
            "x_full": np.ascontiguousarray(x[b]),
            "x_half": np.ascontiguousarray(x[b, half * LH:(half + 1) * LH]),
            "w1T": np.ascontiguousarray(np.asarray(L0_ffn_W1, f32).T),
            "w2T": np.ascontiguousarray(np.asarray(L0_ffn_W2, f32).T),
            "b1c": np.ascontiguousarray(np.asarray(L0_ffn_b1, f32).reshape(32, 128).T),
            "b2c": np.ascontiguousarray(np.asarray(L0_ffn_b2, f32).reshape(8, 128).T),
            "ones1": np.ones((128, 128), f32),
            "onescol": np.ones((128, 1), f32),
            "eye": np.eye(128, dtype=f32),
            "m0c": np.full((128, 1), 1.0 if half == 0 else 0.0, f32),
            "m1c": np.full((128, 1), 1.0 if half == 1 else 0.0, f32),
            "cvals": np.tile(np.array([[EPS, 1.0]], f32), (128, 1)),
        }
        W_in = np.asarray(L0_attn_W_in, f32)
        W_x = np.asarray(L0_attn_W_x, f32)
        W_dt = np.asarray(L0_attn_W_dt, f32)
        W_out = np.asarray(L0_attn_W_out, f32)
        for i in range(NL):
            sel = np.concatenate([W_in[i][r0:r0 + DL], W_in[i][2048 + r0:2048 + r0 + DL]], axis=0)
            m[f"winT{i}"] = np.ascontiguousarray(sel.T)
            m[f"wxT{i}"] = np.ascontiguousarray(W_x[i][:, rows].T)
            m[f"wdtT{i}"] = np.ascontiguousarray(W_dt[i][rows].T)
            m[f"woutT{i}"] = np.ascontiguousarray(W_out[i][:, rows].T)
            m[f"convw{i}"] = tile128(np.asarray(L0_attn_conv_w, f32)[i][rows])
            m[f"convb{i}"] = tile128(np.asarray(L0_attn_conv_b, f32)[i][rows, None])
            m[f"bdt{i}"] = tile128(np.asarray(L0_attn_b_dt, f32)[i][rows, None])
            m[f"alog{i}"] = tile128(np.asarray(L0_attn_A_log, f32)[i][rows])
            m[f"dvec{i}"] = tile128(np.asarray(L0_attn_D, f32)[i][rows, None])
            m[f"normw{i}"] = np.ascontiguousarray(
                np.asarray(L0_attn_norm_w, f32)[i].reshape(8, 128).T)
        in_maps.append(m)

    res = run_bass_kernel_spmd(nc, in_maps, core_ids=list(range(8)))
    out = np.empty((4, L, DM), f32)
    for c in range(8):
        b, half = c // 2, c % 2
        out[b, half * LH:(half + 1) * LH] = res.results[c]["out"]
    return out


# version 12
# speedup vs baseline: 1.0037x; 1.0037x over previous
import numpy as np
import concourse.bass as bass
import concourse.mybir as mybir
from concourse.bass_utils import run_bass_kernel_spmd

F32 = mybir.dt.float32
BF16 = mybir.dt.bfloat16
AOP = mybir.AluOpType
ACT = mybir.ActivationFunctionType
AXX = mybir.AxisListType.X

L = 4096
LH = 2048
DM = 1024
DL = 1024
KT = 8
NST = 16
RK = 64
T = 512
NCH = L // T
SL = T + 1
CW = KT * SL
NL = 4
EPS = 1e-5

SEMS = ["S", "G", "T", "V", "A", "AD"]

_NC_CACHE = {}


class Prog:
    def __init__(self):
        self.steps = []
        self.cnt = dict.fromkeys(SEMS, 0)

    def add(self, eng, fn, waits=None, inc=None, amt=1):
        w = [(s, v) for (s, v) in (waits or []) if v > 0]
        if inc is not None:
            self.cnt[inc] += amt
        self.steps.append((eng, fn, w, inc, amt))

    def c(self, s):
        return self.cnt[s]


def build_nc(dbg=False, nl=NL):
    nc = bass.Bass("TRN2", target_bir_lowering=False, debug=False)
    KIND = "ExternalOutput" if dbg else "Internal"

    x_full = nc.dram_tensor("x_full", [L, DM], F32, kind="ExternalInput").ap()
    x_half = nc.dram_tensor("x_half", [LH, DM], F32, kind="ExternalInput").ap()
    winT, wxT, wdtT, woutT = [], [], [], []
    convw, convb, bdt, alog, dvec, normw = [], [], [], [], [], []
    for i in range(NL):
        winT.append(nc.dram_tensor(f"winT{i}", [DM, 2 * DL], F32, kind="ExternalInput").ap())
        wxT.append(nc.dram_tensor(f"wxT{i}", [DL, 96], F32, kind="ExternalInput").ap())
        wdtT.append(nc.dram_tensor(f"wdtT{i}", [RK, DL], F32, kind="ExternalInput").ap())
        woutT.append(nc.dram_tensor(f"woutT{i}", [DL, DM], F32, kind="ExternalInput").ap())
        convw.append(nc.dram_tensor(f"convw{i}", [128, 32], F32, kind="ExternalInput").ap())
        convb.append(nc.dram_tensor(f"convb{i}", [128, 8], F32, kind="ExternalInput").ap())
        bdt.append(nc.dram_tensor(f"bdt{i}", [128, 8], F32, kind="ExternalInput").ap())
        alog.append(nc.dram_tensor(f"alog{i}", [128, 128], F32, kind="ExternalInput").ap())
        dvec.append(nc.dram_tensor(f"dvec{i}", [128, 8], F32, kind="ExternalInput").ap())
        normw.append(nc.dram_tensor(f"normw{i}", [128, 8], F32, kind="ExternalInput").ap())
    w1T = nc.dram_tensor("w1T", [DM, 4096], F32, kind="ExternalInput").ap()
    w2T = nc.dram_tensor("w2T", [4096, DM], F32, kind="ExternalInput").ap()
    b1c = nc.dram_tensor("b1c", [128, 32], F32, kind="ExternalInput").ap()
    b2c = nc.dram_tensor("b2c", [128, 8], F32, kind="ExternalInput").ap()
    ones1 = nc.dram_tensor("ones1", [128, 128], F32, kind="ExternalInput").ap()
    onescol = nc.dram_tensor("onescol", [128, 1], F32, kind="ExternalInput").ap()
    eye_in = nc.dram_tensor("eye", [128, 128], F32, kind="ExternalInput").ap()
    m0_in = nc.dram_tensor("m0c", [128, 1], F32, kind="ExternalInput").ap()
    m1_in = nc.dram_tensor("m1c", [128, 1], F32, kind="ExternalInput").ap()
    cv_in = nc.dram_tensor("cvals", [128, 2], F32, kind="ExternalInput").ap()

    out_t = nc.dram_tensor("out", [LH, DM], F32, kind="ExternalOutput").ap()

    xm_h = nc.dram_tensor("xm_h", [DM, L], F32, kind=KIND).ap()
    xc_h = nc.dram_tensor("xc_h", [DL, L], F32, kind=KIND).ap()
    zs_h = nc.dram_tensor("zs_h", [DL, L], F32, kind=KIND).ap()
    h1_h = nc.dram_tensor("h1_h", [4096, 2 * LH], BF16, kind="Internal").ap()
    ccd_i = nc.dram_tensor("ccd_i", [96, L], F32, kind="Internal").ap()
    ccd_o = nc.dram_tensor("ccd_o", [96, L], F32, kind="Internal").ap()
    ccw_i = nc.dram_tensor("ccw_i", [DM, L], F32, kind="Internal").ap()
    ccw_o = nc.dram_tensor("ccw_o", [DM, L], F32, kind="Internal").ap()
    if dbg:
        dbg_d = nc.dram_tensor("dbg_d", [96, L], F32, kind="ExternalOutput").ap()
        dbg_w = nc.dram_tensor("dbg_w", [DM, L], F32, kind="ExternalOutput").ap()
        dbg_s = nc.dram_tensor("dbg_s", [128, 512], F32, kind="ExternalOutput").ap()
    RG = [[0, 1], [2, 3], [4, 5], [6, 7]]

    import contextlib
    ctx = contextlib.ExitStack()
    with ctx:
        AR1 = ctx.enter_context(nc.sbuf_tensor("AR1", [128, CW], F32))
        AR2 = ctx.enter_context(nc.sbuf_tensor("AR2", [128, 4096], F32))
        AR3 = ctx.enter_context(nc.sbuf_tensor("AR3", [128, KT * 516], F32))
        AR4 = ctx.enter_context(nc.sbuf_tensor("AR4", [128, 4 * CW], F32))
        AR5 = ctx.enter_context(nc.sbuf_tensor("AR5", [128, 8192], F32))
        AR6 = ctx.enter_context(nc.sbuf_tensor("AR6", [128, 7680], F32))
        AR7 = ctx.enter_context(nc.sbuf_tensor("AR7", [128, CW], F32))
        state = ctx.enter_context(nc.sbuf_tensor("state", [128, NST * KT], F32))
        vecs = ctx.enter_context(nc.sbuf_tensor("vecs", [128, 64], F32))
        acol = ctx.enter_context(nc.sbuf_tensor("acol", [128, 128], F32))
        atmp = ctx.enter_context(nc.sbuf_tensor("atmp", [128, 128], F32))
        wdt_sb = ctx.enter_context(nc.sbuf_tensor("wdt_sb", [64, DL], F32))
        wx_sb = ctx.enter_context(nc.sbuf_tensor("wx_sb", [128, KT * 96], F32))
        b1_sb = ctx.enter_context(nc.sbuf_tensor("b1_sb", [128, 32], F32))
        b2_sb = ctx.enter_context(nc.sbuf_tensor("b2_sb", [128, 8], F32))
        ones1_sb = ctx.enter_context(nc.sbuf_tensor("ones1_sb", [128, 128], F32))
        onesc_sb = ctx.enter_context(nc.sbuf_tensor("onesc_sb", [128, 1], F32))
        eye_sb = ctx.enter_context(nc.sbuf_tensor("eye_sb", [128, 128], F32))
        m0_sb = ctx.enter_context(nc.sbuf_tensor("m0_sb", [128, 1], F32))
        m1_sb = ctx.enter_context(nc.sbuf_tensor("m1_sb", [128, 1], F32))
        cvals = ctx.enter_context(nc.sbuf_tensor("cvals_sb", [128, 2], F32))
        nc.const_aps.register(F32, EPS, cvals[:, 0:1])
        nc.const_aps.register(F32, 1.0, cvals[:, 1:2])
        rs1 = ctx.enter_context(nc.sbuf_tensor("rs1", [1, T], F32))
        rs2 = rs1
        sc1 = ctx.enter_context(nc.sbuf_tensor("sc1", [128, 8], F32))
        gq = ctx.enter_context(nc.sbuf_tensor("gq", [128, 4], F32))

        wq = AR1

        pA0 = ctx.enter_context(nc.psum_tensor("pA0", [128, T], F32))
        pA1 = ctx.enter_context(nc.psum_tensor("pA1", [128, T], F32))
        pdbc = ctx.enter_context(nc.psum_tensor("pdbc", [128, T], F32))
        pbc = ctx.enter_context(nc.psum_tensor("pbc", [128, T], F32))
        pdel = ctx.enter_context(nc.psum_tensor("pdel", [128, T], F32))
        pwo0 = ctx.enter_context(nc.psum_tensor("pwo0", [128, T], F32))
        pwo1 = ctx.enter_context(nc.psum_tensor("pwo1", [128, T], F32))

        sems = {s: ctx.enter_context(nc.semaphore(s)) for s in SEMS}
        blk = ctx.enter_context(nc.Block())

        scarr = AR6[:, 0:768]
        bcst = AR7[:, 0:2048]

        pr = Prog()
        W = pr.c

        def ln_pass_s(xsl, tt, w0):
            pr.add("V", lambda e, xsl=xsl, tt=tt: e.tensor_reduce(
                scarr[:, tt:tt + 1], xsl, AXX, AOP.add), waits=w0, inc="V")
            pr.add("V", lambda e, xsl=xsl: e.tensor_tensor(AR2[:, 0:1024], xsl, xsl, AOP.mult),
                   waits=[("T", W("T")), ("A", W("A"))], inc="V")
            pr.add("V", lambda e, tt=tt: e.tensor_reduce(
                scarr[:, 128 + tt:129 + tt], AR2[:, 0:1024], AXX, AOP.add), inc="V")

        def ln_pass_m():
            pr.add("V", lambda e: e.tensor_scalar_mul(scarr[:, 256:384], scarr[:, 0:128], 1.0 / DM), inc="V")
            pr.add("V", lambda e: e.tensor_tensor(scarr[:, 384:512], scarr[:, 256:384],
                                                  scarr[:, 256:384], AOP.mult), inc="V")
            pr.add("V", lambda e: e.tensor_scalar_mul(scarr[:, 512:640], scarr[:, 128:256], 1.0 / DM), inc="V")
            pr.add("V", lambda e: e.tensor_sub(scarr[:, 512:640], scarr[:, 512:640],
                                               scarr[:, 384:512]), inc="V")
            vW = W("V")
            pr.add("A", lambda e: e.activation(scarr[:, 640:768], scarr[:, 512:640], ACT.Ln, bias=EPS),
                   waits=[("V", vW)], inc="A")
            pr.add("A", lambda e: e.activation(scarr[:, 640:768], scarr[:, 640:768], ACT.Exp, scale=-0.5),
                   inc="A")
            return W("A")

        def ln_pass_n(xsl, xnsl, tt, w0):
            pr.add("V", lambda e, xsl=xsl, xnsl=xnsl, tt=tt: e.tensor_scalar(
                xnsl, xsl, scarr[:, 256 + tt:257 + tt], scarr[:, 640 + tt:641 + tt],
                AOP.subtract, AOP.mult), waits=w0, inc="V")
            return W("V")

        for dst, src in [(cvals[:], cv_in), (ones1_sb[:], ones1), (onesc_sb[:], onescol),
                         (eye_sb[:], eye_in), (b1_sb[:], b1c), (b2_sb[:], b2c),
                         (m0_sb[:], m0_in), (m1_sb[:], m1_in)]:
            pr.add("S", lambda e, dst=dst, src=src: e.dma_start(out=dst, in_=src), inc="S", amt=16)

        for tt in range(L // 128):
            xsl = AR1[:, (tt % 4) * 1024:(tt % 4) * 1024 + 1024]
            pr.add("S", lambda e, xsl=xsl, tt=tt: e.dma_start(out=xsl, in_=x_full[tt * 128:(tt + 1) * 128, :]),
                   waits=[("V", W("V"))], inc="S", amt=16)
            sW = W("S")
            ln_pass_s(xsl, tt, [("S", sW)])
        aW_ln = ln_pass_m()
        for tt in range(L // 128):
            xsl = AR1[:, (tt % 4) * 1024:(tt % 4) * 1024 + 1024]
            pr.add("S", lambda e, xsl=xsl, tt=tt: e.dma_start(out=xsl, in_=x_full[tt * 128:(tt + 1) * 128, :]),
                   waits=[("V", W("V"))], inc="S", amt=16)
            sW = W("S")
            xnsl = AR2[:, 0:1024]
            ln_pass_n(xsl, xnsl, tt, [("S", sW), ("A", aW_ln), ("T", W("T"))])
            vW = W("V")
            prevG = W("G")
            for a in range(8):
                bank = pdel if a % 2 == 0 else pbc
                pr.add("T", lambda e, bank=bank, xnsl=xnsl, a=a: e.transpose(
                    bank[:, 0:128], xnsl[:, a * 128:(a + 1) * 128], eye_sb[:]),
                    waits=[("V", vW), ("A", W("A")), ("G", prevG)], inc="T")
                tW = W("T")
                pr.add("A", lambda e, a=a, bank=bank: e.copy(AR3[:, a * 128:(a + 1) * 128], bank[:, 0:128]),
                       waits=[("T", tW)], inc="A")
            aW = W("A")
            pr.add("G", lambda e, tt=tt: e.dma_start(
                out=xm_h[:, tt * 128:(tt + 1) * 128].rearrange("(a p) t -> p a t", p=128),
                in_=AR3[:, 0:1024].rearrange("p (a t) -> p a t", a=8)),
                waits=[("A", aW)], inc="G", amt=16)

        def spacer():
            pr.add("V", lambda e: e.tensor_copy(AR2[:, 1024:1536], AR2[:, 1536:2048]), inc="V")

        gam_cons = {0: W("A"), 1: W("A")}
        for gi, (wt, n_t, ldfn) in enumerate([
            (w1T, 8, lambda e, t, wsl: e.dma_start(out=wsl, in_=w1T[t * 128:(t + 1) * 128, :])),
            (w2T, 8, lambda e, t, wsl: e.dma_start(
                out=wsl.rearrange("p (a m) -> p a m", a=4),
                in_=w2T[t * 512:(t + 1) * 512, :].rearrange("(a p) m -> p a m", p=128))),
        ]):
            for t in range(n_t):
                wsl = AR5[:, (t % 2) * 4096:(t % 2) * 4096 + 4096]
                pr.add("S", lambda e, t=t, wsl=wsl, ldfn=ldfn: ldfn(e, t, wsl),
                       waits=[("A", gam_cons[t % 2])], inc="S", amt=16)
                sW = W("S")
                pr.add("A", lambda e, wsl=wsl: e.activation(AR2[:, 0:4096], wsl, ACT.Abs),
                       waits=[("S", sW), ("V", W("V"))], inc="A")
                gam_cons[t % 2] = W("A")
                aW = W("A")
                pr.add("V", lambda e, t=t, gi=gi: e.tensor_reduce(
                    atmp[:, 64 + gi * 8 + t:64 + gi * 8 + t + 1], AR2[:, 0:4096], AXX, AOP.add),
                    waits=[("A", aW)], inc="V")
            spacer()
            pr.add("V", lambda e, gi=gi: e.tensor_reduce(
                gq[:, gi:gi + 1], atmp[:, 64 + gi * 8:64 + gi * 8 + 8], AXX, AOP.add), inc="V")
            spacer()
            vW = W("V")
            pr.add("T", lambda e, gi=gi: e.matmul(pbc[0:1, 0:1], onesc_sb[:], gq[:, gi:gi + 1],
                                                  start=True, stop=True), waits=[("V", vW)], inc="T")
            tW = W("T")
            pr.add("V", lambda e: e.tensor_scalar(gq[0:1, 3:4], pbc[0:1, 0:1], 1.0 / (4096.0 * DM), EPS,
                                                  AOP.mult, AOP.add), waits=[("T", tW)], inc="V")
            spacer()
            pr.add("V", lambda e: e.reciprocal(gq[0:1, 2:3], gq[0:1, 3:4]), inc="V")
            spacer()
            vW = W("V")
            pr.add("T", lambda e: e.matmul(pbc[:, 0:1], ones1_sb[0:1, :], gq[0:1, 2:3],
                                           start=True, stop=True), waits=[("V", vW)], inc="T")
            tW = W("T")
            pr.add("A", lambda e, gi=gi: e.copy(gq[:, gi:gi + 1], pbc[:, 0:1]), waits=[("T", tW)], inc="A")
        g1A = g2A = W("A")


        for li in range(nl):
            pr.add("S", lambda e, li=li: e.dma_start(out=vecs[:, 0:32], in_=convw[li]),
                   waits=[("V", W("V")), ("A", W("A"))], inc="S", amt=16)
            pr.add("S", lambda e, li=li: e.dma_start(out=vecs[:, 32:40], in_=convb[li]), inc="S", amt=16)
            pr.add("S", lambda e, li=li: e.dma_start(out=vecs[:, 40:48], in_=bdt[li]), inc="S", amt=16)
            pr.add("S", lambda e, li=li: e.dma_start(out=vecs[:, 48:56], in_=dvec[li]), inc="S", amt=16)
            pr.add("S", lambda e, li=li: e.dma_start(out=vecs[:, 56:64], in_=normw[li]), inc="S", amt=16)
            pr.add("S", lambda e, li=li: e.dma_start(out=atmp[:], in_=alog[li]), inc="S", amt=16)
            pr.add("S", lambda e, li=li: e.dma_start(out=wdt_sb[:], in_=wdtT[li]),
                   waits=[("T", W("T"))], inc="S", amt=16)
            pr.add("S", lambda e, li=li: e.dma_start(
                out=wx_sb[:].rearrange("p (a m) -> p a m", a=KT),
                in_=wxT[li].rearrange("(a p) m -> p a m", p=128)), inc="S", amt=16)
            pr.add("S", lambda e, li=li: e.dma_start(
                out=AR4[:, 0:16384].rearrange("p (a m) -> p a m", a=8),
                in_=winT[li].rearrange("(a p) m -> p a m", p=128)),
                waits=[("V", W("V"))], inc="S", amt=16)
            sWL = W("S")
            pr.add("A", lambda e: e.activation(acol[:], atmp[:], ACT.Exp), waits=[("S", sWL)], inc="A")
            pr.add("V", lambda e: e.tensor_scalar_mul(acol[:], acol[:], -1.0),
                   waits=[("A", W("A"))], inc="V")
            pr.add("V", lambda e: e.memset(state[:], 0.0), inc="V")
            pr.add("V", lambda e: e.memset(
                AR3[:].rearrange("p (a q) -> p a q", a=KT)[:, :, 0:3], 0.0),
                waits=[("A", W("A")), ("G", W("G"))], inc="V")

            evac_cnt = {0: 0, 1: 0}
            for c in range(NCH):
                cs = c * T
                pr.add("S", lambda e, cs=cs: e.dma_start(
                    out=AR1[:, 0:4096].rearrange("p (a t) -> p a t", a=8),
                    in_=xm_h[:, cs:cs + T].rearrange("(a p) t -> p a t", p=128)),
                    waits=[("V", W("V")), ("T", W("T")), ("G", W("G"))], inc="S", amt=16)
                sW = W("S")
                pr.add("A", lambda e: e.activation(AR2[:, 0:4096], AR1[:, 0:4096], ACT.Square),
                       waits=[("S", sW), ("T", W("T")), ("V", W("V"))], inc="A")
                aW = W("A")
                for j in range(8):
                    pr.add("T", lambda e, j=j: e.matmul(pbc[0:1, :], onesc_sb[:], AR2[:, j * T:(j + 1) * T],
                                                        start=(j == 0), stop=(j == 7)),
                           waits=([("A", aW)] if j == 0 else []), inc="T")
                tW = W("T")
                pr.add("A", lambda e: e.activation(rs1[:], pbc[0:1, :], ACT.Ln, scale=1.0 / DM, bias=EPS),
                       waits=[("T", tW)], inc="A")
                pr.add("A", lambda e: e.activation(rs2[:], rs1[:], ACT.Exp, scale=-0.5), inc="A")
                aW = W("A")
                pr.add("T", lambda e: e.matmul(pdel[:], ones1_sb[0:1, :], rs2[:], start=True, stop=True),
                       waits=[("A", aW)], inc="T")
                tW = W("T")
                for j in range(8):
                    pr.add("V", lambda e, j=j: e.scalar_tensor_tensor(
                        AR2[:, j * T:(j + 1) * T], AR1[:, j * T:(j + 1) * T], vecs[:, 56 + j:57 + j],
                        pdel[:], AOP.mult, AOP.mult), waits=([("T", tW)] if j == 0 else []), inc="V")
                hW = W("V")
                silu_done = {}
                for m in range(16):
                    bank = pA0 if m % 2 == 0 else pA1
                    for k in range(8):
                        pr.add("T", lambda e, m=m, k=k, bank=bank: e.matmul(
                            bank[:], AR4[:, k * 2048 + m * 128:k * 2048 + (m + 1) * 128],
                            AR2[:, k * T:(k + 1) * T], start=(k == 0), stop=(k == 7)),
                            waits=([("V", hW), ("A", evac_cnt[m % 2])] if k == 0 else []), inc="T")
                    tW = W("T")
                    if m < 8:
                        base = m * 516
                        xsl = AR3[:, base + 3:base + 515]
                        w_extra = [("G", W("G"))] if (li == 0 and c == 0 and m == 0) else []
                        pr.add("A", lambda e, xsl=xsl, bank=bank: e.copy(xsl, bank[:]),
                               waits=[("T", tW), ("V", W("V"))] + w_extra, inc="A")
                        evac_cnt[m % 2] = W("A")
                        aW = W("A")
                        xcm = AR6[:, 4096 + (m % 2) * T:4096 + (m % 2) * T + T]
                        pr.add("V", lambda e, xcm=xcm, base=base, m=m: e.tensor_scalar_mul(
                            xcm, AR3[:, base:base + T], vecs[:, m * 4:m * 4 + 1]),
                            waits=[("A", aW)], inc="V")
                        for s in range(1, 4):
                            pr.add("V", lambda e, xcm=xcm, base=base, m=m, s=s: e.scalar_tensor_tensor(
                                xcm, AR3[:, base + s:base + s + T], vecs[:, m * 4 + s:m * 4 + s + 1],
                                xcm, AOP.mult, AOP.add), inc="V")
                        vW = W("V")
                        xcs = AR6[:, m * T:(m + 1) * T]
                        pr.add("A", lambda e, xcs=xcs, xcm=xcm, m=m: e.activation(
                            xcs, xcm, ACT.Silu, bias=vecs[:, 32 + m:33 + m]),
                            waits=[("V", vW), ("G", W("G")), ("T", W("T"))], inc="A")
                        silu_done[m] = W("A")
                        aW = W("A")
                        pr.add("G", lambda e, xcs=xcs, m=m, cs=cs: e.dma_start(
                            out=xc_h[m * 128:(m + 1) * 128, cs:cs + T], in_=xcs),
                            waits=[("A", aW)], inc="G", amt=16)
                        pr.add("V", lambda e, base=base: e.tensor_copy(
                            AR3[:, base:base + 3], AR3[:, base + 512:base + 515]), inc="V")
                    else:
                        mm = m - 8
                        szs = AR6[:, 4096 + (m % 2) * T:4096 + (m % 2) * T + T]
                        pr.add("A", lambda e, szs=szs, bank=bank: e.activation(szs, bank[:], ACT.Silu),
                               waits=[("T", tW), ("G", W("G"))], inc="A")
                        evac_cnt[m % 2] = W("A")
                        aW = W("A")
                        pr.add("G", lambda e, szs=szs, mm=mm, cs=cs: e.dma_start(
                            out=zs_h[mm * 128:(mm + 1) * 128, cs:cs + T], in_=szs),
                            waits=[("A", aW)], inc="G", amt=16)
                for m in range(8):
                    pr.add("T", lambda e, m=m: e.matmul(
                        pdbc[0:96, :], wx_sb[:, m * 96:(m + 1) * 96], AR6[:, m * T:(m + 1) * T],
                        start=(m == 0), stop=(m == 7)),
                        waits=([("A", silu_done[7])] if m == 0 else []), inc="T")
                tW = W("T")
                pr.add("A", lambda e, cs=cs: e.copy(AR7[0:96, cs:cs + T], pdbc[0:96, :]),
                       waits=[("T", tW)], inc="A")
            aW = W("A")
            pr.add("G", lambda e: e.dma_start(out=ccd_i, in_=AR7[0:96, 0:4096]),
                   waits=[("A", aW)], inc="G", amt=16)
            gW = W("G")
            pr.add("G", lambda e: e.collective_compute("AllReduce", AOP.add, replica_groups=RG,
                                                       ins=[ccd_i], outs=[ccd_o]),
                   waits=[("G", gW)], inc="G", amt=1)
            gW = W("G")
            pr.add("G", lambda e: e.dma_start(out=AR2[0:96, 0:4096], in_=ccd_o),
                   waits=[("G", gW), ("V", W("V")), ("T", W("T")), ("A", W("A"))], inc="G", amt=16)
            dbcG = W("G")
            pr.add("S", lambda e, li=li: e.dma_start(
                out=AR5[:].rearrange("p (a m) -> p a m", a=KT),
                in_=woutT[li].rearrange("(a p) m -> p a m", p=128)),
                waits=[("A", W("A")), ("T", W("T"))], inc="S", amt=16)
            woutS = W("S")

            U8 = AR4[:, 0 * CW:1 * CW]
            DA8 = AR4[:, 1 * CW:2 * CW]
            BX8 = AR4[:, 2 * CW:3 * CW]
            H8 = AR4[:, 3 * CW:4 * CW]
            pr.add("V", lambda e: e.memset(DA8.rearrange("p (a q) -> p a q", a=KT)[:, :, 0:1], 0.0),
                   waits=[("T", W("T"))], inc="V")
            bx_done = W("V")
            sc_done = W("V")
            ch_done = W("V")
            bc_consumed = {0: 0, 1: 0}
            cc_consumed = {0: 0, 1: 0}
            for c in range(NCH):
                cs = c * T
                for k in range(KT):
                    pr.add("T", lambda e, k=k, cs=cs: e.matmul(
                        pdel[:], wdt_sb[:, k * 128:(k + 1) * 128], AR2[0:64, cs:cs + T],
                        start=True, stop=True),
                        waits=[("G", dbcG), ("A", W("A")), ("V", W("V"))], inc="T")
                    tW = W("T")
                    tmp = AR6[:, 7168:7680]
                    pr.add("A", lambda e, tmp=tmp, k=k: e.activation(
                        tmp, pdel[:], ACT.Exp, bias=vecs[:, 40 + k:41 + k]),
                        waits=[("T", tW)], inc="A")
                    dsl = AR6[:, k * T:(k + 1) * T]
                    pr.add("A", lambda e, dsl=dsl, tmp=tmp: e.activation(dsl, tmp, ACT.Ln, bias=1.0), inc="A")
                    aW = W("A")
                    xck = AR6[:, 4096 + (k % 2) * T:4096 + (k % 2) * T + T]
                    pr.add("A", lambda e, xck=xck, k=k, cs=cs: e.dma_start(
                        out=xck, in_=xc_h[k * 128:(k + 1) * 128, cs:cs + T]),
                        waits=[("V", W("V"))], inc="AD", amt=16)
                    adW = W("AD")
                    pr.add("V", lambda e, k=k, dsl=dsl, xck=xck: e.tensor_tensor(
                        U8[:, k * SL + 1:k * SL + 1 + T], dsl, xck, AOP.mult),
                        waits=[("A", aW), ("AD", adW), ("T", W("T"))], inc="V")
                    pr.add("V", lambda e, k=k, xck=xck: e.tensor_scalar_mul(
                        AR1[:, k * SL + 1:k * SL + 1 + T], xck, vecs[:, 48 + k:49 + k]), inc="V")
                for n in range(NST):
                    bslot = bcst[:, (n % 2) * T:(n % 2) * T + T]
                    cslot = bcst[:, 1024 + (n % 2) * T:1024 + (n % 2) * T + T]
                    pr.add("S", lambda e, n=n, cs=cs, bslot=bslot: e.dma_start(
                        out=bslot, in_=ccd_o[64 + n:65 + n, cs:cs + T].to_broadcast([128, T])),
                        waits=[("V", bc_consumed[n % 2]), ("G", dbcG)], inc="S", amt=16)
                    sWb = W("S")
                    pr.add("G", lambda e, n=n, cs=cs, cslot=cslot: e.dma_start(
                        out=cslot, in_=ccd_o[80 + n:81 + n, cs:cs + T].to_broadcast([128, T])),
                        waits=[("V", cc_consumed[n % 2])], inc="G", amt=16)
                    gWc = W("G")
                    for k in range(KT):
                        pr.add("A", lambda e, k=k, n=n: e.activation(
                            DA8[:, k * SL + 1:k * SL + 1 + T], AR6[:, k * T:(k + 1) * T], ACT.Exp,
                            scale=acol[:, k * NST + n:k * NST + n + 1]),
                            waits=([("V", sc_done)] if k == 0 else []), inc="A")
                    aW = W("A")
                    pr.add("V", lambda e, n=n: e.tensor_copy(
                        BX8.rearrange("p (a q) -> p a q", a=KT)[:, :, 0],
                        state[:, n * KT:(n + 1) * KT]), waits=[("A", aW), ("S", sWb)], inc="V")
                    for k in range(KT):
                        pr.add("V", lambda e, k=k, bslot=bslot: e.tensor_tensor(
                            BX8[:, k * SL + 1:k * SL + 1 + T], U8[:, k * SL + 1:k * SL + 1 + T],
                            bslot, AOP.mult), inc="V")
                    bc_consumed[n % 2] = W("V")
                    bx_done = W("V")
                    pr.add("V", lambda e: e.tensor_tensor_scan(H8[:], DA8[:], BX8[:], 0.0, AOP.mult, AOP.add),
                           inc="V")
                    sc_done = W("V")
                    for k in range(KT):
                        pr.add("V", lambda e, k=k, cslot=cslot: e.tensor_tensor(
                            BX8[:, k * SL + 1:k * SL + 1 + T], H8[:, k * SL + 1:k * SL + 1 + T],
                            cslot, AOP.mult), waits=([("G", gWc)] if k == 0 else []), inc="V")
                    cc_consumed[n % 2] = W("V")
                    ch_done = W("V")
                    pr.add("V", lambda e: e.tensor_tensor(
                        AR1.rearrange("p (a q) -> p a q", a=KT)[:, :, 1:SL],
                        AR1.rearrange("p (a q) -> p a q", a=KT)[:, :, 1:SL],
                        BX8.rearrange("p (a q) -> p a q", a=KT)[:, :, 1:SL], AOP.add), inc="V")
                    pr.add("V", lambda e, n=n: e.tensor_copy(
                        state[:, n * KT:(n + 1) * KT],
                        H8.rearrange("p (a q) -> p a q", a=KT)[:, :, SL - 1]), inc="V")
                for k in range(KT):
                    szk = AR6[:, 5120 + (k % 2) * T:5120 + (k % 2) * T + T]
                    pr.add("S", lambda e, szk=szk, k=k, cs=cs: e.dma_start(
                        out=szk, in_=zs_h[k * 128:(k + 1) * 128, cs:cs + T]),
                        waits=[("V", W("V"))], inc="S", amt=16)
                    sW = W("S")
                    pr.add("V", lambda e, k=k, szk=szk: e.tensor_tensor(
                        AR1[:, k * SL + 1:k * SL + 1 + T], AR1[:, k * SL + 1:k * SL + 1 + T],
                        szk, AOP.mult), waits=[("S", sW)], inc="V")
                gV = W("V")
                for m2 in range(8):
                    bank = pwo0 if m2 % 2 == 0 else pwo1
                    for k in range(KT):
                        pr.add("T", lambda e, m2=m2, k=k, bank=bank: e.matmul(
                            bank[:], AR5[:, k * 1024 + m2 * 128:k * 1024 + (m2 + 1) * 128],
                            AR1[:, k * SL + 1:k * SL + 1 + T], start=(k == 0), stop=(k == KT - 1)),
                            waits=([("V", gV), ("S", woutS), ("A", W("A"))] if k == 0 else []), inc="T")
                    tW = W("T")
                    osl = AR6[:, 6144 + (m2 % 2) * T:6144 + (m2 % 2) * T + T]
                    pr.add("A", lambda e, osl=osl, bank=bank: e.copy(osl, bank[:]),
                           waits=[("T", tW), ("G", W("G"))], inc="A")
                    aW = W("A")
                    pr.add("G", lambda e, osl=osl, m2=m2, cs=cs: e.dma_start(
                        out=ccw_i[m2 * 128:(m2 + 1) * 128, cs:cs + T], in_=osl),
                        waits=[("A", aW)], inc="G", amt=16)
            gW = W("G")
            pr.add("G", lambda e: e.collective_compute("AllReduce", AOP.add, replica_groups=RG,
                                                       ins=[ccw_i], outs=[ccw_o]),
                   waits=[("G", gW)], inc="G", amt=1)
            woG = W("G")
            if dbg:
                pr.add("G", lambda e: e.dma_start(out=dbg_d, in_=ccd_o), inc="G", amt=16)
                pr.add("G", lambda e: e.dma_start(out=dbg_w, in_=ccw_o), inc="G", amt=16)
            upd_v = {0: None, 1: None}
            upd_g = {0: 0, 1: 0}
            for c in range(NCH):
                cs = c * T
                for h2 in range(2):
                    hs = cs + h2 * 256
                    slot = (c * 2 + h2) % 2
                    off = slot * 2048
                    w0 = ([("V", upd_v[slot]), ("G", upd_g[slot])] if upd_v[slot] else [
                        ("V", W("V")), ("T", W("T")), ("A", W("A")), ("G", W("G"))])
                    pr.add("S", lambda e, hs=hs, off=off: e.dma_start(
                        out=AR1[:, off:off + 2048].rearrange("p (a t) -> p a t", a=8),
                        in_=xm_h[:, hs:hs + 256].rearrange("(a p) t -> p a t", p=128)),
                        waits=w0, inc="S", amt=16)
                    pr.add("S", lambda e, hs=hs, off=off: e.dma_start(
                        out=AR2[:, off:off + 2048].rearrange("p (a t) -> p a t", a=8),
                        in_=ccw_o[:, hs:hs + 256].rearrange("(a p) t -> p a t", p=128)),
                        waits=[("G", woG)] if (c == 0 and h2 == 0) else [], inc="S", amt=16)
                    sW = W("S")
                    pr.add("V", lambda e, off=off: e.tensor_tensor(
                        AR1[:, off:off + 2048], AR1[:, off:off + 2048], AR2[:, off:off + 2048], AOP.add),
                        waits=[("S", sW)], inc="V")
                    vW = W("V")
                    upd_v[slot] = vW
                    pr.add("G", lambda e, hs=hs, off=off: e.dma_start(
                        out=xm_h[:, hs:hs + 256].rearrange("(a p) t -> p a t", p=128),
                        in_=AR1[:, off:off + 2048].rearrange("p (a t) -> p a t", a=8)),
                        waits=[("V", vW)], inc="G", amt=16)
                    upd_g[slot] = W("G")

        xnh = AR4[:, 0:8192].bitcast(BF16)
        xnl = AR4[:, 8208:16400].bitcast(BF16)
        xmG = W("G")
        for tt in range(LH // 128):
            xsl = AR1[:, (tt % 2) * 1024:(tt % 2) * 1024 + 1024]
            pr.add("S", lambda e, xsl=xsl, tt=tt: e.dma_start(out=xsl, in_=x_half[tt * 128:(tt + 1) * 128, :]),
                   waits=[("V", W("V")), ("G", xmG), ("T", W("T")), ("A", W("A"))], inc="S", amt=16)
            sW = W("S")
            ln_pass_s(xsl, tt, [("S", sW)])
        aW_ln = ln_pass_m()
        for tt in range(LH // 128):
            xsl = AR1[:, (tt % 2) * 1024:(tt % 2) * 1024 + 1024]
            pr.add("S", lambda e, xsl=xsl, tt=tt: e.dma_start(out=xsl, in_=x_half[tt * 128:(tt + 1) * 128, :]),
                   waits=[("V", W("V"))], inc="S", amt=16)
            sW = W("S")
            xnsl = AR2[:, 0:1024]
            ln_pass_n(xsl, xnsl, tt, [("S", sW), ("A", aW_ln), ("T", W("T"))])
            vW = W("V")
            for a in range(8):
                bank = pdel if a % 2 == 0 else pbc
                pr.add("T", lambda e, a=a, bank=bank, xnsl=xnsl: e.transpose(
                    bank[:, 0:128], xnsl[:, a * 128:(a + 1) * 128], eye_sb[:]),
                    waits=[("V", vW), ("A", W("A"))], inc="T")
                tW = W("T")
                pr.add("A", lambda e, a=a, tt=tt, bank=bank: e.copy(
                    xnh[:, a * LH + tt * 128:a * LH + (tt + 1) * 128], bank[:, 0:128]),
                    waits=[("T", tW)], inc="A")
                aW2 = W("A")
                pr.add("V", lambda e, a=a, tt=tt, bank=bank: e.tensor_tensor(
                    xnl[:, a * LH + tt * 128:a * LH + (tt + 1) * 128], bank[:, 0:128],
                    xnh[:, a * LH + tt * 128:a * LH + (tt + 1) * 128], AOP.subtract),
                    waits=[("A", aW2)], inc="V")

        h1hi = AR2[:, 2048:3072].bitcast(BF16)
        h1lo = AR2[:, 3072:4096].bitcast(BF16)
        for m in range(32):
            slot = m % 2
            wsl = AR5[:, slot * 4096:slot * 4096 + 1024]
            pr.add("S", lambda e, wsl=wsl, m=m: e.dma_start(
                out=wsl.rearrange("p (a q) -> p a q", a=8),
                in_=w1T[:, m * 128:(m + 1) * 128].rearrange("(a p) q -> p a q", p=128)),
                waits=[("V", W("V"))], inc="S", amt=16)
            sW = W("S")
            wqs = wq[:, 2048 + slot * 1024:2048 + slot * 1024 + 1024]
            wqb = wq[:, slot * 512:slot * 512 + 512].bitcast(BF16)
            pr.add("V", lambda e, wsl=wsl, wqs=wqs: e.tensor_scalar_mul(wqs, wsl, gq[:, 0:1]),
                   waits=[("S", sW), ("A", g1A), ("T", W("T"))], inc="V")
            pr.add("V", lambda e, wqs=wqs, wsl=wsl: e.tensor_single_scalar(wsl, wqs, 0.5, AOP.is_gt), inc="V")
            pr.add("V", lambda e, wqs=wqs: e.tensor_single_scalar(wqs, wqs, -0.5, AOP.is_lt), inc="V")
            pr.add("V", lambda e, wqs=wqs, wsl=wsl, wqb=wqb: e.tensor_sub(wqb, wsl, wqs), inc="V")
            vW = W("V")
            for cg in range(4):
                bank = pA0 if cg % 2 == 0 else pA1
                for k in range(8):
                    for hl, xsrc in ((0, xnh), (1, xnl)):
                        pr.add("T", lambda e, k=k, cg=cg, bank=bank, wqb=wqb, xsrc=xsrc: e.matmul(
                            bank[:], wqb[:, k * 128:(k + 1) * 128],
                            xsrc[:, k * LH + cg * T:k * LH + (cg + 1) * T],
                            start=(k == 0 and xsrc is xnh), stop=(k == 7 and xsrc is xnl)),
                            waits=([("V", vW), ("A", W("A"))] if (k == 0 and hl == 0) else []), inc="T")
                tW = W("T")
                pr.add("A", lambda e, cg=cg, bank=bank, m=m: e.activation(
                    AR2[:, cg * T:(cg + 1) * T], bank[:], ACT.Relu, bias=b1_sb[:, m:m + 1]),
                    waits=[("T", tW), ("G", W("G"))], inc="A")
            aW = W("A")
            for cg in range(4):
                pr.add("A", lambda e, cg=cg: e.copy(h1hi[:, cg * T:(cg + 1) * T], AR2[:, cg * T:(cg + 1) * T]),
                       inc="A")
                aW2 = W("A")
                pr.add("V", lambda e, cg=cg: e.tensor_tensor(
                    h1lo[:, cg * T:(cg + 1) * T], AR2[:, cg * T:(cg + 1) * T],
                    h1hi[:, cg * T:(cg + 1) * T], AOP.subtract), waits=[("A", aW2)], inc="V")
            vW2 = W("V")
            pr.add("G", lambda e, m=m: e.dma_start(out=h1_h[m * 128:(m + 1) * 128, 0:LH], in_=h1hi[:]),
                   waits=[("V", vW2)], inc="G", amt=16)
            pr.add("G", lambda e, m=m: e.dma_start(out=h1_h[m * 128:(m + 1) * 128, LH:2 * LH], in_=h1lo[:]),
                   inc="G", amt=16)

        for m2 in range(8):
            w2sl = AR5[:, 4096:8192]
            pr.add("S", lambda e, w2sl=w2sl, m2=m2: e.dma_start(
                out=w2sl.rearrange("p (a q) -> p a q", a=32),
                in_=w2T[:, m2 * 128:(m2 + 1) * 128].rearrange("(a p) q -> p a q", p=128)),
                waits=[("V", W("V")), ("T", W("T"))], inc="S", amt=16)
            sW = W("S")
            wqs = AR3[:, 0:4096]
            wq2b = wq[:, 0:2048].bitcast(BF16)
            pr.add("V", lambda e, w2sl=w2sl, wqs=wqs: e.tensor_scalar_mul(wqs, w2sl, gq[:, 1:2]),
                   waits=[("S", sW), ("A", g2A), ("T", W("T"))], inc="V")
            pr.add("V", lambda e, wqs=wqs, w2sl=w2sl: e.tensor_single_scalar(w2sl[:, 0:4096], wqs, 0.5, AOP.is_gt), inc="V")
            pr.add("V", lambda e, wqs=wqs: e.tensor_single_scalar(wqs, wqs, -0.5, AOP.is_lt), inc="V")
            pr.add("V", lambda e, wqs=wqs, w2sl=w2sl, wq2b=wq2b: e.tensor_sub(wq2b, w2sl[:, 0:4096], wqs), inc="V")
            vW = W("V")
            hread = {0: 0, 1: 0}
            for kf in range(32):
                hslot = kf % 2
                hhi = AR5[:, hslot * 2048:hslot * 2048 + 1024].bitcast(BF16)
                hlo = AR5[:, hslot * 2048 + 1024:hslot * 2048 + 2048].bitcast(BF16)
                pr.add("S", lambda e, hhi=hhi, kf=kf: e.dma_start(
                    out=hhi, in_=h1_h[kf * 128:(kf + 1) * 128, 0:LH]),
                    waits=[("T", hread[hslot]), ("G", W("G")), ("V", vW)], inc="S", amt=16)
                pr.add("S", lambda e, hlo=hlo, kf=kf: e.dma_start(
                    out=hlo, in_=h1_h[kf * 128:(kf + 1) * 128, LH:2 * LH]),
                    inc="S", amt=16)
                sW = W("S")
                for cg in range(4):
                    bank = [pA0, pA1, pwo0, pwo1][cg]
                    for hl, hsrc in ((0, hhi), (1, hlo)):
                        pr.add("T", lambda e, kf=kf, cg=cg, bank=bank, hsrc=hsrc, wq2b=wq2b, hl=hl: e.matmul(
                            bank[:], wq2b[:, kf * 128:(kf + 1) * 128], hsrc[:, cg * T:(cg + 1) * T],
                            start=(kf == 0 and hl == 0), stop=(kf == 31 and hl == 1)),
                            waits=([("S", sW)] + ([("V", vW), ("A", W("A"))] if (kf == 0 and hl == 0) else [])), inc="T")
                hread[hslot] = W("T")
            tW = W("T")
            for cg in range(4):
                bank = [pA0, pA1, pwo0, pwo1][cg]
                pr.add("A", lambda e, m2=m2, cg=cg, bank=bank: e.activation(
                    AR4[:, m2 * LH + cg * T:m2 * LH + (cg + 1) * T], bank[:], ACT.Silu,
                    bias=b2_sb[:, m2:m2 + 1]), waits=([("T", tW)] if cg == 0 else []), inc="A")
        gA = W("A")

        outG = []
        for tt in range(LH // 128):
            xsl = AR1[:, (tt % 2) * 1024:(tt % 2) * 1024 + 1024]
            pr.add("S", lambda e, xsl=xsl, tt=tt: e.dma_start(out=xsl, in_=x_half[tt * 128:(tt + 1) * 128, :]),
                   waits=[("V", W("V")), ("T", W("T"))], inc="S", amt=16)
            pr.add("S", lambda e, tt=tt: e.dma_start(
                out=AR2[:, 0:1024].rearrange("p (a t) -> p a t", a=8),
                in_=xm_h[:, tt * 128:(tt + 1) * 128].rearrange("(a p) t -> p a t", p=128)),
                waits=[("V", W("V")), ("G", W("G"))], inc="S", amt=16)
            pr.add("S", lambda e, tt=tt: e.dma_start(
                out=AR2[:, 1024:2048].rearrange("p (a t) -> p a t", a=8),
                in_=xm_h[:, LH + tt * 128:LH + (tt + 1) * 128].rearrange("(a p) t -> p a t", p=128)),
                inc="S", amt=16)
            sW = W("S")
            pr.add("V", lambda e: e.tensor_scalar_mul(AR2[:, 2048:3072], AR2[:, 1024:2048], m1_sb[:, 0:1]),
                   waits=[("S", sW), ("A", gA)], inc="V")
            pr.add("V", lambda e: e.scalar_tensor_tensor(
                AR2[:, 3072:4096], AR2[:, 0:1024], m0_sb[:, 0:1], AR2[:, 2048:3072],
                AOP.mult, AOP.add), inc="V")
            for a in range(8):
                pr.add("V", lambda e, a=a, tt=tt: e.tensor_tensor(
                    AR2[:, a * 128:(a + 1) * 128], AR2[:, 3072 + a * 128:3072 + (a + 1) * 128],
                    AR4[:, a * LH + tt * 128:a * LH + (tt + 1) * 128], AOP.mult), inc="V")
            vW = W("V")
            for a in range(8):
                bank = pdel if a % 2 == 0 else pbc
                pr.add("T", lambda e, a=a, bank=bank: e.transpose(
                    bank[:, 0:128], AR2[:, a * 128:(a + 1) * 128], eye_sb[:]),
                    waits=[("V", vW), ("A", W("A"))], inc="T")
                tW = W("T")
                pr.add("A", lambda e, a=a, bank=bank, tt=tt: e.copy(
                    AR3[:, 2048 + (tt % 2) * 1024 + a * 128:2048 + (tt % 2) * 1024 + (a + 1) * 128],
                    bank[:, 0:128]),
                    waits=[("T", tW)] + ([("G", outG[tt - 2])] if tt >= 2 and a == 0 else []), inc="A")
            aW = W("A")
            osl = AR3[:, 2048 + (tt % 2) * 1024:2048 + (tt % 2) * 1024 + 1024]
            pr.add("V", lambda e, osl=osl, xsl=xsl: e.tensor_tensor(osl, osl, xsl, AOP.add),
                   waits=[("A", aW)], inc="V")
            vW = W("V")
            pr.add("G", lambda e, osl=osl, tt=tt: e.dma_start(out=out_t[tt * 128:(tt + 1) * 128, :], in_=osl),
                   waits=[("V", vW)], inc="G", amt=16)
            outG.append(W("G"))

        pr.add("A", lambda e: e.copy(sc1[:, 0:1], sc1[:, 0:1]),
               waits=[("G", W("G")), ("S", W("S"))], inc="A")

        cross_waits = {s: set() for s in SEMS}
        for (eng, fn, waits, inc, amt) in pr.steps:
            for (s, v) in waits:
                cross_waits[s].add((v, eng))
        cnt_replay = dict.fromkeys(SEMS, 0)
        needs_drain = []
        for (eng, fn, waits, inc, amt) in pr.steps:
            nd = False
            if inc is not None:
                lo = cnt_replay[inc]
                cnt_replay[inc] += amt
                hi = cnt_replay[inc]
                if eng in ("V", "A"):
                    for (v, weng) in cross_waits[inc]:
                        if weng != eng and lo < v <= hi:
                            nd = True
                            break
            needs_drain.append(nd)

        def emitter(name):
            def go(E):
                for i, (eng, fn, waits, inc, amt) in enumerate(pr.steps):
                    if eng != name:
                        continue
                    for (s, v) in waits:
                        E.wait_ge(sems[s], v)
                    inst = fn(E)
                    if inc is not None:
                        inst.then_inc(sems[inc], amt)
            return go

        blk.sync(emitter("S"))
        blk.vector(emitter("V"))
        blk.scalar(emitter("A"))
        blk.tensor(emitter("T"))
        blk.gpsimd(emitter("G"))

    return nc


def kernel(L0_attn_x, L0_attn_norm_w, L0_attn_W_in, L0_attn_conv_w, L0_attn_conv_b,
           L0_attn_W_x, L0_attn_W_dt, L0_attn_b_dt, L0_attn_A_log, L0_attn_D,
           L0_attn_W_out, L0_ffn_W1, L0_ffn_b1, L0_ffn_W2, L0_ffn_b2):
    f32 = np.float32
    x = np.ascontiguousarray(np.asarray(L0_attn_x, dtype=f32))

    def tile128(a):
        dl, k = a.shape
        return np.ascontiguousarray(a.reshape(KT, 128, k).transpose(1, 0, 2).reshape(128, KT * k))

    if "prog" not in _NC_CACHE:
        _NC_CACHE["prog"] = build_nc()
    nc = _NC_CACHE["prog"]

    in_maps = []
    for c in range(8):
        b, half = c // 2, c % 2
        r0 = half * DL
        rows = slice(r0, r0 + DL)
        m = {
            "x_full": np.ascontiguousarray(x[b]),
            "x_half": np.ascontiguousarray(x[b, half * LH:(half + 1) * LH]),
            "w1T": np.ascontiguousarray(np.asarray(L0_ffn_W1, f32).T),
            "w2T": np.ascontiguousarray(np.asarray(L0_ffn_W2, f32).T),
            "b1c": np.ascontiguousarray(np.asarray(L0_ffn_b1, f32).reshape(32, 128).T),
            "b2c": np.ascontiguousarray(np.asarray(L0_ffn_b2, f32).reshape(8, 128).T),
            "ones1": np.ones((128, 128), f32),
            "onescol": np.ones((128, 1), f32),
            "eye": np.eye(128, dtype=f32),
            "m0c": np.full((128, 1), 1.0 if half == 0 else 0.0, f32),
            "m1c": np.full((128, 1), 1.0 if half == 1 else 0.0, f32),
            "cvals": np.tile(np.array([[EPS, 1.0]], f32), (128, 1)),
        }
        W_in = np.asarray(L0_attn_W_in, f32)
        W_x = np.asarray(L0_attn_W_x, f32)
        W_dt = np.asarray(L0_attn_W_dt, f32)
        W_out = np.asarray(L0_attn_W_out, f32)
        for i in range(NL):
            sel = np.concatenate([W_in[i][r0:r0 + DL], W_in[i][2048 + r0:2048 + r0 + DL]], axis=0)
            m[f"winT{i}"] = np.ascontiguousarray(sel.T)
            m[f"wxT{i}"] = np.ascontiguousarray(W_x[i][:, rows].T)
            m[f"wdtT{i}"] = np.ascontiguousarray(W_dt[i][rows].T)
            m[f"woutT{i}"] = np.ascontiguousarray(W_out[i][:, rows].T)
            m[f"convw{i}"] = tile128(np.asarray(L0_attn_conv_w, f32)[i][rows])
            m[f"convb{i}"] = tile128(np.asarray(L0_attn_conv_b, f32)[i][rows, None])
            m[f"bdt{i}"] = tile128(np.asarray(L0_attn_b_dt, f32)[i][rows, None])
            m[f"alog{i}"] = tile128(np.asarray(L0_attn_A_log, f32)[i][rows])
            m[f"dvec{i}"] = tile128(np.asarray(L0_attn_D, f32)[i][rows, None])
            m[f"normw{i}"] = np.ascontiguousarray(
                np.asarray(L0_attn_norm_w, f32)[i].reshape(8, 128).T)
        in_maps.append(m)

    res = run_bass_kernel_spmd(nc, in_maps, core_ids=list(range(8)))
    out = np.empty((4, L, DM), f32)
    for c in range(8):
        b, half = c // 2, c % 2
        out[b, half * LH:(half + 1) * LH] = res.results[c]["out"]
    return out


# version 13
# speedup vs baseline: 1.0154x; 1.0117x over previous
import numpy as np
import concourse.bass as bass
import concourse.mybir as mybir
from concourse.bass_utils import run_bass_kernel_spmd

F32 = mybir.dt.float32
BF16 = mybir.dt.bfloat16
AOP = mybir.AluOpType
ACT = mybir.ActivationFunctionType
AXX = mybir.AxisListType.X

L = 4096
LH = 2048
DM = 1024
DL = 1024
KT = 8
NST = 16
RK = 64
T = 512
NCH = L // T
SL = T + 1
CW = KT * SL
NL = 4
EPS = 1e-5

SEMS = ["S", "G", "T", "V", "A", "AD"]

_NC_CACHE = {}


class Prog:
    def __init__(self):
        self.steps = []
        self.cnt = dict.fromkeys(SEMS, 0)

    def add(self, eng, fn, waits=None, inc=None, amt=1):
        w = [(s, v) for (s, v) in (waits or []) if v > 0]
        if inc is not None:
            self.cnt[inc] += amt
        self.steps.append((eng, fn, w, inc, amt))

    def c(self, s):
        return self.cnt[s]


def build_nc(dbg=False, nl=NL):
    nc = bass.Bass("TRN2", target_bir_lowering=False, debug=False)
    KIND = "ExternalOutput" if dbg else "Internal"

    x_full = nc.dram_tensor("x_full", [L, DM], F32, kind="ExternalInput").ap()
    x_half = nc.dram_tensor("x_half", [LH, DM], F32, kind="ExternalInput").ap()
    winT, wxT, wdtT, woutT = [], [], [], []
    convw, convb, bdt, alog, dvec, normw = [], [], [], [], [], []
    for i in range(NL):
        winT.append(nc.dram_tensor(f"winT{i}", [DM, 2 * DL], F32, kind="ExternalInput").ap())
        wxT.append(nc.dram_tensor(f"wxT{i}", [DL, 96], F32, kind="ExternalInput").ap())
        wdtT.append(nc.dram_tensor(f"wdtT{i}", [RK, DL], F32, kind="ExternalInput").ap())
        woutT.append(nc.dram_tensor(f"woutT{i}", [DL, DM], F32, kind="ExternalInput").ap())
        convw.append(nc.dram_tensor(f"convw{i}", [128, 32], F32, kind="ExternalInput").ap())
        convb.append(nc.dram_tensor(f"convb{i}", [128, 8], F32, kind="ExternalInput").ap())
        bdt.append(nc.dram_tensor(f"bdt{i}", [128, 8], F32, kind="ExternalInput").ap())
        alog.append(nc.dram_tensor(f"alog{i}", [128, 128], F32, kind="ExternalInput").ap())
        dvec.append(nc.dram_tensor(f"dvec{i}", [128, 8], F32, kind="ExternalInput").ap())
        normw.append(nc.dram_tensor(f"normw{i}", [128, 8], F32, kind="ExternalInput").ap())
    w1T = nc.dram_tensor("w1T", [DM, 4096], F32, kind="ExternalInput").ap()
    w2T = nc.dram_tensor("w2T", [4096, DM], F32, kind="ExternalInput").ap()
    b1c = nc.dram_tensor("b1c", [128, 32], F32, kind="ExternalInput").ap()
    b2c = nc.dram_tensor("b2c", [128, 8], F32, kind="ExternalInput").ap()
    ones1 = nc.dram_tensor("ones1", [128, 128], F32, kind="ExternalInput").ap()
    onescol = nc.dram_tensor("onescol", [128, 1], F32, kind="ExternalInput").ap()
    eye_in = nc.dram_tensor("eye", [128, 128], F32, kind="ExternalInput").ap()
    m0_in = nc.dram_tensor("m0c", [128, 1], F32, kind="ExternalInput").ap()
    m1_in = nc.dram_tensor("m1c", [128, 1], F32, kind="ExternalInput").ap()
    cv_in = nc.dram_tensor("cvals", [128, 2], F32, kind="ExternalInput").ap()

    out_t = nc.dram_tensor("out", [LH, DM], F32, kind="ExternalOutput").ap()

    xm_h = nc.dram_tensor("xm_h", [DM, L], F32, kind=KIND).ap()
    xc_h = nc.dram_tensor("xc_h", [DL, L], F32, kind=KIND).ap()
    zs_h = nc.dram_tensor("zs_h", [DL, L], F32, kind=KIND).ap()
    h1_h = nc.dram_tensor("h1_h", [4096, 2 * LH], BF16, kind="Internal").ap()
    rs_h = nc.dram_tensor("rs_h", [1, L], F32, kind="Internal").ap()
    ccd_i = nc.dram_tensor("ccd_i", [96, L], F32, kind="Internal").ap()
    ccd_o = nc.dram_tensor("ccd_o", [96, L], F32, kind="Internal").ap()
    ccw_i = nc.dram_tensor("ccw_i", [DM, L], F32, kind="Internal").ap()
    ccw_o = nc.dram_tensor("ccw_o", [DM, L], F32, kind="Internal").ap()
    if dbg:
        dbg_d = nc.dram_tensor("dbg_d", [96, L], F32, kind="ExternalOutput").ap()
        dbg_w = nc.dram_tensor("dbg_w", [DM, L], F32, kind="ExternalOutput").ap()
        dbg_s = nc.dram_tensor("dbg_s", [128, 512], F32, kind="ExternalOutput").ap()
    RG = [[0, 1], [2, 3], [4, 5], [6, 7]]

    import contextlib
    ctx = contextlib.ExitStack()
    with ctx:
        AR1 = ctx.enter_context(nc.sbuf_tensor("AR1", [128, CW], F32))
        AR2 = ctx.enter_context(nc.sbuf_tensor("AR2", [128, 4096], F32))
        AR3 = ctx.enter_context(nc.sbuf_tensor("AR3", [128, KT * 516], F32))
        AR4 = ctx.enter_context(nc.sbuf_tensor("AR4", [128, 4 * CW], F32))
        AR5 = ctx.enter_context(nc.sbuf_tensor("AR5", [128, 8192], F32))
        AR6 = ctx.enter_context(nc.sbuf_tensor("AR6", [128, 7680], F32))
        AR7 = ctx.enter_context(nc.sbuf_tensor("AR7", [128, CW], F32))
        state = ctx.enter_context(nc.sbuf_tensor("state", [128, NST * KT], F32))
        vecs = ctx.enter_context(nc.sbuf_tensor("vecs", [128, 64], F32))
        acol = ctx.enter_context(nc.sbuf_tensor("acol", [128, 128], F32))
        atmp = ctx.enter_context(nc.sbuf_tensor("atmp", [128, 128], F32))
        wdt_sb = ctx.enter_context(nc.sbuf_tensor("wdt_sb", [64, DL], F32))
        wx_sb = ctx.enter_context(nc.sbuf_tensor("wx_sb", [128, KT * 96], F32))
        b1_sb = ctx.enter_context(nc.sbuf_tensor("b1_sb", [128, 32], F32))
        b2_sb = ctx.enter_context(nc.sbuf_tensor("b2_sb", [128, 8], F32))
        ones1_sb = ctx.enter_context(nc.sbuf_tensor("ones1_sb", [128, 128], F32))
        onesc_sb = ctx.enter_context(nc.sbuf_tensor("onesc_sb", [128, 1], F32))
        eye_sb = ctx.enter_context(nc.sbuf_tensor("eye_sb", [128, 128], F32))
        m0_sb = ctx.enter_context(nc.sbuf_tensor("m0_sb", [128, 1], F32))
        m1_sb = ctx.enter_context(nc.sbuf_tensor("m1_sb", [128, 1], F32))
        cvals = ctx.enter_context(nc.sbuf_tensor("cvals_sb", [128, 2], F32))
        nc.const_aps.register(F32, EPS, cvals[:, 0:1])
        nc.const_aps.register(F32, 1.0, cvals[:, 1:2])
        rs1 = ctx.enter_context(nc.sbuf_tensor("rs1", [1, T], F32))
        rs2 = rs1
        sc1 = ctx.enter_context(nc.sbuf_tensor("sc1", [128, 8], F32))
        gq = ctx.enter_context(nc.sbuf_tensor("gq", [128, 4], F32))

        wq = AR1

        pA0 = ctx.enter_context(nc.psum_tensor("pA0", [128, T], F32))
        pA1 = ctx.enter_context(nc.psum_tensor("pA1", [128, T], F32))
        pdbc = ctx.enter_context(nc.psum_tensor("pdbc", [128, T], F32))
        pbc = ctx.enter_context(nc.psum_tensor("pbc", [128, T], F32))
        pdel = ctx.enter_context(nc.psum_tensor("pdel", [128, T], F32))
        pwo0 = ctx.enter_context(nc.psum_tensor("pwo0", [128, T], F32))
        pwo1 = ctx.enter_context(nc.psum_tensor("pwo1", [128, T], F32))

        sems = {s: ctx.enter_context(nc.semaphore(s)) for s in SEMS}
        blk = ctx.enter_context(nc.Block())

        scarr = AR6[:, 0:768]
        bcst = AR7[:, 0:2048]

        pr = Prog()
        W = pr.c

        def ln_pass_s(xsl, tt, w0):
            pr.add("V", lambda e, xsl=xsl, tt=tt: e.tensor_reduce(
                scarr[:, tt:tt + 1], xsl, AXX, AOP.add), waits=w0, inc="V")
            pr.add("V", lambda e, xsl=xsl: e.tensor_tensor(AR2[:, 0:1024], xsl, xsl, AOP.mult),
                   waits=[("T", W("T")), ("A", W("A"))], inc="V")
            pr.add("V", lambda e, tt=tt: e.tensor_reduce(
                scarr[:, 128 + tt:129 + tt], AR2[:, 0:1024], AXX, AOP.add), inc="V")

        def ln_pass_m():
            pr.add("V", lambda e: e.tensor_scalar_mul(scarr[:, 256:384], scarr[:, 0:128], 1.0 / DM), inc="V")
            pr.add("V", lambda e: e.tensor_tensor(scarr[:, 384:512], scarr[:, 256:384],
                                                  scarr[:, 256:384], AOP.mult), inc="V")
            pr.add("V", lambda e: e.tensor_scalar_mul(scarr[:, 512:640], scarr[:, 128:256], 1.0 / DM), inc="V")
            pr.add("V", lambda e: e.tensor_sub(scarr[:, 512:640], scarr[:, 512:640],
                                               scarr[:, 384:512]), inc="V")
            vW = W("V")
            pr.add("A", lambda e: e.activation(scarr[:, 640:768], scarr[:, 512:640], ACT.Ln, bias=EPS),
                   waits=[("V", vW)], inc="A")
            pr.add("A", lambda e: e.activation(scarr[:, 640:768], scarr[:, 640:768], ACT.Exp, scale=-0.5),
                   inc="A")
            return W("A")

        def ln_pass_n(xsl, xnsl, tt, w0):
            pr.add("V", lambda e, xsl=xsl, xnsl=xnsl, tt=tt: e.tensor_scalar(
                xnsl, xsl, scarr[:, 256 + tt:257 + tt], scarr[:, 640 + tt:641 + tt],
                AOP.subtract, AOP.mult), waits=w0, inc="V")
            return W("V")

        for dst, src in [(cvals[:], cv_in), (ones1_sb[:], ones1), (onesc_sb[:], onescol),
                         (eye_sb[:], eye_in), (b1_sb[:], b1c), (b2_sb[:], b2c),
                         (m0_sb[:], m0_in), (m1_sb[:], m1_in)]:
            pr.add("S", lambda e, dst=dst, src=src: e.dma_start(out=dst, in_=src), inc="S", amt=16)

        for tt in range(L // 128):
            xsl = AR1[:, (tt % 4) * 1024:(tt % 4) * 1024 + 1024]
            pr.add("S", lambda e, xsl=xsl, tt=tt: e.dma_start(out=xsl, in_=x_full[tt * 128:(tt + 1) * 128, :]),
                   waits=[("V", W("V"))], inc="S", amt=16)
            sW = W("S")
            ln_pass_s(xsl, tt, [("S", sW)])
        aW_ln = ln_pass_m()
        for tt in range(L // 128):
            xsl = AR1[:, (tt % 4) * 1024:(tt % 4) * 1024 + 1024]
            pr.add("S", lambda e, xsl=xsl, tt=tt: e.dma_start(out=xsl, in_=x_full[tt * 128:(tt + 1) * 128, :]),
                   waits=[("V", W("V"))], inc="S", amt=16)
            sW = W("S")
            xnsl = AR2[:, 0:1024]
            ln_pass_n(xsl, xnsl, tt, [("S", sW), ("A", aW_ln), ("T", W("T"))])
            vW = W("V")
            pr.add("A", lambda e, xnsl=xnsl: e.activation(AR2[:, 1024:2048], xnsl, ACT.Square),
                   waits=[("V", vW)], inc="A")
            aWq = W("A")
            pr.add("V", lambda e: e.tensor_reduce(sc1[:, 0:1], AR2[:, 1024:2048], AXX, AOP.add),
                   waits=[("A", aWq)], inc="V")
            vWq = W("V")
            pr.add("G", lambda e, tt=tt: e.dma_start(out=rs_h[0:1, tt * 128:(tt + 1) * 128],
                                                     in_=sc1[:, 0:1]),
                   waits=[("V", vWq)], inc="G", amt=16)
            prevG = W("G")
            for a in range(8):
                bank = pdel if a % 2 == 0 else pbc
                pr.add("T", lambda e, bank=bank, xnsl=xnsl, a=a: e.transpose(
                    bank[:, 0:128], xnsl[:, a * 128:(a + 1) * 128], eye_sb[:]),
                    waits=[("V", vW), ("A", W("A")), ("G", prevG)], inc="T")
                tW = W("T")
                pr.add("A", lambda e, a=a, bank=bank: e.copy(AR3[:, a * 128:(a + 1) * 128], bank[:, 0:128]),
                       waits=[("T", tW)], inc="A")
            aW = W("A")
            pr.add("G", lambda e, tt=tt: e.dma_start(
                out=xm_h[:, tt * 128:(tt + 1) * 128].rearrange("(a p) t -> p a t", p=128),
                in_=AR3[:, 0:1024].rearrange("p (a t) -> p a t", a=8)),
                waits=[("A", aW)], inc="G", amt=16)

        def spacer():
            pr.add("V", lambda e: e.tensor_copy(AR2[:, 1024:1536], AR2[:, 1536:2048]), inc="V")

        gam_cons = {0: W("A"), 1: W("A")}
        for gi, (wt, n_t, ldfn) in enumerate([
            (w1T, 8, lambda e, t, wsl: e.dma_start(out=wsl, in_=w1T[t * 128:(t + 1) * 128, :])),
            (w2T, 8, lambda e, t, wsl: e.dma_start(
                out=wsl.rearrange("p (a m) -> p a m", a=4),
                in_=w2T[t * 512:(t + 1) * 512, :].rearrange("(a p) m -> p a m", p=128))),
        ]):
            for t in range(n_t):
                wsl = AR5[:, (t % 2) * 4096:(t % 2) * 4096 + 4096]
                pr.add("S", lambda e, t=t, wsl=wsl, ldfn=ldfn: ldfn(e, t, wsl),
                       waits=[("A", gam_cons[t % 2])], inc="S", amt=16)
                sW = W("S")
                pr.add("A", lambda e, wsl=wsl: e.activation(AR2[:, 0:4096], wsl, ACT.Abs),
                       waits=[("S", sW), ("V", W("V"))], inc="A")
                gam_cons[t % 2] = W("A")
                aW = W("A")
                pr.add("V", lambda e, t=t, gi=gi: e.tensor_reduce(
                    atmp[:, 64 + gi * 8 + t:64 + gi * 8 + t + 1], AR2[:, 0:4096], AXX, AOP.add),
                    waits=[("A", aW)], inc="V")
            spacer()
            pr.add("V", lambda e, gi=gi: e.tensor_reduce(
                gq[:, gi:gi + 1], atmp[:, 64 + gi * 8:64 + gi * 8 + 8], AXX, AOP.add), inc="V")
            spacer()
            vW = W("V")
            pr.add("T", lambda e, gi=gi: e.matmul(pbc[0:1, 0:1], onesc_sb[:], gq[:, gi:gi + 1],
                                                  start=True, stop=True), waits=[("V", vW)], inc="T")
            tW = W("T")
            pr.add("V", lambda e: e.tensor_scalar(gq[0:1, 3:4], pbc[0:1, 0:1], 1.0 / (4096.0 * DM), EPS,
                                                  AOP.mult, AOP.add), waits=[("T", tW)], inc="V")
            spacer()
            pr.add("V", lambda e: e.reciprocal(gq[0:1, 2:3], gq[0:1, 3:4]), inc="V")
            spacer()
            vW = W("V")
            pr.add("T", lambda e: e.matmul(pbc[:, 0:1], ones1_sb[0:1, :], gq[0:1, 2:3],
                                           start=True, stop=True), waits=[("V", vW)], inc="T")
            tW = W("T")
            pr.add("A", lambda e, gi=gi: e.copy(gq[:, gi:gi + 1], pbc[:, 0:1]), waits=[("T", tW)], inc="A")
        g1A = g2A = W("A")


        rs_g = W("G")
        for li in range(nl):
            pr.add("S", lambda e, li=li: e.dma_start(out=vecs[:, 0:32], in_=convw[li]),
                   waits=[("V", W("V")), ("A", W("A"))], inc="S", amt=16)
            pr.add("S", lambda e, li=li: e.dma_start(out=vecs[:, 32:40], in_=convb[li]), inc="S", amt=16)
            pr.add("S", lambda e, li=li: e.dma_start(out=vecs[:, 40:48], in_=bdt[li]), inc="S", amt=16)
            pr.add("S", lambda e, li=li: e.dma_start(out=vecs[:, 48:56], in_=dvec[li]), inc="S", amt=16)
            pr.add("S", lambda e, li=li: e.dma_start(out=vecs[:, 56:64], in_=normw[li]), inc="S", amt=16)
            pr.add("S", lambda e, li=li: e.dma_start(out=atmp[:], in_=alog[li]), inc="S", amt=16)
            pr.add("S", lambda e, li=li: e.dma_start(out=wdt_sb[:], in_=wdtT[li]),
                   waits=[("T", W("T"))], inc="S", amt=16)
            pr.add("S", lambda e, li=li: e.dma_start(
                out=wx_sb[:].rearrange("p (a m) -> p a m", a=KT),
                in_=wxT[li].rearrange("(a p) m -> p a m", p=128)), inc="S", amt=16)
            pr.add("S", lambda e, li=li: e.dma_start(
                out=AR4[:, 0:16384].rearrange("p (a m) -> p a m", a=8),
                in_=winT[li].rearrange("(a p) m -> p a m", p=128)),
                waits=[("V", W("V"))], inc="S", amt=16)
            sWL = W("S")
            pr.add("A", lambda e: e.activation(acol[:], atmp[:], ACT.Exp), waits=[("S", sWL)], inc="A")
            pr.add("V", lambda e: e.tensor_scalar_mul(acol[:], acol[:], -1.0),
                   waits=[("A", W("A"))], inc="V")
            pr.add("V", lambda e: e.memset(state[:], 0.0), inc="V")
            pr.add("V", lambda e: e.memset(
                AR3[:].rearrange("p (a q) -> p a q", a=KT)[:, :, 0:3], 0.0),
                waits=[("A", W("A")), ("G", W("G"))], inc="V")

            evac_cnt = {0: 0, 1: 0}
            for c in range(NCH):
                cs = c * T
                pr.add("S", lambda e, cs=cs: e.dma_start(
                    out=AR1[:, 0:4096].rearrange("p (a t) -> p a t", a=8),
                    in_=xm_h[:, cs:cs + T].rearrange("(a p) t -> p a t", p=128)),
                    waits=[("V", W("V")), ("T", W("T")), ("G", W("G"))], inc="S", amt=16)
                sW = W("S")
                pr.add("S", lambda e, cs=cs: e.dma_start(out=rs1[:], in_=rs_h[0:1, cs:cs + T]),
                       waits=[("A", W("A")), ("G", rs_g), ("T", W("T"))], inc="S", amt=16)
                sWr = W("S")
                pr.add("A", lambda e: e.activation(rs1[:], rs1[:], ACT.Ln, scale=1.0 / DM, bias=EPS),
                       waits=[("S", sWr)], inc="A")
                pr.add("A", lambda e: e.activation(rs2[:], rs1[:], ACT.Exp, scale=-0.5), inc="A")
                aW = W("A")
                pr.add("T", lambda e: e.matmul(pdel[:], ones1_sb[0:1, :], rs2[:], start=True, stop=True),
                       waits=[("A", aW)], inc="T")
                tW = W("T")
                for j in range(8):
                    pr.add("V", lambda e, j=j: e.scalar_tensor_tensor(
                        AR2[:, j * T:(j + 1) * T], AR1[:, j * T:(j + 1) * T], vecs[:, 56 + j:57 + j],
                        pdel[:], AOP.mult, AOP.mult), waits=([("T", tW)] if j == 0 else []), inc="V")
                hW = W("V")
                silu_done = {}
                for m in range(16):
                    bank = pA0 if m % 2 == 0 else pA1
                    for k in range(8):
                        pr.add("T", lambda e, m=m, k=k, bank=bank: e.matmul(
                            bank[:], AR4[:, k * 2048 + m * 128:k * 2048 + (m + 1) * 128],
                            AR2[:, k * T:(k + 1) * T], start=(k == 0), stop=(k == 7)),
                            waits=([("V", hW), ("A", evac_cnt[m % 2])] if k == 0 else []), inc="T")
                    tW = W("T")
                    if m < 8:
                        base = m * 516
                        xsl = AR3[:, base + 3:base + 515]
                        w_extra = [("G", W("G"))] if (li == 0 and c == 0 and m == 0) else []
                        pr.add("A", lambda e, xsl=xsl, bank=bank: e.copy(xsl, bank[:]),
                               waits=[("T", tW), ("V", W("V"))] + w_extra, inc="A")
                        evac_cnt[m % 2] = W("A")
                        aW = W("A")
                        xcm = AR6[:, 4096 + (m % 2) * T:4096 + (m % 2) * T + T]
                        pr.add("V", lambda e, xcm=xcm, base=base, m=m: e.tensor_scalar_mul(
                            xcm, AR3[:, base:base + T], vecs[:, m * 4:m * 4 + 1]),
                            waits=[("A", aW)], inc="V")
                        for s in range(1, 4):
                            pr.add("V", lambda e, xcm=xcm, base=base, m=m, s=s: e.scalar_tensor_tensor(
                                xcm, AR3[:, base + s:base + s + T], vecs[:, m * 4 + s:m * 4 + s + 1],
                                xcm, AOP.mult, AOP.add), inc="V")
                        vW = W("V")
                        xcs = AR6[:, m * T:(m + 1) * T]
                        pr.add("A", lambda e, xcs=xcs, xcm=xcm, m=m: e.activation(
                            xcs, xcm, ACT.Silu, bias=vecs[:, 32 + m:33 + m]),
                            waits=[("V", vW), ("G", W("G")), ("T", W("T"))], inc="A")
                        silu_done[m] = W("A")
                        aW = W("A")
                        pr.add("G", lambda e, xcs=xcs, m=m, cs=cs: e.dma_start(
                            out=xc_h[m * 128:(m + 1) * 128, cs:cs + T], in_=xcs),
                            waits=[("A", aW)], inc="G", amt=16)
                        pr.add("V", lambda e, base=base: e.tensor_copy(
                            AR3[:, base:base + 3], AR3[:, base + 512:base + 515]), inc="V")
                    else:
                        mm = m - 8
                        szs = AR6[:, 4096 + (m % 2) * T:4096 + (m % 2) * T + T]
                        pr.add("A", lambda e, szs=szs, bank=bank: e.activation(szs, bank[:], ACT.Silu),
                               waits=[("T", tW), ("G", W("G"))], inc="A")
                        evac_cnt[m % 2] = W("A")
                        aW = W("A")
                        pr.add("G", lambda e, szs=szs, mm=mm, cs=cs: e.dma_start(
                            out=zs_h[mm * 128:(mm + 1) * 128, cs:cs + T], in_=szs),
                            waits=[("A", aW)], inc="G", amt=16)
                for m in range(8):
                    pr.add("T", lambda e, m=m: e.matmul(
                        pdbc[0:96, :], wx_sb[:, m * 96:(m + 1) * 96], AR6[:, m * T:(m + 1) * T],
                        start=(m == 0), stop=(m == 7)),
                        waits=([("A", silu_done[7])] if m == 0 else []), inc="T")
                tW = W("T")
                pr.add("A", lambda e, cs=cs: e.copy(AR7[0:96, cs:cs + T], pdbc[0:96, :]),
                       waits=[("T", tW)], inc="A")
            aW = W("A")
            pr.add("G", lambda e: e.dma_start(out=ccd_i, in_=AR7[0:96, 0:4096]),
                   waits=[("A", aW)], inc="G", amt=16)
            gW = W("G")
            pr.add("G", lambda e: e.collective_compute("AllReduce", AOP.add, replica_groups=RG,
                                                       ins=[ccd_i], outs=[ccd_o]),
                   waits=[("G", gW)], inc="G", amt=1)
            gW = W("G")
            pr.add("G", lambda e: e.dma_start(out=AR2[0:96, 0:4096], in_=ccd_o),
                   waits=[("G", gW), ("V", W("V")), ("T", W("T")), ("A", W("A"))], inc="G", amt=16)
            dbcG = W("G")
            pr.add("S", lambda e, li=li: e.dma_start(
                out=AR5[:].rearrange("p (a m) -> p a m", a=KT),
                in_=woutT[li].rearrange("(a p) m -> p a m", p=128)),
                waits=[("A", W("A")), ("T", W("T"))], inc="S", amt=16)
            woutS = W("S")

            U8 = AR4[:, 0 * CW:1 * CW]
            DA8 = AR4[:, 1 * CW:2 * CW]
            BX8 = AR4[:, 2 * CW:3 * CW]
            H8 = AR4[:, 3 * CW:4 * CW]
            pr.add("V", lambda e: e.memset(DA8.rearrange("p (a q) -> p a q", a=KT)[:, :, 0:1], 0.0),
                   waits=[("T", W("T"))], inc="V")
            bx_done = W("V")
            sc_done = W("V")
            ch_done = W("V")
            bc_consumed = {0: 0, 1: 0}
            cc_consumed = {0: 0, 1: 0}
            for c in range(NCH):
                cs = c * T
                for k in range(KT):
                    pr.add("T", lambda e, k=k, cs=cs: e.matmul(
                        pdel[:], wdt_sb[:, k * 128:(k + 1) * 128], AR2[0:64, cs:cs + T],
                        start=True, stop=True),
                        waits=[("G", dbcG), ("A", W("A")), ("V", W("V"))], inc="T")
                    tW = W("T")
                    tmp = AR6[:, 7168:7680]
                    pr.add("A", lambda e, tmp=tmp, k=k: e.activation(
                        tmp, pdel[:], ACT.Exp, bias=vecs[:, 40 + k:41 + k]),
                        waits=[("T", tW)], inc="A")
                    dsl = AR6[:, k * T:(k + 1) * T]
                    pr.add("A", lambda e, dsl=dsl, tmp=tmp: e.activation(dsl, tmp, ACT.Ln, bias=1.0), inc="A")
                    aW = W("A")
                    xck = AR6[:, 4096 + (k % 2) * T:4096 + (k % 2) * T + T]
                    pr.add("A", lambda e, xck=xck, k=k, cs=cs: e.dma_start(
                        out=xck, in_=xc_h[k * 128:(k + 1) * 128, cs:cs + T]),
                        waits=[("V", W("V"))], inc="AD", amt=16)
                    adW = W("AD")
                    pr.add("V", lambda e, k=k, dsl=dsl, xck=xck: e.tensor_tensor(
                        U8[:, k * SL + 1:k * SL + 1 + T], dsl, xck, AOP.mult),
                        waits=[("A", aW), ("AD", adW), ("T", W("T"))], inc="V")
                    pr.add("V", lambda e, k=k, xck=xck: e.tensor_scalar_mul(
                        AR1[:, k * SL + 1:k * SL + 1 + T], xck, vecs[:, 48 + k:49 + k]), inc="V")
                for n in range(NST):
                    bslot = bcst[:, (n % 2) * T:(n % 2) * T + T]
                    cslot = bcst[:, 1024 + (n % 2) * T:1024 + (n % 2) * T + T]
                    pr.add("S", lambda e, n=n, cs=cs, bslot=bslot: e.dma_start(
                        out=bslot, in_=ccd_o[64 + n:65 + n, cs:cs + T].to_broadcast([128, T])),
                        waits=[("V", bc_consumed[n % 2]), ("G", dbcG)], inc="S", amt=16)
                    sWb = W("S")
                    pr.add("G", lambda e, n=n, cs=cs, cslot=cslot: e.dma_start(
                        out=cslot, in_=ccd_o[80 + n:81 + n, cs:cs + T].to_broadcast([128, T])),
                        waits=[("V", cc_consumed[n % 2])], inc="G", amt=16)
                    gWc = W("G")
                    for k in range(KT):
                        pr.add("A", lambda e, k=k, n=n: e.activation(
                            DA8[:, k * SL + 1:k * SL + 1 + T], AR6[:, k * T:(k + 1) * T], ACT.Exp,
                            scale=acol[:, k * NST + n:k * NST + n + 1]),
                            waits=([("V", sc_done)] if k == 0 else []), inc="A")
                    aW = W("A")
                    pr.add("V", lambda e, n=n: e.tensor_copy(
                        BX8.rearrange("p (a q) -> p a q", a=KT)[:, :, 0],
                        state[:, n * KT:(n + 1) * KT]), waits=[("A", aW), ("S", sWb)], inc="V")
                    for k in range(KT):
                        pr.add("V", lambda e, k=k, bslot=bslot: e.tensor_tensor(
                            BX8[:, k * SL + 1:k * SL + 1 + T], U8[:, k * SL + 1:k * SL + 1 + T],
                            bslot, AOP.mult), inc="V")
                    bc_consumed[n % 2] = W("V")
                    bx_done = W("V")
                    pr.add("V", lambda e: e.tensor_tensor_scan(H8[:], DA8[:], BX8[:], 0.0, AOP.mult, AOP.add),
                           inc="V")
                    sc_done = W("V")
                    for k in range(KT):
                        pr.add("V", lambda e, k=k, cslot=cslot: e.tensor_tensor(
                            BX8[:, k * SL + 1:k * SL + 1 + T], H8[:, k * SL + 1:k * SL + 1 + T],
                            cslot, AOP.mult), waits=([("G", gWc)] if k == 0 else []), inc="V")
                    cc_consumed[n % 2] = W("V")
                    ch_done = W("V")
                    pr.add("V", lambda e: e.tensor_tensor(
                        AR1.rearrange("p (a q) -> p a q", a=KT)[:, :, 1:SL],
                        AR1.rearrange("p (a q) -> p a q", a=KT)[:, :, 1:SL],
                        BX8.rearrange("p (a q) -> p a q", a=KT)[:, :, 1:SL], AOP.add), inc="V")
                    pr.add("V", lambda e, n=n: e.tensor_copy(
                        state[:, n * KT:(n + 1) * KT],
                        H8.rearrange("p (a q) -> p a q", a=KT)[:, :, SL - 1]), inc="V")
                for k in range(KT):
                    szk = AR6[:, 5120 + (k % 2) * T:5120 + (k % 2) * T + T]
                    pr.add("S", lambda e, szk=szk, k=k, cs=cs: e.dma_start(
                        out=szk, in_=zs_h[k * 128:(k + 1) * 128, cs:cs + T]),
                        waits=[("V", W("V"))], inc="S", amt=16)
                    sW = W("S")
                    pr.add("V", lambda e, k=k, szk=szk: e.tensor_tensor(
                        AR1[:, k * SL + 1:k * SL + 1 + T], AR1[:, k * SL + 1:k * SL + 1 + T],
                        szk, AOP.mult), waits=[("S", sW)], inc="V")
                gV = W("V")
                for m2 in range(8):
                    bank = pwo0 if m2 % 2 == 0 else pwo1
                    for k in range(KT):
                        pr.add("T", lambda e, m2=m2, k=k, bank=bank: e.matmul(
                            bank[:], AR5[:, k * 1024 + m2 * 128:k * 1024 + (m2 + 1) * 128],
                            AR1[:, k * SL + 1:k * SL + 1 + T], start=(k == 0), stop=(k == KT - 1)),
                            waits=([("V", gV), ("S", woutS), ("A", W("A"))] if k == 0 else []), inc="T")
                    tW = W("T")
                    osl = AR6[:, 6144 + (m2 % 2) * T:6144 + (m2 % 2) * T + T]
                    pr.add("A", lambda e, osl=osl, bank=bank: e.copy(osl, bank[:]),
                           waits=[("T", tW), ("G", W("G"))], inc="A")
                    aW = W("A")
                    pr.add("G", lambda e, osl=osl, m2=m2, cs=cs: e.dma_start(
                        out=ccw_i[m2 * 128:(m2 + 1) * 128, cs:cs + T], in_=osl),
                        waits=[("A", aW)], inc="G", amt=16)
            gW = W("G")
            pr.add("G", lambda e: e.collective_compute("AllReduce", AOP.add, replica_groups=RG,
                                                       ins=[ccw_i], outs=[ccw_o]),
                   waits=[("G", gW)], inc="G", amt=1)
            woG = W("G")
            if dbg:
                pr.add("G", lambda e: e.dma_start(out=dbg_d, in_=ccd_o), inc="G", amt=16)
                pr.add("G", lambda e: e.dma_start(out=dbg_w, in_=ccw_o), inc="G", amt=16)
            upd_v = {0: None, 1: None}
            upd_g = {0: 0, 1: 0}
            for c in range(NCH):
                cs = c * T
                for h2 in range(2):
                    hs = cs + h2 * 256
                    slot = (c * 2 + h2) % 2
                    off = slot * 2048
                    w0 = ([("V", upd_v[slot]), ("G", upd_g[slot])] if upd_v[slot] else [
                        ("V", W("V")), ("T", W("T")), ("A", W("A")), ("G", W("G"))])
                    pr.add("S", lambda e, hs=hs, off=off: e.dma_start(
                        out=AR1[:, off:off + 2048].rearrange("p (a t) -> p a t", a=8),
                        in_=xm_h[:, hs:hs + 256].rearrange("(a p) t -> p a t", p=128)),
                        waits=w0, inc="S", amt=16)
                    pr.add("S", lambda e, hs=hs, off=off: e.dma_start(
                        out=AR2[:, off:off + 2048].rearrange("p (a t) -> p a t", a=8),
                        in_=ccw_o[:, hs:hs + 256].rearrange("(a p) t -> p a t", p=128)),
                        waits=[("G", woG)] if (c == 0 and h2 == 0) else [], inc="S", amt=16)
                    sW = W("S")
                    pr.add("V", lambda e, off=off: e.tensor_tensor(
                        AR1[:, off:off + 2048], AR1[:, off:off + 2048], AR2[:, off:off + 2048], AOP.add),
                        waits=[("S", sW)], inc="V")
                    vW = W("V")
                    upd_v[slot] = vW
                    pr.add("G", lambda e, hs=hs, off=off: e.dma_start(
                        out=xm_h[:, hs:hs + 256].rearrange("(a p) t -> p a t", p=128),
                        in_=AR1[:, off:off + 2048].rearrange("p (a t) -> p a t", a=8)),
                        waits=[("V", vW)], inc="G", amt=16)
                    upd_g[slot] = W("G")
                    if li < nl - 1:
                        pr.add("A", lambda e, off=off: e.activation(
                            AR3[:, 0:2048], AR1[:, off:off + 2048], ACT.Square),
                            waits=[("V", vW), ("T", W("T"))], inc="A")
                        aWu = W("A")
                        for j in range(8):
                            pr.add("T", lambda e, j=j, h2=h2: e.matmul(
                                pbc[0:1, h2 * 256:h2 * 256 + 256], onesc_sb[:],
                                AR3[:, j * 256:(j + 1) * 256], start=(j == 0), stop=(j == 7)),
                                waits=([("A", aWu)] if j == 0 else []), inc="T")
                        tWu = W("T")
                        if h2 == 1:
                            pr.add("A", lambda e: e.copy(rs1[:], pbc[0:1, 0:T]),
                                   waits=[("T", tWu)], inc="A")
                            aWu2 = W("A")
                            pr.add("G", lambda e, cs=cs: e.dma_start(
                                out=rs_h[0:1, cs:cs + T], in_=rs1[:]),
                                waits=[("A", aWu2)], inc="G", amt=16)
            rs_g = W("G")

        xnh = AR4[:, 0:8192].bitcast(BF16)
        xnl = AR4[:, 8208:16400].bitcast(BF16)
        xmG = W("G")
        for tt in range(LH // 128):
            xsl = AR1[:, (tt % 2) * 1024:(tt % 2) * 1024 + 1024]
            pr.add("S", lambda e, xsl=xsl, tt=tt: e.dma_start(out=xsl, in_=x_half[tt * 128:(tt + 1) * 128, :]),
                   waits=[("V", W("V")), ("G", xmG), ("T", W("T")), ("A", W("A"))], inc="S", amt=16)
            sW = W("S")
            ln_pass_s(xsl, tt, [("S", sW)])
        aW_ln = ln_pass_m()
        for tt in range(LH // 128):
            xsl = AR1[:, (tt % 2) * 1024:(tt % 2) * 1024 + 1024]
            pr.add("S", lambda e, xsl=xsl, tt=tt: e.dma_start(out=xsl, in_=x_half[tt * 128:(tt + 1) * 128, :]),
                   waits=[("V", W("V"))], inc="S", amt=16)
            sW = W("S")
            xnsl = AR2[:, 0:1024]
            ln_pass_n(xsl, xnsl, tt, [("S", sW), ("A", aW_ln), ("T", W("T"))])
            vW = W("V")
            for a in range(8):
                bank = pdel if a % 2 == 0 else pbc
                pr.add("T", lambda e, a=a, bank=bank, xnsl=xnsl: e.transpose(
                    bank[:, 0:128], xnsl[:, a * 128:(a + 1) * 128], eye_sb[:]),
                    waits=[("V", vW), ("A", W("A"))], inc="T")
                tW = W("T")
                pr.add("A", lambda e, a=a, tt=tt, bank=bank: e.copy(
                    xnh[:, a * LH + tt * 128:a * LH + (tt + 1) * 128], bank[:, 0:128]),
                    waits=[("T", tW)], inc="A")
                aW2 = W("A")
                pr.add("V", lambda e, a=a, tt=tt, bank=bank: e.tensor_tensor(
                    xnl[:, a * LH + tt * 128:a * LH + (tt + 1) * 128], bank[:, 0:128],
                    xnh[:, a * LH + tt * 128:a * LH + (tt + 1) * 128], AOP.subtract),
                    waits=[("A", aW2)], inc="V")

        h1hi = AR2[:, 2048:3072].bitcast(BF16)
        h1lo = AR2[:, 3072:4096].bitcast(BF16)
        for m in range(32):
            slot = m % 2
            wsl = AR5[:, slot * 4096:slot * 4096 + 1024]
            pr.add("S", lambda e, wsl=wsl, m=m: e.dma_start(
                out=wsl.rearrange("p (a q) -> p a q", a=8),
                in_=w1T[:, m * 128:(m + 1) * 128].rearrange("(a p) q -> p a q", p=128)),
                waits=[("V", W("V"))], inc="S", amt=16)
            sW = W("S")
            wqs = wq[:, 2048 + slot * 1024:2048 + slot * 1024 + 1024]
            wqb = wq[:, slot * 512:slot * 512 + 512].bitcast(BF16)
            pr.add("V", lambda e, wsl=wsl, wqs=wqs: e.tensor_scalar_mul(wqs, wsl, gq[:, 0:1]),
                   waits=[("S", sW), ("A", g1A), ("T", W("T"))], inc="V")
            pr.add("V", lambda e, wqs=wqs, wsl=wsl: e.tensor_single_scalar(wsl, wqs, 0.5, AOP.is_gt), inc="V")
            pr.add("V", lambda e, wqs=wqs: e.tensor_single_scalar(wqs, wqs, -0.5, AOP.is_lt), inc="V")
            pr.add("V", lambda e, wqs=wqs, wsl=wsl, wqb=wqb: e.tensor_sub(wqb, wsl, wqs), inc="V")
            vW = W("V")
            for cg in range(4):
                bank = pA0 if cg % 2 == 0 else pA1
                for k in range(8):
                    for hl, xsrc in ((0, xnh), (1, xnl)):
                        pr.add("T", lambda e, k=k, cg=cg, bank=bank, wqb=wqb, xsrc=xsrc: e.matmul(
                            bank[:], wqb[:, k * 128:(k + 1) * 128],
                            xsrc[:, k * LH + cg * T:k * LH + (cg + 1) * T],
                            start=(k == 0 and xsrc is xnh), stop=(k == 7 and xsrc is xnl)),
                            waits=([("V", vW), ("A", W("A"))] if (k == 0 and hl == 0) else []), inc="T")
                tW = W("T")
                pr.add("A", lambda e, cg=cg, bank=bank, m=m: e.activation(
                    AR2[:, cg * T:(cg + 1) * T], bank[:], ACT.Relu, bias=b1_sb[:, m:m + 1]),
                    waits=[("T", tW), ("G", W("G"))], inc="A")
            aW = W("A")
            for cg in range(4):
                pr.add("A", lambda e, cg=cg: e.copy(h1hi[:, cg * T:(cg + 1) * T], AR2[:, cg * T:(cg + 1) * T]),
                       inc="A")
                aW2 = W("A")
                pr.add("V", lambda e, cg=cg: e.tensor_tensor(
                    h1lo[:, cg * T:(cg + 1) * T], AR2[:, cg * T:(cg + 1) * T],
                    h1hi[:, cg * T:(cg + 1) * T], AOP.subtract), waits=[("A", aW2)], inc="V")
            vW2 = W("V")
            pr.add("G", lambda e, m=m: e.dma_start(out=h1_h[m * 128:(m + 1) * 128, 0:LH], in_=h1hi[:]),
                   waits=[("V", vW2)], inc="G", amt=16)
            pr.add("G", lambda e, m=m: e.dma_start(out=h1_h[m * 128:(m + 1) * 128, LH:2 * LH], in_=h1lo[:]),
                   inc="G", amt=16)

        for m2 in range(8):
            w2sl = AR5[:, 4096:8192]
            pr.add("S", lambda e, w2sl=w2sl, m2=m2: e.dma_start(
                out=w2sl.rearrange("p (a q) -> p a q", a=32),
                in_=w2T[:, m2 * 128:(m2 + 1) * 128].rearrange("(a p) q -> p a q", p=128)),
                waits=[("V", W("V")), ("T", W("T"))], inc="S", amt=16)
            sW = W("S")
            wqs = AR3[:, 0:4096]
            wq2b = wq[:, 0:2048].bitcast(BF16)
            pr.add("V", lambda e, w2sl=w2sl, wqs=wqs: e.tensor_scalar_mul(wqs, w2sl, gq[:, 1:2]),
                   waits=[("S", sW), ("A", g2A), ("T", W("T"))], inc="V")
            pr.add("V", lambda e, wqs=wqs, w2sl=w2sl: e.tensor_single_scalar(w2sl[:, 0:4096], wqs, 0.5, AOP.is_gt), inc="V")
            pr.add("V", lambda e, wqs=wqs: e.tensor_single_scalar(wqs, wqs, -0.5, AOP.is_lt), inc="V")
            pr.add("V", lambda e, wqs=wqs, w2sl=w2sl, wq2b=wq2b: e.tensor_sub(wq2b, w2sl[:, 0:4096], wqs), inc="V")
            vW = W("V")
            hread = {0: 0, 1: 0}
            for kf in range(32):
                hslot = kf % 2
                hhi = AR5[:, hslot * 2048:hslot * 2048 + 1024].bitcast(BF16)
                hlo = AR5[:, hslot * 2048 + 1024:hslot * 2048 + 2048].bitcast(BF16)
                pr.add("S", lambda e, hhi=hhi, kf=kf: e.dma_start(
                    out=hhi, in_=h1_h[kf * 128:(kf + 1) * 128, 0:LH]),
                    waits=[("T", hread[hslot]), ("G", W("G")), ("V", vW)], inc="S", amt=16)
                pr.add("S", lambda e, hlo=hlo, kf=kf: e.dma_start(
                    out=hlo, in_=h1_h[kf * 128:(kf + 1) * 128, LH:2 * LH]),
                    inc="S", amt=16)
                sW = W("S")
                for cg in range(4):
                    bank = [pA0, pA1, pwo0, pwo1][cg]
                    for hl, hsrc in ((0, hhi), (1, hlo)):
                        pr.add("T", lambda e, kf=kf, cg=cg, bank=bank, hsrc=hsrc, wq2b=wq2b, hl=hl: e.matmul(
                            bank[:], wq2b[:, kf * 128:(kf + 1) * 128], hsrc[:, cg * T:(cg + 1) * T],
                            start=(kf == 0 and hl == 0), stop=(kf == 31 and hl == 1)),
                            waits=([("S", sW)] + ([("V", vW), ("A", W("A"))] if (kf == 0 and hl == 0) else [])), inc="T")
                hread[hslot] = W("T")
            tW = W("T")
            for cg in range(4):
                bank = [pA0, pA1, pwo0, pwo1][cg]
                pr.add("A", lambda e, m2=m2, cg=cg, bank=bank: e.activation(
                    AR4[:, m2 * LH + cg * T:m2 * LH + (cg + 1) * T], bank[:], ACT.Silu,
                    bias=b2_sb[:, m2:m2 + 1]), waits=([("T", tW)] if cg == 0 else []), inc="A")
        gA = W("A")

        outG = []
        for tt in range(LH // 128):
            xsl = AR1[:, (tt % 2) * 1024:(tt % 2) * 1024 + 1024]
            pr.add("S", lambda e, xsl=xsl, tt=tt: e.dma_start(out=xsl, in_=x_half[tt * 128:(tt + 1) * 128, :]),
                   waits=[("V", W("V")), ("T", W("T"))], inc="S", amt=16)
            pr.add("S", lambda e, tt=tt: e.dma_start(
                out=AR2[:, 0:1024].rearrange("p (a t) -> p a t", a=8),
                in_=xm_h[:, tt * 128:(tt + 1) * 128].rearrange("(a p) t -> p a t", p=128)),
                waits=[("V", W("V")), ("G", W("G"))], inc="S", amt=16)
            pr.add("S", lambda e, tt=tt: e.dma_start(
                out=AR2[:, 1024:2048].rearrange("p (a t) -> p a t", a=8),
                in_=xm_h[:, LH + tt * 128:LH + (tt + 1) * 128].rearrange("(a p) t -> p a t", p=128)),
                inc="S", amt=16)
            sW = W("S")
            pr.add("V", lambda e: e.tensor_scalar_mul(AR2[:, 2048:3072], AR2[:, 1024:2048], m1_sb[:, 0:1]),
                   waits=[("S", sW), ("A", gA)], inc="V")
            pr.add("V", lambda e: e.scalar_tensor_tensor(
                AR2[:, 3072:4096], AR2[:, 0:1024], m0_sb[:, 0:1], AR2[:, 2048:3072],
                AOP.mult, AOP.add), inc="V")
            for a in range(8):
                pr.add("V", lambda e, a=a, tt=tt: e.tensor_tensor(
                    AR2[:, a * 128:(a + 1) * 128], AR2[:, 3072 + a * 128:3072 + (a + 1) * 128],
                    AR4[:, a * LH + tt * 128:a * LH + (tt + 1) * 128], AOP.mult), inc="V")
            vW = W("V")
            for a in range(8):
                bank = pdel if a % 2 == 0 else pbc
                pr.add("T", lambda e, a=a, bank=bank: e.transpose(
                    bank[:, 0:128], AR2[:, a * 128:(a + 1) * 128], eye_sb[:]),
                    waits=[("V", vW), ("A", W("A"))], inc="T")
                tW = W("T")
                pr.add("A", lambda e, a=a, bank=bank, tt=tt: e.copy(
                    AR3[:, 2048 + (tt % 2) * 1024 + a * 128:2048 + (tt % 2) * 1024 + (a + 1) * 128],
                    bank[:, 0:128]),
                    waits=[("T", tW)] + ([("G", outG[tt - 2])] if tt >= 2 and a == 0 else []), inc="A")
            aW = W("A")
            osl = AR3[:, 2048 + (tt % 2) * 1024:2048 + (tt % 2) * 1024 + 1024]
            pr.add("V", lambda e, osl=osl, xsl=xsl: e.tensor_tensor(osl, osl, xsl, AOP.add),
                   waits=[("A", aW)], inc="V")
            vW = W("V")
            pr.add("G", lambda e, osl=osl, tt=tt: e.dma_start(out=out_t[tt * 128:(tt + 1) * 128, :], in_=osl),
                   waits=[("V", vW)], inc="G", amt=16)
            outG.append(W("G"))

        pr.add("A", lambda e: e.copy(sc1[:, 0:1], sc1[:, 0:1]),
               waits=[("G", W("G")), ("S", W("S"))], inc="A")

        cross_waits = {s: set() for s in SEMS}
        for (eng, fn, waits, inc, amt) in pr.steps:
            for (s, v) in waits:
                cross_waits[s].add((v, eng))
        cnt_replay = dict.fromkeys(SEMS, 0)
        needs_drain = []
        for (eng, fn, waits, inc, amt) in pr.steps:
            nd = False
            if inc is not None:
                lo = cnt_replay[inc]
                cnt_replay[inc] += amt
                hi = cnt_replay[inc]
                if eng in ("V", "A"):
                    for (v, weng) in cross_waits[inc]:
                        if weng != eng and lo < v <= hi:
                            nd = True
                            break
            needs_drain.append(nd)

        def emitter(name):
            def go(E):
                for i, (eng, fn, waits, inc, amt) in enumerate(pr.steps):
                    if eng != name:
                        continue
                    for (s, v) in waits:
                        E.wait_ge(sems[s], v)
                    inst = fn(E)
                    if inc is not None:
                        inst.then_inc(sems[inc], amt)
            return go

        blk.sync(emitter("S"))
        blk.vector(emitter("V"))
        blk.scalar(emitter("A"))
        blk.tensor(emitter("T"))
        blk.gpsimd(emitter("G"))

    return nc


def kernel(L0_attn_x, L0_attn_norm_w, L0_attn_W_in, L0_attn_conv_w, L0_attn_conv_b,
           L0_attn_W_x, L0_attn_W_dt, L0_attn_b_dt, L0_attn_A_log, L0_attn_D,
           L0_attn_W_out, L0_ffn_W1, L0_ffn_b1, L0_ffn_W2, L0_ffn_b2):
    f32 = np.float32
    x = np.ascontiguousarray(np.asarray(L0_attn_x, dtype=f32))

    def tile128(a):
        dl, k = a.shape
        return np.ascontiguousarray(a.reshape(KT, 128, k).transpose(1, 0, 2).reshape(128, KT * k))

    if "prog" not in _NC_CACHE:
        _NC_CACHE["prog"] = build_nc()
    nc = _NC_CACHE["prog"]

    in_maps = []
    for c in range(8):
        b, half = c // 2, c % 2
        r0 = half * DL
        rows = slice(r0, r0 + DL)
        m = {
            "x_full": np.ascontiguousarray(x[b]),
            "x_half": np.ascontiguousarray(x[b, half * LH:(half + 1) * LH]),
            "w1T": np.ascontiguousarray(np.asarray(L0_ffn_W1, f32).T),
            "w2T": np.ascontiguousarray(np.asarray(L0_ffn_W2, f32).T),
            "b1c": np.ascontiguousarray(np.asarray(L0_ffn_b1, f32).reshape(32, 128).T),
            "b2c": np.ascontiguousarray(np.asarray(L0_ffn_b2, f32).reshape(8, 128).T),
            "ones1": np.ones((128, 128), f32),
            "onescol": np.ones((128, 1), f32),
            "eye": np.eye(128, dtype=f32),
            "m0c": np.full((128, 1), 1.0 if half == 0 else 0.0, f32),
            "m1c": np.full((128, 1), 1.0 if half == 1 else 0.0, f32),
            "cvals": np.tile(np.array([[EPS, 1.0]], f32), (128, 1)),
        }
        W_in = np.asarray(L0_attn_W_in, f32)
        W_x = np.asarray(L0_attn_W_x, f32)
        W_dt = np.asarray(L0_attn_W_dt, f32)
        W_out = np.asarray(L0_attn_W_out, f32)
        for i in range(NL):
            sel = np.concatenate([W_in[i][r0:r0 + DL], W_in[i][2048 + r0:2048 + r0 + DL]], axis=0)
            m[f"winT{i}"] = np.ascontiguousarray(sel.T)
            m[f"wxT{i}"] = np.ascontiguousarray(W_x[i][:, rows].T)
            m[f"wdtT{i}"] = np.ascontiguousarray(W_dt[i][rows].T)
            m[f"woutT{i}"] = np.ascontiguousarray(W_out[i][:, rows].T)
            m[f"convw{i}"] = tile128(np.asarray(L0_attn_conv_w, f32)[i][rows])
            m[f"convb{i}"] = tile128(np.asarray(L0_attn_conv_b, f32)[i][rows, None])
            m[f"bdt{i}"] = tile128(np.asarray(L0_attn_b_dt, f32)[i][rows, None])
            m[f"alog{i}"] = tile128(np.asarray(L0_attn_A_log, f32)[i][rows])
            m[f"dvec{i}"] = tile128(np.asarray(L0_attn_D, f32)[i][rows, None])
            m[f"normw{i}"] = np.ascontiguousarray(
                np.asarray(L0_attn_norm_w, f32)[i].reshape(8, 128).T)
        in_maps.append(m)

    res = run_bass_kernel_spmd(nc, in_maps, core_ids=list(range(8)))
    out = np.empty((4, L, DM), f32)
    for c in range(8):
        b, half = c // 2, c % 2
        out[b, half * LH:(half + 1) * LH] = res.results[c]["out"]
    return out


# version 14
# speedup vs baseline: 1.0247x; 1.0091x over previous
import numpy as np
import concourse.bass as bass
import concourse.mybir as mybir
from concourse.bass_utils import run_bass_kernel_spmd

F32 = mybir.dt.float32
BF16 = mybir.dt.bfloat16
AOP = mybir.AluOpType
ACT = mybir.ActivationFunctionType
AXX = mybir.AxisListType.X

L = 4096
LH = 2048
DM = 1024
DL = 1024
KT = 8
NST = 16
RK = 64
T = 512
NCH = L // T
SL = T + 1
CW = KT * SL
NL = 4
EPS = 1e-5

SEMS = ["S", "G", "T", "V", "A", "AD"]

_NC_CACHE = {}


class Prog:
    def __init__(self):
        self.steps = []
        self.cnt = dict.fromkeys(SEMS, 0)

    def add(self, eng, fn, waits=None, inc=None, amt=1):
        w = [(s, v) for (s, v) in (waits or []) if v > 0]
        if inc is not None:
            self.cnt[inc] += amt
        self.steps.append((eng, fn, w, inc, amt))

    def c(self, s):
        return self.cnt[s]


def build_nc(dbg=False, nl=NL):
    nc = bass.Bass("TRN2", target_bir_lowering=False, debug=False)
    KIND = "ExternalOutput" if dbg else "Internal"

    x_full = nc.dram_tensor("x_full", [L, DM], F32, kind="ExternalInput").ap()
    x_half = nc.dram_tensor("x_half", [LH, DM], F32, kind="ExternalInput").ap()
    winT, wxT, wdtT, woutT = [], [], [], []
    convw, convb, bdt, alog, dvec, normw = [], [], [], [], [], []
    for i in range(NL):
        winT.append(nc.dram_tensor(f"winT{i}", [DM, 2 * DL], F32, kind="ExternalInput").ap())
        wxT.append(nc.dram_tensor(f"wxT{i}", [DL, 96], F32, kind="ExternalInput").ap())
        wdtT.append(nc.dram_tensor(f"wdtT{i}", [RK, DL], F32, kind="ExternalInput").ap())
        woutT.append(nc.dram_tensor(f"woutT{i}", [DL, DM], F32, kind="ExternalInput").ap())
        convw.append(nc.dram_tensor(f"convw{i}", [128, 32], F32, kind="ExternalInput").ap())
        convb.append(nc.dram_tensor(f"convb{i}", [128, 8], F32, kind="ExternalInput").ap())
        bdt.append(nc.dram_tensor(f"bdt{i}", [128, 8], F32, kind="ExternalInput").ap())
        alog.append(nc.dram_tensor(f"alog{i}", [128, 128], F32, kind="ExternalInput").ap())
        dvec.append(nc.dram_tensor(f"dvec{i}", [128, 8], F32, kind="ExternalInput").ap())
        normw.append(nc.dram_tensor(f"normw{i}", [128, 8], F32, kind="ExternalInput").ap())
    w1T = nc.dram_tensor("w1T", [DM, 4096], F32, kind="ExternalInput").ap()
    w2T = nc.dram_tensor("w2T", [4096, DM], F32, kind="ExternalInput").ap()
    b1c = nc.dram_tensor("b1c", [128, 32], F32, kind="ExternalInput").ap()
    b2c = nc.dram_tensor("b2c", [128, 8], F32, kind="ExternalInput").ap()
    ones1 = nc.dram_tensor("ones1", [128, 128], F32, kind="ExternalInput").ap()
    onescol = nc.dram_tensor("onescol", [128, 1], F32, kind="ExternalInput").ap()
    eye_in = nc.dram_tensor("eye", [128, 128], F32, kind="ExternalInput").ap()
    m0_in = nc.dram_tensor("m0c", [128, 1], F32, kind="ExternalInput").ap()
    m1_in = nc.dram_tensor("m1c", [128, 1], F32, kind="ExternalInput").ap()
    cv_in = nc.dram_tensor("cvals", [128, 2], F32, kind="ExternalInput").ap()

    out_t = nc.dram_tensor("out", [LH, DM], F32, kind="ExternalOutput").ap()

    xm_h = nc.dram_tensor("xm_h", [DM, L], F32, kind=KIND).ap()
    xc_h = nc.dram_tensor("xc_h", [DL, L], F32, kind=KIND).ap()
    zs_h = nc.dram_tensor("zs_h", [DL, L], F32, kind=KIND).ap()
    h1_h = nc.dram_tensor("h1_h", [4096, 2 * LH], BF16, kind="Internal").ap()
    ccd_i = nc.dram_tensor("ccd_i", [96, L], F32, kind="Internal").ap()
    ccd_o = nc.dram_tensor("ccd_o", [96, L], F32, kind="Internal").ap()
    ccw_i = nc.dram_tensor("ccw_i", [DM, L], F32, kind="Internal").ap()
    ccw_o = nc.dram_tensor("ccw_o", [DM, L], F32, kind="Internal").ap()
    if dbg:
        dbg_d = nc.dram_tensor("dbg_d", [96, L], F32, kind="ExternalOutput").ap()
        dbg_w = nc.dram_tensor("dbg_w", [DM, L], F32, kind="ExternalOutput").ap()
        dbg_s = nc.dram_tensor("dbg_s", [128, 512], F32, kind="ExternalOutput").ap()
    RG = [[0, 1], [2, 3], [4, 5], [6, 7]]

    import contextlib
    ctx = contextlib.ExitStack()
    with ctx:
        AR1 = ctx.enter_context(nc.sbuf_tensor("AR1", [128, CW], F32))
        AR2 = ctx.enter_context(nc.sbuf_tensor("AR2", [128, 4096], F32))
        AR3 = ctx.enter_context(nc.sbuf_tensor("AR3", [128, KT * 516], F32))
        AR4 = ctx.enter_context(nc.sbuf_tensor("AR4", [128, 4 * CW], F32))
        AR5 = ctx.enter_context(nc.sbuf_tensor("AR5", [128, 8192], F32))
        AR6 = ctx.enter_context(nc.sbuf_tensor("AR6", [128, 7680], F32))
        AR7 = ctx.enter_context(nc.sbuf_tensor("AR7", [128, CW], F32))
        state = ctx.enter_context(nc.sbuf_tensor("state", [128, NST * KT], F32))
        vecs = ctx.enter_context(nc.sbuf_tensor("vecs", [128, 64], F32))
        acol = ctx.enter_context(nc.sbuf_tensor("acol", [128, 128], F32))
        atmp = ctx.enter_context(nc.sbuf_tensor("atmp", [128, 128], F32))
        wdt_sb = ctx.enter_context(nc.sbuf_tensor("wdt_sb", [64, DL], F32))
        wx_sb = ctx.enter_context(nc.sbuf_tensor("wx_sb", [128, KT * 96], F32))
        b1_sb = ctx.enter_context(nc.sbuf_tensor("b1_sb", [128, 32], F32))
        b2_sb = ctx.enter_context(nc.sbuf_tensor("b2_sb", [128, 8], F32))
        ones1_sb = ctx.enter_context(nc.sbuf_tensor("ones1_sb", [128, 128], F32))
        onesc_sb = ctx.enter_context(nc.sbuf_tensor("onesc_sb", [128, 1], F32))
        eye_sb = ctx.enter_context(nc.sbuf_tensor("eye_sb", [128, 128], F32))
        m0_sb = ctx.enter_context(nc.sbuf_tensor("m0_sb", [128, 1], F32))
        m1_sb = ctx.enter_context(nc.sbuf_tensor("m1_sb", [128, 1], F32))
        cvals = ctx.enter_context(nc.sbuf_tensor("cvals_sb", [128, 2], F32))
        nc.const_aps.register(F32, EPS, cvals[:, 0:1])
        nc.const_aps.register(F32, 1.0, cvals[:, 1:2])
        rs1 = ctx.enter_context(nc.sbuf_tensor("rs1", [1, T], F32))
        rs2 = rs1
        sc1 = ctx.enter_context(nc.sbuf_tensor("sc1", [128, 8], F32))
        gq = ctx.enter_context(nc.sbuf_tensor("gq", [128, 4], F32))

        wq = AR1

        pA0 = ctx.enter_context(nc.psum_tensor("pA0", [128, T], F32))
        pA1 = ctx.enter_context(nc.psum_tensor("pA1", [128, T], F32))
        pdbc = ctx.enter_context(nc.psum_tensor("pdbc", [128, T], F32))
        pbc = ctx.enter_context(nc.psum_tensor("pbc", [128, T], F32))
        pdel = ctx.enter_context(nc.psum_tensor("pdel", [128, T], F32))
        pwo0 = ctx.enter_context(nc.psum_tensor("pwo0", [128, T], F32))
        pwo1 = ctx.enter_context(nc.psum_tensor("pwo1", [128, T], F32))

        sems = {s: ctx.enter_context(nc.semaphore(s)) for s in SEMS}
        blk = ctx.enter_context(nc.Block())

        scarr = AR6[:, 0:768]
        bcst = AR7[:, 0:2048]

        pr = Prog()
        W = pr.c

        def ln_pass_s(xsl, tt, w0):
            pr.add("V", lambda e, xsl=xsl, tt=tt: e.tensor_reduce(
                scarr[:, tt:tt + 1], xsl, AXX, AOP.add), waits=w0, inc="V")
            pr.add("V", lambda e, xsl=xsl: e.tensor_tensor(AR2[:, 0:1024], xsl, xsl, AOP.mult),
                   waits=[("T", W("T")), ("A", W("A"))], inc="V")
            pr.add("V", lambda e, tt=tt: e.tensor_reduce(
                scarr[:, 128 + tt:129 + tt], AR2[:, 0:1024], AXX, AOP.add), inc="V")

        def ln_pass_m():
            pr.add("V", lambda e: e.tensor_scalar_mul(scarr[:, 256:384], scarr[:, 0:128], 1.0 / DM), inc="V")
            pr.add("V", lambda e: e.tensor_tensor(scarr[:, 384:512], scarr[:, 256:384],
                                                  scarr[:, 256:384], AOP.mult), inc="V")
            pr.add("V", lambda e: e.tensor_scalar_mul(scarr[:, 512:640], scarr[:, 128:256], 1.0 / DM), inc="V")
            pr.add("V", lambda e: e.tensor_sub(scarr[:, 512:640], scarr[:, 512:640],
                                               scarr[:, 384:512]), inc="V")
            vW = W("V")
            pr.add("A", lambda e: e.activation(scarr[:, 640:768], scarr[:, 512:640], ACT.Ln, bias=EPS),
                   waits=[("V", vW)], inc="A")
            pr.add("A", lambda e: e.activation(scarr[:, 640:768], scarr[:, 640:768], ACT.Exp, scale=-0.5),
                   inc="A")
            return W("A")

        def ln_pass_n(xsl, xnsl, tt, w0):
            pr.add("V", lambda e, xsl=xsl, xnsl=xnsl, tt=tt: e.tensor_scalar(
                xnsl, xsl, scarr[:, 256 + tt:257 + tt], scarr[:, 640 + tt:641 + tt],
                AOP.subtract, AOP.mult), waits=w0, inc="V")
            return W("V")

        for dst, src in [(cvals[:], cv_in), (ones1_sb[:], ones1), (onesc_sb[:], onescol),
                         (eye_sb[:], eye_in), (b1_sb[:], b1c), (b2_sb[:], b2c),
                         (m0_sb[:], m0_in), (m1_sb[:], m1_in)]:
            pr.add("S", lambda e, dst=dst, src=src: e.dma_start(out=dst, in_=src), inc="S", amt=16)

        for tt in range(L // 128):
            xsl = AR1[:, (tt % 4) * 1024:(tt % 4) * 1024 + 1024]
            pr.add("S", lambda e, xsl=xsl, tt=tt: e.dma_start(out=xsl, in_=x_full[tt * 128:(tt + 1) * 128, :]),
                   waits=[("V", W("V"))], inc="S", amt=16)
            sW = W("S")
            ln_pass_s(xsl, tt, [("S", sW)])
        aW_ln = ln_pass_m()
        for tt in range(L // 128):
            xsl = AR1[:, (tt % 4) * 1024:(tt % 4) * 1024 + 1024]
            pr.add("S", lambda e, xsl=xsl, tt=tt: e.dma_start(out=xsl, in_=x_full[tt * 128:(tt + 1) * 128, :]),
                   waits=[("V", W("V"))], inc="S", amt=16)
            sW = W("S")
            xnsl = AR2[:, 0:1024]
            ln_pass_n(xsl, xnsl, tt, [("S", sW), ("A", aW_ln), ("T", W("T"))])
            vW = W("V")
            prevG = W("G")
            for a in range(8):
                bank = pdel if a % 2 == 0 else pbc
                pr.add("T", lambda e, bank=bank, xnsl=xnsl, a=a: e.transpose(
                    bank[:, 0:128], xnsl[:, a * 128:(a + 1) * 128], eye_sb[:]),
                    waits=[("V", vW), ("A", W("A")), ("G", prevG)], inc="T")
                tW = W("T")
                pr.add("A", lambda e, a=a, bank=bank: e.copy(AR3[:, a * 128:(a + 1) * 128], bank[:, 0:128]),
                       waits=[("T", tW)], inc="A")
            aW = W("A")
            pr.add("G", lambda e, tt=tt: e.dma_start(
                out=xm_h[:, tt * 128:(tt + 1) * 128].rearrange("(a p) t -> p a t", p=128),
                in_=AR3[:, 0:1024].rearrange("p (a t) -> p a t", a=8)),
                waits=[("A", aW)], inc="G", amt=16)

        def spacer():
            pr.add("V", lambda e: e.tensor_copy(AR2[:, 1024:1536], AR2[:, 1536:2048]), inc="V")

        gam_cons = {0: W("A"), 1: W("A")}
        for gi, (wt, n_t, ldfn) in enumerate([
            (w1T, 8, lambda e, t, wsl: e.dma_start(out=wsl, in_=w1T[t * 128:(t + 1) * 128, :])),
            (w2T, 8, lambda e, t, wsl: e.dma_start(
                out=wsl.rearrange("p (a m) -> p a m", a=4),
                in_=w2T[t * 512:(t + 1) * 512, :].rearrange("(a p) m -> p a m", p=128))),
        ]):
            for t in range(n_t):
                wsl = AR5[:, (t % 2) * 4096:(t % 2) * 4096 + 4096]
                pr.add("S", lambda e, t=t, wsl=wsl, ldfn=ldfn: ldfn(e, t, wsl),
                       waits=[("A", gam_cons[t % 2])], inc="S", amt=16)
                sW = W("S")
                pr.add("A", lambda e, wsl=wsl: e.activation(AR2[:, 0:4096], wsl, ACT.Abs),
                       waits=[("S", sW), ("V", W("V"))], inc="A")
                gam_cons[t % 2] = W("A")
                aW = W("A")
                pr.add("V", lambda e, t=t, gi=gi: e.tensor_reduce(
                    atmp[:, 64 + gi * 8 + t:64 + gi * 8 + t + 1], AR2[:, 0:4096], AXX, AOP.add),
                    waits=[("A", aW)], inc="V")
            spacer()
            pr.add("V", lambda e, gi=gi: e.tensor_reduce(
                gq[:, gi:gi + 1], atmp[:, 64 + gi * 8:64 + gi * 8 + 8], AXX, AOP.add), inc="V")
            spacer()
            vW = W("V")
            pr.add("T", lambda e, gi=gi: e.matmul(pbc[0:1, 0:1], onesc_sb[:], gq[:, gi:gi + 1],
                                                  start=True, stop=True), waits=[("V", vW)], inc="T")
            tW = W("T")
            pr.add("V", lambda e: e.tensor_scalar(gq[0:1, 3:4], pbc[0:1, 0:1], 1.0 / (4096.0 * DM), EPS,
                                                  AOP.mult, AOP.add), waits=[("T", tW)], inc="V")
            spacer()
            pr.add("V", lambda e: e.reciprocal(gq[0:1, 2:3], gq[0:1, 3:4]), inc="V")
            spacer()
            vW = W("V")
            pr.add("T", lambda e: e.matmul(pbc[:, 0:1], ones1_sb[0:1, :], gq[0:1, 2:3],
                                           start=True, stop=True), waits=[("V", vW)], inc="T")
            tW = W("T")
            pr.add("A", lambda e, gi=gi: e.copy(gq[:, gi:gi + 1], pbc[:, 0:1]), waits=[("T", tW)], inc="A")
        g1A = g2A = W("A")


        for li in range(nl):
            pr.add("S", lambda e, li=li: e.dma_start(out=vecs[:, 0:32], in_=convw[li]),
                   waits=[("V", W("V")), ("A", W("A"))], inc="S", amt=16)
            pr.add("S", lambda e, li=li: e.dma_start(out=vecs[:, 32:40], in_=convb[li]), inc="S", amt=16)
            pr.add("S", lambda e, li=li: e.dma_start(out=vecs[:, 40:48], in_=bdt[li]), inc="S", amt=16)
            pr.add("S", lambda e, li=li: e.dma_start(out=vecs[:, 48:56], in_=dvec[li]), inc="S", amt=16)
            pr.add("S", lambda e, li=li: e.dma_start(out=vecs[:, 56:64], in_=normw[li]), inc="S", amt=16)
            pr.add("S", lambda e, li=li: e.dma_start(out=atmp[:], in_=alog[li]), inc="S", amt=16)
            pr.add("S", lambda e, li=li: e.dma_start(out=wdt_sb[:], in_=wdtT[li]),
                   waits=[("T", W("T"))], inc="S", amt=16)
            pr.add("S", lambda e, li=li: e.dma_start(
                out=wx_sb[:].rearrange("p (a m) -> p a m", a=KT),
                in_=wxT[li].rearrange("(a p) m -> p a m", p=128)), inc="S", amt=16)
            pr.add("S", lambda e, li=li: e.dma_start(
                out=AR4[:, 0:16384].rearrange("p (a m) -> p a m", a=8),
                in_=winT[li].rearrange("(a p) m -> p a m", p=128)),
                waits=[("V", W("V"))], inc="S", amt=16)
            sWL = W("S")
            pr.add("A", lambda e: e.activation(acol[:], atmp[:], ACT.Exp), waits=[("S", sWL)], inc="A")
            pr.add("V", lambda e: e.tensor_scalar_mul(acol[:], acol[:], -1.0),
                   waits=[("A", W("A"))], inc="V")
            pr.add("V", lambda e: e.memset(state[:], 0.0), inc="V")
            pr.add("V", lambda e: e.memset(
                AR3[:].rearrange("p (a q) -> p a q", a=KT)[:, :, 0:3], 0.0),
                waits=[("A", W("A")), ("G", W("G"))], inc="V")

            evac_cnt = {0: 0, 1: 0}
            for c in range(NCH):
                cs = c * T
                pr.add("S", lambda e, cs=cs: e.dma_start(
                    out=AR1[:, 0:4096].rearrange("p (a t) -> p a t", a=8),
                    in_=xm_h[:, cs:cs + T].rearrange("(a p) t -> p a t", p=128)),
                    waits=[("V", W("V")), ("T", W("T")), ("G", W("G"))], inc="S", amt=16)
                sW = W("S")
                pr.add("A", lambda e: e.activation(AR2[:, 0:4096], AR1[:, 0:4096], ACT.Square),
                       waits=[("S", sW), ("T", W("T")), ("V", W("V"))], inc="A")
                aW = W("A")
                for j in range(8):
                    pr.add("T", lambda e, j=j: e.matmul(pbc[0:1, :], onesc_sb[:], AR2[:, j * T:(j + 1) * T],
                                                        start=(j == 0), stop=(j == 7)),
                           waits=([("A", aW)] if j == 0 else []), inc="T")
                tW = W("T")
                pr.add("A", lambda e: e.activation(rs1[:], pbc[0:1, :], ACT.Ln, scale=1.0 / DM, bias=EPS),
                       waits=[("T", tW)], inc="A")
                pr.add("A", lambda e: e.activation(rs2[:], rs1[:], ACT.Exp, scale=-0.5), inc="A")
                aW = W("A")
                pr.add("T", lambda e: e.matmul(pdel[:], ones1_sb[0:1, :], rs2[:], start=True, stop=True),
                       waits=[("A", aW)], inc="T")
                tW = W("T")
                for j in range(8):
                    pr.add("V", lambda e, j=j: e.scalar_tensor_tensor(
                        AR2[:, j * T:(j + 1) * T], AR1[:, j * T:(j + 1) * T], vecs[:, 56 + j:57 + j],
                        pdel[:], AOP.mult, AOP.mult), waits=([("T", tW)] if j == 0 else []), inc="V")
                hW = W("V")
                silu_done = {}
                for m in range(16):
                    bank = pA0 if m % 2 == 0 else pA1
                    for k in range(8):
                        pr.add("T", lambda e, m=m, k=k, bank=bank: e.matmul(
                            bank[:], AR4[:, k * 2048 + m * 128:k * 2048 + (m + 1) * 128],
                            AR2[:, k * T:(k + 1) * T], start=(k == 0), stop=(k == 7)),
                            waits=([("V", hW), ("A", evac_cnt[m % 2])] if k == 0 else []), inc="T")
                    tW = W("T")
                    if m < 8:
                        base = m * 516
                        xsl = AR3[:, base + 3:base + 515]
                        w_extra = [("G", W("G"))] if (li == 0 and c == 0 and m == 0) else []
                        pr.add("A", lambda e, xsl=xsl, bank=bank: e.copy(xsl, bank[:]),
                               waits=[("T", tW), ("V", W("V"))] + w_extra, inc="A")
                        evac_cnt[m % 2] = W("A")
                        aW = W("A")
                        xcm = AR6[:, 4096 + (m % 2) * T:4096 + (m % 2) * T + T]
                        pr.add("V", lambda e, xcm=xcm, base=base, m=m: e.tensor_scalar_mul(
                            xcm, AR3[:, base:base + T], vecs[:, m * 4:m * 4 + 1]),
                            waits=[("A", aW)], inc="V")
                        for s in range(1, 4):
                            pr.add("V", lambda e, xcm=xcm, base=base, m=m, s=s: e.scalar_tensor_tensor(
                                xcm, AR3[:, base + s:base + s + T], vecs[:, m * 4 + s:m * 4 + s + 1],
                                xcm, AOP.mult, AOP.add), inc="V")
                        vW = W("V")
                        xcs = AR6[:, m * T:(m + 1) * T]
                        pr.add("A", lambda e, xcs=xcs, xcm=xcm, m=m: e.activation(
                            xcs, xcm, ACT.Silu, bias=vecs[:, 32 + m:33 + m]),
                            waits=[("V", vW), ("G", W("G")), ("T", W("T"))], inc="A")
                        silu_done[m] = W("A")
                        aW = W("A")
                        pr.add("G", lambda e, xcs=xcs, m=m, cs=cs: e.dma_start(
                            out=xc_h[m * 128:(m + 1) * 128, cs:cs + T], in_=xcs),
                            waits=[("A", aW)], inc="G", amt=16)
                        pr.add("V", lambda e, base=base: e.tensor_copy(
                            AR3[:, base:base + 3], AR3[:, base + 512:base + 515]), inc="V")
                    else:
                        mm = m - 8
                        szs = AR6[:, 4096 + (m % 2) * T:4096 + (m % 2) * T + T]
                        pr.add("A", lambda e, szs=szs, bank=bank: e.activation(szs, bank[:], ACT.Silu),
                               waits=[("T", tW), ("G", W("G"))], inc="A")
                        evac_cnt[m % 2] = W("A")
                        aW = W("A")
                        pr.add("G", lambda e, szs=szs, mm=mm, cs=cs: e.dma_start(
                            out=zs_h[mm * 128:(mm + 1) * 128, cs:cs + T], in_=szs),
                            waits=[("A", aW)], inc="G", amt=16)
                for m in range(8):
                    pr.add("T", lambda e, m=m: e.matmul(
                        pdbc[0:96, :], wx_sb[:, m * 96:(m + 1) * 96], AR6[:, m * T:(m + 1) * T],
                        start=(m == 0), stop=(m == 7)),
                        waits=([("A", silu_done[7])] if m == 0 else []), inc="T")
                tW = W("T")
                pr.add("A", lambda e, cs=cs: e.copy(AR7[0:96, cs:cs + T], pdbc[0:96, :]),
                       waits=[("T", tW)], inc="A")
            aW = W("A")
            pr.add("G", lambda e: e.dma_start(out=ccd_i, in_=AR7[0:96, 0:4096]),
                   waits=[("A", aW)], inc="G", amt=16)
            gW = W("G")
            pr.add("G", lambda e: e.collective_compute("AllReduce", AOP.add, replica_groups=RG,
                                                       ins=[ccd_i], outs=[ccd_o]),
                   waits=[("G", gW)], inc="G", amt=1)
            gW = W("G")
            pr.add("G", lambda e: e.dma_start(out=AR2[0:96, 0:4096], in_=ccd_o),
                   waits=[("G", gW), ("V", W("V")), ("T", W("T")), ("A", W("A"))], inc="G", amt=16)
            dbcG = W("G")
            pr.add("S", lambda e, li=li: e.dma_start(
                out=AR5[:].rearrange("p (a m) -> p a m", a=KT),
                in_=woutT[li].rearrange("(a p) m -> p a m", p=128)),
                waits=[("A", W("A")), ("T", W("T"))], inc="S", amt=16)
            woutS = W("S")

            U8 = AR4[:, 0 * CW:1 * CW]
            DA8 = AR4[:, 1 * CW:2 * CW]
            BX8 = AR4[:, 2 * CW:3 * CW]
            H8 = AR4[:, 3 * CW:4 * CW]
            pr.add("V", lambda e: e.memset(DA8.rearrange("p (a q) -> p a q", a=KT)[:, :, 0:1], 0.0),
                   waits=[("T", W("T"))], inc="V")
            bx_done = W("V")
            sc_done = W("V")
            ch_done = W("V")
            bc_consumed = {0: 0, 1: 0}
            cc_consumed = {0: 0, 1: 0}
            for c in range(NCH):
                cs = c * T
                del_bank = {0: 0, 1: 0}
                for k in range(KT):
                    bankd = pdel if k % 2 == 0 else pbc
                    pr.add("T", lambda e, k=k, cs=cs, bankd=bankd: e.matmul(
                        bankd[:], wdt_sb[:, k * 128:(k + 1) * 128], AR2[0:64, cs:cs + T],
                        start=True, stop=True),
                        waits=([("G", dbcG), ("A", W("A")), ("V", W("V"))] if k == 0
                               else [("A", del_bank[k % 2])]), inc="T")
                    tW = W("T")
                    tmp = AR6[:, 7168:7680]
                    pr.add("A", lambda e, tmp=tmp, k=k, bankd=bankd: e.activation(
                        tmp, bankd[:], ACT.Exp, bias=vecs[:, 40 + k:41 + k]),
                        waits=[("T", tW)], inc="A")
                    dsl = AR6[:, k * T:(k + 1) * T]
                    pr.add("A", lambda e, dsl=dsl, tmp=tmp: e.activation(dsl, tmp, ACT.Ln, bias=1.0), inc="A")
                    del_bank[k % 2] = W("A")
                    aW = W("A")
                    xck = AR6[:, 4096 + (k % 2) * T:4096 + (k % 2) * T + T]
                    pr.add("A", lambda e, xck=xck, k=k, cs=cs: e.dma_start(
                        out=xck, in_=xc_h[k * 128:(k + 1) * 128, cs:cs + T]),
                        waits=[("V", W("V"))], inc="AD", amt=16)
                    adW = W("AD")
                    pr.add("V", lambda e, k=k, dsl=dsl, xck=xck: e.tensor_tensor(
                        U8[:, k * SL + 1:k * SL + 1 + T], dsl, xck, AOP.mult),
                        waits=[("A", aW), ("AD", adW), ("T", W("T"))], inc="V")
                    pr.add("V", lambda e, k=k, xck=xck: e.tensor_scalar_mul(
                        AR1[:, k * SL + 1:k * SL + 1 + T], xck, vecs[:, 48 + k:49 + k]), inc="V")
                for n in range(NST):
                    bslot = bcst[:, (n % 2) * T:(n % 2) * T + T]
                    cslot = bcst[:, 1024 + (n % 2) * T:1024 + (n % 2) * T + T]
                    pr.add("S", lambda e, n=n, cs=cs, bslot=bslot: e.dma_start(
                        out=bslot, in_=ccd_o[64 + n:65 + n, cs:cs + T].to_broadcast([128, T])),
                        waits=[("V", bc_consumed[n % 2]), ("G", dbcG)], inc="S", amt=16)
                    sWb = W("S")
                    pr.add("G", lambda e, n=n, cs=cs, cslot=cslot: e.dma_start(
                        out=cslot, in_=ccd_o[80 + n:81 + n, cs:cs + T].to_broadcast([128, T])),
                        waits=[("V", cc_consumed[n % 2])], inc="G", amt=16)
                    gWc = W("G")
                    for k in range(KT):
                        pr.add("A", lambda e, k=k, n=n: e.activation(
                            DA8[:, k * SL + 1:k * SL + 1 + T], AR6[:, k * T:(k + 1) * T], ACT.Exp,
                            scale=acol[:, k * NST + n:k * NST + n + 1]),
                            waits=([("V", sc_done)] if k == 0 else []), inc="A")
                    aW = W("A")
                    pr.add("V", lambda e, n=n: e.tensor_copy(
                        BX8.rearrange("p (a q) -> p a q", a=KT)[:, :, 0],
                        state[:, n * KT:(n + 1) * KT]), waits=[("A", aW), ("S", sWb)], inc="V")
                    for k in range(KT):
                        pr.add("V", lambda e, k=k, bslot=bslot: e.tensor_tensor(
                            BX8[:, k * SL + 1:k * SL + 1 + T], U8[:, k * SL + 1:k * SL + 1 + T],
                            bslot, AOP.mult), inc="V")
                    bc_consumed[n % 2] = W("V")
                    bx_done = W("V")
                    pr.add("V", lambda e: e.tensor_tensor_scan(H8[:], DA8[:], BX8[:], 0.0, AOP.mult, AOP.add),
                           inc="V")
                    sc_done = W("V")
                    for k in range(KT):
                        pr.add("V", lambda e, k=k, cslot=cslot: e.tensor_tensor(
                            BX8[:, k * SL + 1:k * SL + 1 + T], H8[:, k * SL + 1:k * SL + 1 + T],
                            cslot, AOP.mult), waits=([("G", gWc)] if k == 0 else []), inc="V")
                    cc_consumed[n % 2] = W("V")
                    ch_done = W("V")
                    pr.add("V", lambda e: e.tensor_tensor(
                        AR1.rearrange("p (a q) -> p a q", a=KT)[:, :, 1:SL],
                        AR1.rearrange("p (a q) -> p a q", a=KT)[:, :, 1:SL],
                        BX8.rearrange("p (a q) -> p a q", a=KT)[:, :, 1:SL], AOP.add), inc="V")
                    pr.add("V", lambda e, n=n: e.tensor_copy(
                        state[:, n * KT:(n + 1) * KT],
                        H8.rearrange("p (a q) -> p a q", a=KT)[:, :, SL - 1]), inc="V")
                for k in range(KT):
                    szk = AR6[:, 5120 + (k % 2) * T:5120 + (k % 2) * T + T]
                    pr.add("S", lambda e, szk=szk, k=k, cs=cs: e.dma_start(
                        out=szk, in_=zs_h[k * 128:(k + 1) * 128, cs:cs + T]),
                        waits=[("V", W("V"))], inc="S", amt=16)
                    sW = W("S")
                    pr.add("V", lambda e, k=k, szk=szk: e.tensor_tensor(
                        AR1[:, k * SL + 1:k * SL + 1 + T], AR1[:, k * SL + 1:k * SL + 1 + T],
                        szk, AOP.mult), waits=[("S", sW)], inc="V")
                gV = W("V")
                for m2 in range(8):
                    bank = pwo0 if m2 % 2 == 0 else pwo1
                    for k in range(KT):
                        pr.add("T", lambda e, m2=m2, k=k, bank=bank: e.matmul(
                            bank[:], AR5[:, k * 1024 + m2 * 128:k * 1024 + (m2 + 1) * 128],
                            AR1[:, k * SL + 1:k * SL + 1 + T], start=(k == 0), stop=(k == KT - 1)),
                            waits=([("V", gV), ("S", woutS), ("A", W("A"))] if k == 0 else []), inc="T")
                    tW = W("T")
                    osl = AR6[:, 6144 + (m2 % 2) * T:6144 + (m2 % 2) * T + T]
                    pr.add("A", lambda e, osl=osl, bank=bank: e.copy(osl, bank[:]),
                           waits=[("T", tW), ("G", W("G"))], inc="A")
                    aW = W("A")
                    pr.add("G", lambda e, osl=osl, m2=m2, cs=cs: e.dma_start(
                        out=ccw_i[m2 * 128:(m2 + 1) * 128, cs:cs + T], in_=osl),
                        waits=[("A", aW)], inc="G", amt=16)
            gW = W("G")
            pr.add("G", lambda e: e.collective_compute("AllReduce", AOP.add, replica_groups=RG,
                                                       ins=[ccw_i], outs=[ccw_o]),
                   waits=[("G", gW)], inc="G", amt=1)
            woG = W("G")
            if dbg:
                pr.add("G", lambda e: e.dma_start(out=dbg_d, in_=ccd_o), inc="G", amt=16)
                pr.add("G", lambda e: e.dma_start(out=dbg_w, in_=ccw_o), inc="G", amt=16)
            upd_v = {0: None, 1: None}
            upd_g = {0: 0, 1: 0}
            for c in range(NCH):
                cs = c * T
                for h2 in range(2):
                    hs = cs + h2 * 256
                    slot = (c * 2 + h2) % 2
                    off = slot * 2048
                    w0 = ([("V", upd_v[slot]), ("G", upd_g[slot])] if upd_v[slot] else [
                        ("V", W("V")), ("T", W("T")), ("A", W("A")), ("G", W("G"))])
                    pr.add("S", lambda e, hs=hs, off=off: e.dma_start(
                        out=AR1[:, off:off + 2048].rearrange("p (a t) -> p a t", a=8),
                        in_=xm_h[:, hs:hs + 256].rearrange("(a p) t -> p a t", p=128)),
                        waits=w0, inc="S", amt=16)
                    pr.add("S", lambda e, hs=hs, off=off: e.dma_start(
                        out=AR2[:, off:off + 2048].rearrange("p (a t) -> p a t", a=8),
                        in_=ccw_o[:, hs:hs + 256].rearrange("(a p) t -> p a t", p=128)),
                        waits=[("G", woG)] if (c == 0 and h2 == 0) else [], inc="S", amt=16)
                    sW = W("S")
                    pr.add("V", lambda e, off=off: e.tensor_tensor(
                        AR1[:, off:off + 2048], AR1[:, off:off + 2048], AR2[:, off:off + 2048], AOP.add),
                        waits=[("S", sW)], inc="V")
                    vW = W("V")
                    upd_v[slot] = vW
                    pr.add("G", lambda e, hs=hs, off=off: e.dma_start(
                        out=xm_h[:, hs:hs + 256].rearrange("(a p) t -> p a t", p=128),
                        in_=AR1[:, off:off + 2048].rearrange("p (a t) -> p a t", a=8)),
                        waits=[("V", vW)], inc="G", amt=16)
                    upd_g[slot] = W("G")

        xnh = AR4[:, 0:8192].bitcast(BF16)
        xnl = AR4[:, 8208:16400].bitcast(BF16)
        xmG = W("G")
        for tt in range(LH // 128):
            xsl = AR1[:, (tt % 2) * 1024:(tt % 2) * 1024 + 1024]
            pr.add("S", lambda e, xsl=xsl, tt=tt: e.dma_start(out=xsl, in_=x_half[tt * 128:(tt + 1) * 128, :]),
                   waits=[("V", W("V")), ("G", xmG), ("T", W("T")), ("A", W("A"))], inc="S", amt=16)
            sW = W("S")
            ln_pass_s(xsl, tt, [("S", sW)])
        aW_ln = ln_pass_m()
        for tt in range(LH // 128):
            xsl = AR1[:, (tt % 2) * 1024:(tt % 2) * 1024 + 1024]
            pr.add("S", lambda e, xsl=xsl, tt=tt: e.dma_start(out=xsl, in_=x_half[tt * 128:(tt + 1) * 128, :]),
                   waits=[("V", W("V"))], inc="S", amt=16)
            sW = W("S")
            xnsl = AR2[:, 0:1024]
            ln_pass_n(xsl, xnsl, tt, [("S", sW), ("A", aW_ln), ("T", W("T"))])
            vW = W("V")
            for a in range(8):
                bank = pdel if a % 2 == 0 else pbc
                pr.add("T", lambda e, a=a, bank=bank, xnsl=xnsl: e.transpose(
                    bank[:, 0:128], xnsl[:, a * 128:(a + 1) * 128], eye_sb[:]),
                    waits=[("V", vW), ("A", W("A"))], inc="T")
                tW = W("T")
                pr.add("A", lambda e, a=a, tt=tt, bank=bank: e.copy(
                    xnh[:, a * LH + tt * 128:a * LH + (tt + 1) * 128], bank[:, 0:128]),
                    waits=[("T", tW)], inc="A")
                aW2 = W("A")
                pr.add("V", lambda e, a=a, tt=tt, bank=bank: e.tensor_tensor(
                    xnl[:, a * LH + tt * 128:a * LH + (tt + 1) * 128], bank[:, 0:128],
                    xnh[:, a * LH + tt * 128:a * LH + (tt + 1) * 128], AOP.subtract),
                    waits=[("A", aW2)], inc="V")

        h1hi = AR2[:, 2048:3072].bitcast(BF16)
        h1lo = AR2[:, 3072:4096].bitcast(BF16)
        for m in range(32):
            slot = m % 2
            wsl = AR5[:, slot * 4096:slot * 4096 + 1024]
            pr.add("S", lambda e, wsl=wsl, m=m: e.dma_start(
                out=wsl.rearrange("p (a q) -> p a q", a=8),
                in_=w1T[:, m * 128:(m + 1) * 128].rearrange("(a p) q -> p a q", p=128)),
                waits=[("V", W("V"))], inc="S", amt=16)
            sW = W("S")
            wqs = wq[:, 2048 + slot * 1024:2048 + slot * 1024 + 1024]
            wqb = wq[:, slot * 512:slot * 512 + 512].bitcast(BF16)
            pr.add("V", lambda e, wsl=wsl, wqs=wqs: e.tensor_scalar_mul(wqs, wsl, gq[:, 0:1]),
                   waits=[("S", sW), ("A", g1A), ("T", W("T"))], inc="V")
            pr.add("V", lambda e, wqs=wqs, wsl=wsl: e.tensor_single_scalar(wsl, wqs, 0.5, AOP.is_gt), inc="V")
            pr.add("V", lambda e, wqs=wqs: e.tensor_single_scalar(wqs, wqs, -0.5, AOP.is_lt), inc="V")
            pr.add("V", lambda e, wqs=wqs, wsl=wsl, wqb=wqb: e.tensor_sub(wqb, wsl, wqs), inc="V")
            vW = W("V")
            for cg in range(4):
                bank = pA0 if cg % 2 == 0 else pA1
                for k in range(8):
                    for hl, xsrc in ((0, xnh), (1, xnl)):
                        pr.add("T", lambda e, k=k, cg=cg, bank=bank, wqb=wqb, xsrc=xsrc: e.matmul(
                            bank[:], wqb[:, k * 128:(k + 1) * 128],
                            xsrc[:, k * LH + cg * T:k * LH + (cg + 1) * T],
                            start=(k == 0 and xsrc is xnh), stop=(k == 7 and xsrc is xnl)),
                            waits=([("V", vW), ("A", W("A"))] if (k == 0 and hl == 0) else []), inc="T")
                tW = W("T")
                pr.add("A", lambda e, cg=cg, bank=bank, m=m: e.activation(
                    AR2[:, cg * T:(cg + 1) * T], bank[:], ACT.Relu, bias=b1_sb[:, m:m + 1]),
                    waits=[("T", tW), ("G", W("G"))], inc="A")
            aW = W("A")
            for cg in range(4):
                pr.add("A", lambda e, cg=cg: e.copy(h1hi[:, cg * T:(cg + 1) * T], AR2[:, cg * T:(cg + 1) * T]),
                       inc="A")
                aW2 = W("A")
                pr.add("V", lambda e, cg=cg: e.tensor_tensor(
                    h1lo[:, cg * T:(cg + 1) * T], AR2[:, cg * T:(cg + 1) * T],
                    h1hi[:, cg * T:(cg + 1) * T], AOP.subtract), waits=[("A", aW2)], inc="V")
            vW2 = W("V")
            pr.add("G", lambda e, m=m: e.dma_start(out=h1_h[m * 128:(m + 1) * 128, 0:LH], in_=h1hi[:]),
                   waits=[("V", vW2)], inc="G", amt=16)
            pr.add("G", lambda e, m=m: e.dma_start(out=h1_h[m * 128:(m + 1) * 128, LH:2 * LH], in_=h1lo[:]),
                   inc="G", amt=16)

        for m2 in range(8):
            w2sl = AR5[:, 4096:8192]
            pr.add("S", lambda e, w2sl=w2sl, m2=m2: e.dma_start(
                out=w2sl.rearrange("p (a q) -> p a q", a=32),
                in_=w2T[:, m2 * 128:(m2 + 1) * 128].rearrange("(a p) q -> p a q", p=128)),
                waits=[("V", W("V")), ("T", W("T"))], inc="S", amt=16)
            sW = W("S")
            wqs = AR3[:, 0:4096]
            wq2b = wq[:, 0:2048].bitcast(BF16)
            pr.add("V", lambda e, w2sl=w2sl, wqs=wqs: e.tensor_scalar_mul(wqs, w2sl, gq[:, 1:2]),
                   waits=[("S", sW), ("A", g2A), ("T", W("T"))], inc="V")
            pr.add("V", lambda e, wqs=wqs, w2sl=w2sl: e.tensor_single_scalar(w2sl[:, 0:4096], wqs, 0.5, AOP.is_gt), inc="V")
            pr.add("V", lambda e, wqs=wqs: e.tensor_single_scalar(wqs, wqs, -0.5, AOP.is_lt), inc="V")
            pr.add("V", lambda e, wqs=wqs, w2sl=w2sl, wq2b=wq2b: e.tensor_sub(wq2b, w2sl[:, 0:4096], wqs), inc="V")
            vW = W("V")
            hread = {0: 0, 1: 0}
            for kf in range(32):
                hslot = kf % 2
                hhi = AR5[:, hslot * 2048:hslot * 2048 + 1024].bitcast(BF16)
                hlo = AR5[:, hslot * 2048 + 1024:hslot * 2048 + 2048].bitcast(BF16)
                pr.add("S", lambda e, hhi=hhi, kf=kf: e.dma_start(
                    out=hhi, in_=h1_h[kf * 128:(kf + 1) * 128, 0:LH]),
                    waits=[("T", hread[hslot]), ("G", W("G")), ("V", vW)], inc="S", amt=16)
                pr.add("S", lambda e, hlo=hlo, kf=kf: e.dma_start(
                    out=hlo, in_=h1_h[kf * 128:(kf + 1) * 128, LH:2 * LH]),
                    inc="S", amt=16)
                sW = W("S")
                for cg in range(4):
                    bank = [pA0, pA1, pwo0, pwo1][cg]
                    for hl, hsrc in ((0, hhi), (1, hlo)):
                        pr.add("T", lambda e, kf=kf, cg=cg, bank=bank, hsrc=hsrc, wq2b=wq2b, hl=hl: e.matmul(
                            bank[:], wq2b[:, kf * 128:(kf + 1) * 128], hsrc[:, cg * T:(cg + 1) * T],
                            start=(kf == 0 and hl == 0), stop=(kf == 31 and hl == 1)),
                            waits=([("S", sW)] + ([("V", vW), ("A", W("A"))] if (kf == 0 and hl == 0) else [])), inc="T")
                hread[hslot] = W("T")
            tW = W("T")
            for cg in range(4):
                bank = [pA0, pA1, pwo0, pwo1][cg]
                pr.add("A", lambda e, m2=m2, cg=cg, bank=bank: e.activation(
                    AR4[:, m2 * LH + cg * T:m2 * LH + (cg + 1) * T], bank[:], ACT.Silu,
                    bias=b2_sb[:, m2:m2 + 1]), waits=([("T", tW)] if cg == 0 else []), inc="A")
        gA = W("A")

        outG = []
        for tt in range(LH // 128):
            xsl = AR1[:, (tt % 2) * 1024:(tt % 2) * 1024 + 1024]
            pr.add("S", lambda e, xsl=xsl, tt=tt: e.dma_start(out=xsl, in_=x_half[tt * 128:(tt + 1) * 128, :]),
                   waits=[("V", W("V")), ("T", W("T"))], inc="S", amt=16)
            pr.add("S", lambda e, tt=tt: e.dma_start(
                out=AR2[:, 0:1024].rearrange("p (a t) -> p a t", a=8),
                in_=xm_h[:, tt * 128:(tt + 1) * 128].rearrange("(a p) t -> p a t", p=128)),
                waits=[("V", W("V")), ("G", W("G"))], inc="S", amt=16)
            pr.add("S", lambda e, tt=tt: e.dma_start(
                out=AR2[:, 1024:2048].rearrange("p (a t) -> p a t", a=8),
                in_=xm_h[:, LH + tt * 128:LH + (tt + 1) * 128].rearrange("(a p) t -> p a t", p=128)),
                inc="S", amt=16)
            sW = W("S")
            pr.add("V", lambda e: e.tensor_scalar_mul(AR2[:, 2048:3072], AR2[:, 1024:2048], m1_sb[:, 0:1]),
                   waits=[("S", sW), ("A", gA)], inc="V")
            pr.add("V", lambda e: e.scalar_tensor_tensor(
                AR2[:, 3072:4096], AR2[:, 0:1024], m0_sb[:, 0:1], AR2[:, 2048:3072],
                AOP.mult, AOP.add), inc="V")
            for a in range(8):
                pr.add("V", lambda e, a=a, tt=tt: e.tensor_tensor(
                    AR2[:, a * 128:(a + 1) * 128], AR2[:, 3072 + a * 128:3072 + (a + 1) * 128],
                    AR4[:, a * LH + tt * 128:a * LH + (tt + 1) * 128], AOP.mult), inc="V")
            vW = W("V")
            for a in range(8):
                bank = pdel if a % 2 == 0 else pbc
                pr.add("T", lambda e, a=a, bank=bank: e.transpose(
                    bank[:, 0:128], AR2[:, a * 128:(a + 1) * 128], eye_sb[:]),
                    waits=[("V", vW), ("A", W("A"))], inc="T")
                tW = W("T")
                pr.add("A", lambda e, a=a, bank=bank, tt=tt: e.copy(
                    AR3[:, 2048 + (tt % 2) * 1024 + a * 128:2048 + (tt % 2) * 1024 + (a + 1) * 128],
                    bank[:, 0:128]),
                    waits=[("T", tW)] + ([("G", outG[tt - 2])] if tt >= 2 and a == 0 else []), inc="A")
            aW = W("A")
            osl = AR3[:, 2048 + (tt % 2) * 1024:2048 + (tt % 2) * 1024 + 1024]
            pr.add("V", lambda e, osl=osl, xsl=xsl: e.tensor_tensor(osl, osl, xsl, AOP.add),
                   waits=[("A", aW)], inc="V")
            vW = W("V")
            pr.add("G", lambda e, osl=osl, tt=tt: e.dma_start(out=out_t[tt * 128:(tt + 1) * 128, :], in_=osl),
                   waits=[("V", vW)], inc="G", amt=16)
            outG.append(W("G"))

        pr.add("A", lambda e: e.copy(sc1[:, 0:1], sc1[:, 0:1]),
               waits=[("G", W("G")), ("S", W("S"))], inc="A")

        cross_waits = {s: set() for s in SEMS}
        for (eng, fn, waits, inc, amt) in pr.steps:
            for (s, v) in waits:
                cross_waits[s].add((v, eng))
        cnt_replay = dict.fromkeys(SEMS, 0)
        needs_drain = []
        for (eng, fn, waits, inc, amt) in pr.steps:
            nd = False
            if inc is not None:
                lo = cnt_replay[inc]
                cnt_replay[inc] += amt
                hi = cnt_replay[inc]
                if eng in ("V", "A"):
                    for (v, weng) in cross_waits[inc]:
                        if weng != eng and lo < v <= hi:
                            nd = True
                            break
            needs_drain.append(nd)

        def emitter(name):
            def go(E):
                for i, (eng, fn, waits, inc, amt) in enumerate(pr.steps):
                    if eng != name:
                        continue
                    for (s, v) in waits:
                        E.wait_ge(sems[s], v)
                    inst = fn(E)
                    if inc is not None:
                        inst.then_inc(sems[inc], amt)
            return go

        blk.sync(emitter("S"))
        blk.vector(emitter("V"))
        blk.scalar(emitter("A"))
        blk.tensor(emitter("T"))
        blk.gpsimd(emitter("G"))

    return nc


def kernel(L0_attn_x, L0_attn_norm_w, L0_attn_W_in, L0_attn_conv_w, L0_attn_conv_b,
           L0_attn_W_x, L0_attn_W_dt, L0_attn_b_dt, L0_attn_A_log, L0_attn_D,
           L0_attn_W_out, L0_ffn_W1, L0_ffn_b1, L0_ffn_W2, L0_ffn_b2):
    f32 = np.float32
    x = np.ascontiguousarray(np.asarray(L0_attn_x, dtype=f32))

    def tile128(a):
        dl, k = a.shape
        return np.ascontiguousarray(a.reshape(KT, 128, k).transpose(1, 0, 2).reshape(128, KT * k))

    if "prog" not in _NC_CACHE:
        _NC_CACHE["prog"] = build_nc()
    nc = _NC_CACHE["prog"]

    in_maps = []
    for c in range(8):
        b, half = c // 2, c % 2
        r0 = half * DL
        rows = slice(r0, r0 + DL)
        m = {
            "x_full": np.ascontiguousarray(x[b]),
            "x_half": np.ascontiguousarray(x[b, half * LH:(half + 1) * LH]),
            "w1T": np.ascontiguousarray(np.asarray(L0_ffn_W1, f32).T),
            "w2T": np.ascontiguousarray(np.asarray(L0_ffn_W2, f32).T),
            "b1c": np.ascontiguousarray(np.asarray(L0_ffn_b1, f32).reshape(32, 128).T),
            "b2c": np.ascontiguousarray(np.asarray(L0_ffn_b2, f32).reshape(8, 128).T),
            "ones1": np.ones((128, 128), f32),
            "onescol": np.ones((128, 1), f32),
            "eye": np.eye(128, dtype=f32),
            "m0c": np.full((128, 1), 1.0 if half == 0 else 0.0, f32),
            "m1c": np.full((128, 1), 1.0 if half == 1 else 0.0, f32),
            "cvals": np.tile(np.array([[EPS, 1.0]], f32), (128, 1)),
        }
        W_in = np.asarray(L0_attn_W_in, f32)
        W_x = np.asarray(L0_attn_W_x, f32)
        W_dt = np.asarray(L0_attn_W_dt, f32)
        W_out = np.asarray(L0_attn_W_out, f32)
        for i in range(NL):
            sel = np.concatenate([W_in[i][r0:r0 + DL], W_in[i][2048 + r0:2048 + r0 + DL]], axis=0)
            m[f"winT{i}"] = np.ascontiguousarray(sel.T)
            m[f"wxT{i}"] = np.ascontiguousarray(W_x[i][:, rows].T)
            m[f"wdtT{i}"] = np.ascontiguousarray(W_dt[i][rows].T)
            m[f"woutT{i}"] = np.ascontiguousarray(W_out[i][:, rows].T)
            m[f"convw{i}"] = tile128(np.asarray(L0_attn_conv_w, f32)[i][rows])
            m[f"convb{i}"] = tile128(np.asarray(L0_attn_conv_b, f32)[i][rows, None])
            m[f"bdt{i}"] = tile128(np.asarray(L0_attn_b_dt, f32)[i][rows, None])
            m[f"alog{i}"] = tile128(np.asarray(L0_attn_A_log, f32)[i][rows])
            m[f"dvec{i}"] = tile128(np.asarray(L0_attn_D, f32)[i][rows, None])
            m[f"normw{i}"] = np.ascontiguousarray(
                np.asarray(L0_attn_norm_w, f32)[i].reshape(8, 128).T)
        in_maps.append(m)

    res = run_bass_kernel_spmd(nc, in_maps, core_ids=list(range(8)))
    out = np.empty((4, L, DM), f32)
    for c in range(8):
        b, half = c // 2, c % 2
        out[b, half * LH:(half + 1) * LH] = res.results[c]["out"]
    return out


# version 15
# speedup vs baseline: 1.0276x; 1.0028x over previous
import numpy as np
import concourse.bass as bass
import concourse.mybir as mybir
from concourse.bass_utils import run_bass_kernel_spmd

F32 = mybir.dt.float32
BF16 = mybir.dt.bfloat16
AOP = mybir.AluOpType
ACT = mybir.ActivationFunctionType
AXX = mybir.AxisListType.X

L = 4096
LH = 2048
DM = 1024
DL = 1024
KT = 8
NST = 16
RK = 64
T = 512
NCH = L // T
SL = T + 1
CW = KT * SL
NL = 4
EPS = 1e-5

SEMS = ["S", "G", "T", "V", "A", "AD"]

_NC_CACHE = {}


class Prog:
    def __init__(self):
        self.steps = []
        self.cnt = dict.fromkeys(SEMS, 0)

    def add(self, eng, fn, waits=None, inc=None, amt=1):
        w = [(s, v) for (s, v) in (waits or []) if v > 0]
        if inc is not None:
            self.cnt[inc] += amt
        self.steps.append((eng, fn, w, inc, amt))

    def c(self, s):
        return self.cnt[s]


def build_nc(dbg=False, nl=NL):
    nc = bass.Bass("TRN2", target_bir_lowering=False, debug=False)
    KIND = "ExternalOutput" if dbg else "Internal"

    x_full = nc.dram_tensor("x_full", [L, DM], F32, kind="ExternalInput").ap()
    x_half = nc.dram_tensor("x_half", [LH, DM], F32, kind="ExternalInput").ap()
    winT, wxT, wdtT, woutT = [], [], [], []
    convw, convb, bdt, alog, dvec, normw = [], [], [], [], [], []
    for i in range(NL):
        winT.append(nc.dram_tensor(f"winT{i}", [DM, 2 * DL], F32, kind="ExternalInput").ap())
        wxT.append(nc.dram_tensor(f"wxT{i}", [DL, 96], F32, kind="ExternalInput").ap())
        wdtT.append(nc.dram_tensor(f"wdtT{i}", [RK, DL], F32, kind="ExternalInput").ap())
        woutT.append(nc.dram_tensor(f"woutT{i}", [DL, DM], F32, kind="ExternalInput").ap())
        convw.append(nc.dram_tensor(f"convw{i}", [128, 32], F32, kind="ExternalInput").ap())
        convb.append(nc.dram_tensor(f"convb{i}", [128, 8], F32, kind="ExternalInput").ap())
        bdt.append(nc.dram_tensor(f"bdt{i}", [128, 8], F32, kind="ExternalInput").ap())
        alog.append(nc.dram_tensor(f"alog{i}", [128, 128], F32, kind="ExternalInput").ap())
        dvec.append(nc.dram_tensor(f"dvec{i}", [128, 8], F32, kind="ExternalInput").ap())
        normw.append(nc.dram_tensor(f"normw{i}", [128, 8], F32, kind="ExternalInput").ap())
    w1T = nc.dram_tensor("w1T", [DM, 4096], F32, kind="ExternalInput").ap()
    w2T = nc.dram_tensor("w2T", [4096, DM], F32, kind="ExternalInput").ap()
    b1c = nc.dram_tensor("b1c", [128, 32], F32, kind="ExternalInput").ap()
    b2c = nc.dram_tensor("b2c", [128, 8], F32, kind="ExternalInput").ap()
    ones1 = nc.dram_tensor("ones1", [128, 128], F32, kind="ExternalInput").ap()
    onescol = nc.dram_tensor("onescol", [128, 1], F32, kind="ExternalInput").ap()
    eye_in = nc.dram_tensor("eye", [128, 128], F32, kind="ExternalInput").ap()
    m0_in = nc.dram_tensor("m0c", [128, 1], F32, kind="ExternalInput").ap()
    m1_in = nc.dram_tensor("m1c", [128, 1], F32, kind="ExternalInput").ap()
    cv_in = nc.dram_tensor("cvals", [128, 2], F32, kind="ExternalInput").ap()

    out_t = nc.dram_tensor("out", [LH, DM], F32, kind="ExternalOutput").ap()

    xm_h = nc.dram_tensor("xm_h", [DM, L], F32, kind=KIND).ap()
    xc_h = nc.dram_tensor("xc_h", [DL, L], F32, kind=KIND).ap()
    zs_h = nc.dram_tensor("zs_h", [DL, L], F32, kind=KIND).ap()
    h1_h = nc.dram_tensor("h1_h", [4096, 2 * LH], BF16, kind="Internal").ap()
    ccd_i = nc.dram_tensor("ccd_i", [96, L], F32, kind="Internal").ap()
    ccd_o = nc.dram_tensor("ccd_o", [96, L], F32, kind="Internal").ap()
    ccw_i = nc.dram_tensor("ccw_i", [DM, L], F32, kind="Internal").ap()
    ccw_o = nc.dram_tensor("ccw_o", [DM, L], F32, kind="Internal").ap()
    if dbg:
        dbg_d = nc.dram_tensor("dbg_d", [96, L], F32, kind="ExternalOutput").ap()
        dbg_w = nc.dram_tensor("dbg_w", [DM, L], F32, kind="ExternalOutput").ap()
        dbg_s = nc.dram_tensor("dbg_s", [128, 512], F32, kind="ExternalOutput").ap()
    RG = [[0, 1], [2, 3], [4, 5], [6, 7]]

    import contextlib
    ctx = contextlib.ExitStack()
    with ctx:
        AR1 = ctx.enter_context(nc.sbuf_tensor("AR1", [128, CW], F32))
        AR2 = ctx.enter_context(nc.sbuf_tensor("AR2", [128, 4096], F32))
        AR3 = ctx.enter_context(nc.sbuf_tensor("AR3", [128, KT * 516], F32))
        AR4 = ctx.enter_context(nc.sbuf_tensor("AR4", [128, 4 * CW], F32))
        AR5 = ctx.enter_context(nc.sbuf_tensor("AR5", [128, 8192], F32))
        AR6 = ctx.enter_context(nc.sbuf_tensor("AR6", [128, 7680], F32))
        AR7 = ctx.enter_context(nc.sbuf_tensor("AR7", [128, CW], F32))
        state = ctx.enter_context(nc.sbuf_tensor("state", [128, NST * KT], F32))
        vecs = ctx.enter_context(nc.sbuf_tensor("vecs", [128, 64], F32))
        acol = ctx.enter_context(nc.sbuf_tensor("acol", [128, 128], F32))
        atmp = ctx.enter_context(nc.sbuf_tensor("atmp", [128, 128], F32))
        wdt_sb = ctx.enter_context(nc.sbuf_tensor("wdt_sb", [64, DL], F32))
        wx_sb = ctx.enter_context(nc.sbuf_tensor("wx_sb", [128, KT * 96], F32))
        b1_sb = ctx.enter_context(nc.sbuf_tensor("b1_sb", [128, 32], F32))
        b2_sb = ctx.enter_context(nc.sbuf_tensor("b2_sb", [128, 8], F32))
        ones1_sb = ctx.enter_context(nc.sbuf_tensor("ones1_sb", [128, 128], F32))
        onesc_sb = ctx.enter_context(nc.sbuf_tensor("onesc_sb", [128, 1], F32))
        eye_sb = ctx.enter_context(nc.sbuf_tensor("eye_sb", [128, 128], F32))
        m0_sb = ctx.enter_context(nc.sbuf_tensor("m0_sb", [128, 1], F32))
        m1_sb = ctx.enter_context(nc.sbuf_tensor("m1_sb", [128, 1], F32))
        cvals = ctx.enter_context(nc.sbuf_tensor("cvals_sb", [128, 2], F32))
        nc.const_aps.register(F32, EPS, cvals[:, 0:1])
        nc.const_aps.register(F32, 1.0, cvals[:, 1:2])
        rs1 = ctx.enter_context(nc.sbuf_tensor("rs1", [1, T], F32))
        rs2 = rs1
        sc1 = ctx.enter_context(nc.sbuf_tensor("sc1", [128, 8], F32))
        gq = ctx.enter_context(nc.sbuf_tensor("gq", [128, 4], F32))

        wq = AR1

        pA0 = ctx.enter_context(nc.psum_tensor("pA0", [128, T], F32))
        pA1 = ctx.enter_context(nc.psum_tensor("pA1", [128, T], F32))
        pdbc = ctx.enter_context(nc.psum_tensor("pdbc", [128, T], F32))
        pbc = ctx.enter_context(nc.psum_tensor("pbc", [128, T], F32))
        pdel = ctx.enter_context(nc.psum_tensor("pdel", [128, T], F32))
        pwo0 = ctx.enter_context(nc.psum_tensor("pwo0", [128, T], F32))
        pwo1 = ctx.enter_context(nc.psum_tensor("pwo1", [128, T], F32))

        sems = {s: ctx.enter_context(nc.semaphore(s)) for s in SEMS}
        blk = ctx.enter_context(nc.Block())

        scarr = AR6[:, 0:768]
        bcst = AR7[:, 0:2048]

        pr = Prog()
        W = pr.c

        def ln_pass_s(xsl, tt, w0):
            pr.add("V", lambda e, xsl=xsl, tt=tt: e.tensor_reduce(
                scarr[:, tt:tt + 1], xsl, AXX, AOP.add), waits=w0, inc="V")
            pr.add("V", lambda e, xsl=xsl: e.tensor_tensor(AR2[:, 0:1024], xsl, xsl, AOP.mult),
                   waits=[("T", W("T")), ("A", W("A"))], inc="V")
            pr.add("V", lambda e, tt=tt: e.tensor_reduce(
                scarr[:, 128 + tt:129 + tt], AR2[:, 0:1024], AXX, AOP.add), inc="V")

        def ln_pass_m():
            pr.add("V", lambda e: e.tensor_scalar_mul(scarr[:, 256:384], scarr[:, 0:128], 1.0 / DM), inc="V")
            pr.add("V", lambda e: e.tensor_tensor(scarr[:, 384:512], scarr[:, 256:384],
                                                  scarr[:, 256:384], AOP.mult), inc="V")
            pr.add("V", lambda e: e.tensor_scalar_mul(scarr[:, 512:640], scarr[:, 128:256], 1.0 / DM), inc="V")
            pr.add("V", lambda e: e.tensor_sub(scarr[:, 512:640], scarr[:, 512:640],
                                               scarr[:, 384:512]), inc="V")
            vW = W("V")
            pr.add("A", lambda e: e.activation(scarr[:, 640:768], scarr[:, 512:640], ACT.Ln, bias=EPS),
                   waits=[("V", vW)], inc="A")
            pr.add("A", lambda e: e.activation(scarr[:, 640:768], scarr[:, 640:768], ACT.Exp, scale=-0.5),
                   inc="A")
            return W("A")

        def ln_pass_n(xsl, xnsl, tt, w0):
            pr.add("V", lambda e, xsl=xsl, xnsl=xnsl, tt=tt: e.tensor_scalar(
                xnsl, xsl, scarr[:, 256 + tt:257 + tt], scarr[:, 640 + tt:641 + tt],
                AOP.subtract, AOP.mult), waits=w0, inc="V")
            return W("V")

        for dst, src in [(cvals[:], cv_in), (ones1_sb[:], ones1), (onesc_sb[:], onescol),
                         (eye_sb[:], eye_in), (b1_sb[:], b1c), (b2_sb[:], b2c),
                         (m0_sb[:], m0_in), (m1_sb[:], m1_in)]:
            pr.add("S", lambda e, dst=dst, src=src: e.dma_start(out=dst, in_=src), inc="S", amt=16)

        for tt in range(L // 128):
            xsl = AR1[:, (tt % 4) * 1024:(tt % 4) * 1024 + 1024]
            pr.add("S", lambda e, xsl=xsl, tt=tt: e.dma_start(out=xsl, in_=x_full[tt * 128:(tt + 1) * 128, :]),
                   waits=[("V", W("V"))], inc="S", amt=16)
            sW = W("S")
            ln_pass_s(xsl, tt, [("S", sW)])
        aW_ln = ln_pass_m()
        for tt in range(L // 128):
            xsl = AR1[:, (tt % 4) * 1024:(tt % 4) * 1024 + 1024]
            pr.add("S", lambda e, xsl=xsl, tt=tt: e.dma_start(out=xsl, in_=x_full[tt * 128:(tt + 1) * 128, :]),
                   waits=[("V", W("V"))], inc="S", amt=16)
            sW = W("S")
            xnsl = AR2[:, 0:1024]
            ln_pass_n(xsl, xnsl, tt, [("S", sW), ("A", aW_ln), ("T", W("T"))])
            vW = W("V")
            prevG = W("G")
            for a in range(8):
                bank = pdel if a % 2 == 0 else pbc
                pr.add("T", lambda e, bank=bank, xnsl=xnsl, a=a: e.transpose(
                    bank[:, 0:128], xnsl[:, a * 128:(a + 1) * 128], eye_sb[:]),
                    waits=[("V", vW), ("A", W("A")), ("G", prevG)], inc="T")
                tW = W("T")
                pr.add("A", lambda e, a=a, bank=bank: e.copy(AR3[:, a * 128:(a + 1) * 128], bank[:, 0:128]),
                       waits=[("T", tW)], inc="A")
            aW = W("A")
            pr.add("G", lambda e, tt=tt: e.dma_start(
                out=xm_h[:, tt * 128:(tt + 1) * 128].rearrange("(a p) t -> p a t", p=128),
                in_=AR3[:, 0:1024].rearrange("p (a t) -> p a t", a=8)),
                waits=[("A", aW)], inc="G", amt=16)

        def spacer():
            pr.add("V", lambda e: e.tensor_copy(AR2[:, 1024:1536], AR2[:, 1536:2048]), inc="V")

        gam_cons = {0: W("A"), 1: W("A")}
        for gi, (wt, n_t, ldfn) in enumerate([
            (w1T, 8, lambda e, t, wsl: e.dma_start(out=wsl, in_=w1T[t * 128:(t + 1) * 128, :])),
            (w2T, 8, lambda e, t, wsl: e.dma_start(
                out=wsl.rearrange("p (a m) -> p a m", a=4),
                in_=w2T[t * 512:(t + 1) * 512, :].rearrange("(a p) m -> p a m", p=128))),
        ]):
            for t in range(n_t):
                wsl = AR5[:, (t % 2) * 4096:(t % 2) * 4096 + 4096]
                pr.add("S", lambda e, t=t, wsl=wsl, ldfn=ldfn: ldfn(e, t, wsl),
                       waits=[("A", gam_cons[t % 2])], inc="S", amt=16)
                sW = W("S")
                pr.add("A", lambda e, wsl=wsl: e.activation(AR2[:, 0:4096], wsl, ACT.Abs),
                       waits=[("S", sW), ("V", W("V"))], inc="A")
                gam_cons[t % 2] = W("A")
                aW = W("A")
                pr.add("V", lambda e, t=t, gi=gi: e.tensor_reduce(
                    atmp[:, 64 + gi * 8 + t:64 + gi * 8 + t + 1], AR2[:, 0:4096], AXX, AOP.add),
                    waits=[("A", aW)], inc="V")
            spacer()
            pr.add("V", lambda e, gi=gi: e.tensor_reduce(
                gq[:, gi:gi + 1], atmp[:, 64 + gi * 8:64 + gi * 8 + 8], AXX, AOP.add), inc="V")
            spacer()
            vW = W("V")
            pr.add("T", lambda e, gi=gi: e.matmul(pbc[0:1, 0:1], onesc_sb[:], gq[:, gi:gi + 1],
                                                  start=True, stop=True), waits=[("V", vW)], inc="T")
            tW = W("T")
            pr.add("V", lambda e: e.tensor_scalar(gq[0:1, 3:4], pbc[0:1, 0:1], 1.0 / (4096.0 * DM), EPS,
                                                  AOP.mult, AOP.add), waits=[("T", tW)], inc="V")
            spacer()
            pr.add("V", lambda e: e.reciprocal(gq[0:1, 2:3], gq[0:1, 3:4]), inc="V")
            spacer()
            vW = W("V")
            pr.add("T", lambda e: e.matmul(pbc[:, 0:1], ones1_sb[0:1, :], gq[0:1, 2:3],
                                           start=True, stop=True), waits=[("V", vW)], inc="T")
            tW = W("T")
            pr.add("A", lambda e, gi=gi: e.copy(gq[:, gi:gi + 1], pbc[:, 0:1]), waits=[("T", tW)], inc="A")
        g1A = g2A = W("A")


        for li in range(nl):
            pr.add("S", lambda e, li=li: e.dma_start(out=vecs[:, 0:32], in_=convw[li]),
                   waits=[("V", W("V")), ("A", W("A"))], inc="S", amt=16)
            pr.add("S", lambda e, li=li: e.dma_start(out=vecs[:, 32:40], in_=convb[li]), inc="S", amt=16)
            pr.add("S", lambda e, li=li: e.dma_start(out=vecs[:, 40:48], in_=bdt[li]), inc="S", amt=16)
            pr.add("S", lambda e, li=li: e.dma_start(out=vecs[:, 48:56], in_=dvec[li]), inc="S", amt=16)
            pr.add("S", lambda e, li=li: e.dma_start(out=vecs[:, 56:64], in_=normw[li]), inc="S", amt=16)
            pr.add("S", lambda e, li=li: e.dma_start(out=atmp[:], in_=alog[li]), inc="S", amt=16)
            pr.add("S", lambda e, li=li: e.dma_start(out=wdt_sb[:], in_=wdtT[li]),
                   waits=[("T", W("T"))], inc="S", amt=16)
            pr.add("S", lambda e, li=li: e.dma_start(
                out=wx_sb[:].rearrange("p (a m) -> p a m", a=KT),
                in_=wxT[li].rearrange("(a p) m -> p a m", p=128)), inc="S", amt=16)
            winh = AR4[:, 0:8192].bitcast(BF16)
            winl = AR4[:, 8208:16400].bitcast(BF16)
            wsp_v = {0: W("V"), 1: W("V")}
            for a in range(8):
                wscr = AR6[:, (a % 2) * 2048:(a % 2) * 2048 + 2048]
                pr.add("S", lambda e, li=li, a=a, wscr=wscr: e.dma_start(
                    out=wscr, in_=winT[li][a * 128:(a + 1) * 128, :]),
                    waits=[("V", wsp_v[a % 2]), ("A", W("A"))], inc="S", amt=16)
                sWp = W("S")
                pr.add("A", lambda e, a=a, wscr=wscr: e.copy(
                    winh[:, a * 2048:(a + 1) * 2048], wscr), waits=[("S", sWp)], inc="A")
                aWp = W("A")
                pr.add("V", lambda e, a=a, wscr=wscr: e.tensor_tensor(
                    winl[:, a * 2048:(a + 1) * 2048], wscr,
                    winh[:, a * 2048:(a + 1) * 2048], AOP.subtract),
                    waits=[("A", aWp)], inc="V")
                wsp_v[a % 2] = W("V")
            wsplit_v = W("V")
            wsplit_a = W("A")
            sWL = W("S")
            pr.add("A", lambda e: e.activation(acol[:], atmp[:], ACT.Exp), waits=[("S", sWL)], inc="A")
            pr.add("V", lambda e: e.tensor_scalar_mul(acol[:], acol[:], -1.0),
                   waits=[("A", W("A"))], inc="V")
            pr.add("V", lambda e: e.memset(state[:], 0.0), inc="V")
            pr.add("V", lambda e: e.memset(
                AR3[:].rearrange("p (a q) -> p a q", a=KT)[:, :, 0:3], 0.0),
                waits=[("A", W("A")), ("G", W("G"))], inc="V")

            evac_cnt = {0: 0, 1: 0}
            for c in range(NCH):
                cs = c * T
                pr.add("S", lambda e, cs=cs: e.dma_start(
                    out=AR1[:, 0:4096].rearrange("p (a t) -> p a t", a=8),
                    in_=xm_h[:, cs:cs + T].rearrange("(a p) t -> p a t", p=128)),
                    waits=[("V", W("V")), ("T", W("T")), ("G", W("G"))], inc="S", amt=16)
                sW = W("S")
                pr.add("A", lambda e: e.activation(AR2[:, 0:4096], AR1[:, 0:4096], ACT.Square),
                       waits=[("S", sW), ("T", W("T")), ("V", W("V"))], inc="A")
                aW = W("A")
                for j in range(8):
                    pr.add("T", lambda e, j=j: e.matmul(pbc[0:1, :], onesc_sb[:], AR2[:, j * T:(j + 1) * T],
                                                        start=(j == 0), stop=(j == 7)),
                           waits=([("A", aW)] if j == 0 else []), inc="T")
                tW = W("T")
                pr.add("A", lambda e: e.activation(rs1[:], pbc[0:1, :], ACT.Ln, scale=1.0 / DM, bias=EPS),
                       waits=[("T", tW)], inc="A")
                pr.add("A", lambda e: e.activation(rs2[:], rs1[:], ACT.Exp, scale=-0.5), inc="A")
                aW = W("A")
                pr.add("T", lambda e: e.matmul(pdel[:], ones1_sb[0:1, :], rs2[:], start=True, stop=True),
                       waits=[("A", aW)], inc="T")
                tW = W("T")
                for j in range(8):
                    pr.add("V", lambda e, j=j: e.scalar_tensor_tensor(
                        AR2[:, j * T:(j + 1) * T], AR1[:, j * T:(j + 1) * T], vecs[:, 56 + j:57 + j],
                        pdel[:], AOP.mult, AOP.mult), waits=([("T", tW)] if j == 0 else []), inc="V")
                hWs = W("V")
                hh = AR1[:, 0:2048].bitcast(BF16)
                hl = AR1[:, 2048:4096].bitcast(BF16)
                pr.add("A", lambda e, hh=hh: e.copy(hh, AR2[:, 0:4096]), waits=[("V", hWs)], inc="A")
                aWh = W("A")
                pr.add("V", lambda e, hh=hh, hl=hl: e.tensor_tensor(hl, AR2[:, 0:4096], hh, AOP.subtract),
                       waits=[("A", aWh)], inc="V")
                hW = W("V")
                silu_done = {}
                for m in range(16):
                    bank = pA0 if m % 2 == 0 else pA1
                    for k in range(8):
                        passes = (
                            (winh, hh), (winh, hl), (winl, hh))
                        for pi, (wsrc, hsrc) in enumerate(passes):
                            pr.add("T", lambda e, m=m, k=k, bank=bank, wsrc=wsrc, hsrc=hsrc,
                                   pi=pi: e.matmul(
                                bank[:], wsrc[:, k * 2048 + m * 128:k * 2048 + (m + 1) * 128],
                                hsrc[:, k * T:(k + 1) * T],
                                start=(k == 0 and pi == 0), stop=(k == 7 and pi == 2)),
                                waits=([("V", hW), ("A", evac_cnt[m % 2] if evac_cnt[m % 2] else wsplit_a)]
                                       if (k == 0 and pi == 0) else []), inc="T")
                    tW = W("T")
                    if m < 8:
                        base = m * 516
                        xsl = AR3[:, base + 3:base + 515]
                        w_extra = [("G", W("G"))] if (li == 0 and c == 0 and m == 0) else []
                        pr.add("A", lambda e, xsl=xsl, bank=bank: e.copy(xsl, bank[:]),
                               waits=[("T", tW), ("V", W("V"))] + w_extra, inc="A")
                        evac_cnt[m % 2] = W("A")
                        aW = W("A")
                        xcm = AR6[:, 4096 + (m % 2) * T:4096 + (m % 2) * T + T]
                        pr.add("V", lambda e, xcm=xcm, base=base, m=m: e.tensor_scalar_mul(
                            xcm, AR3[:, base:base + T], vecs[:, m * 4:m * 4 + 1]),
                            waits=[("A", aW)], inc="V")
                        for s in range(1, 4):
                            pr.add("V", lambda e, xcm=xcm, base=base, m=m, s=s: e.scalar_tensor_tensor(
                                xcm, AR3[:, base + s:base + s + T], vecs[:, m * 4 + s:m * 4 + s + 1],
                                xcm, AOP.mult, AOP.add), inc="V")
                        vW = W("V")
                        xcs = AR6[:, m * T:(m + 1) * T]
                        pr.add("A", lambda e, xcs=xcs, xcm=xcm, m=m: e.activation(
                            xcs, xcm, ACT.Silu, bias=vecs[:, 32 + m:33 + m]),
                            waits=[("V", vW), ("G", W("G")), ("T", W("T"))], inc="A")
                        silu_done[m] = W("A")
                        aW = W("A")
                        pr.add("G", lambda e, xcs=xcs, m=m, cs=cs: e.dma_start(
                            out=xc_h[m * 128:(m + 1) * 128, cs:cs + T], in_=xcs),
                            waits=[("A", aW)], inc="G", amt=16)
                        pr.add("V", lambda e, base=base: e.tensor_copy(
                            AR3[:, base:base + 3], AR3[:, base + 512:base + 515]), inc="V")
                    else:
                        mm = m - 8
                        szs = AR6[:, 4096 + (m % 2) * T:4096 + (m % 2) * T + T]
                        pr.add("A", lambda e, szs=szs, bank=bank: e.activation(szs, bank[:], ACT.Silu),
                               waits=[("T", tW), ("G", W("G"))], inc="A")
                        evac_cnt[m % 2] = W("A")
                        aW = W("A")
                        pr.add("G", lambda e, szs=szs, mm=mm, cs=cs: e.dma_start(
                            out=zs_h[mm * 128:(mm + 1) * 128, cs:cs + T], in_=szs),
                            waits=[("A", aW)], inc="G", amt=16)
                for m in range(8):
                    pr.add("T", lambda e, m=m: e.matmul(
                        pdbc[0:96, :], wx_sb[:, m * 96:(m + 1) * 96], AR6[:, m * T:(m + 1) * T],
                        start=(m == 0), stop=(m == 7)),
                        waits=([("A", silu_done[7])] if m == 0 else []), inc="T")
                tW = W("T")
                pr.add("A", lambda e, cs=cs: e.copy(AR7[0:96, cs:cs + T], pdbc[0:96, :]),
                       waits=[("T", tW)], inc="A")
            aW = W("A")
            pr.add("G", lambda e: e.dma_start(out=ccd_i, in_=AR7[0:96, 0:4096]),
                   waits=[("A", aW)], inc="G", amt=16)
            gW = W("G")
            pr.add("G", lambda e: e.collective_compute("AllReduce", AOP.add, replica_groups=RG,
                                                       ins=[ccd_i], outs=[ccd_o]),
                   waits=[("G", gW)], inc="G", amt=1)
            gW = W("G")
            pr.add("G", lambda e: e.dma_start(out=AR2[0:96, 0:4096], in_=ccd_o),
                   waits=[("G", gW), ("V", W("V")), ("T", W("T")), ("A", W("A"))], inc="G", amt=16)
            dbcG = W("G")
            pr.add("S", lambda e, li=li: e.dma_start(
                out=AR5[:].rearrange("p (a m) -> p a m", a=KT),
                in_=woutT[li].rearrange("(a p) m -> p a m", p=128)),
                waits=[("A", W("A")), ("T", W("T"))], inc="S", amt=16)
            woutS = W("S")

            U8 = AR4[:, 0 * CW:1 * CW]
            DA8 = AR4[:, 1 * CW:2 * CW]
            BX8 = AR4[:, 2 * CW:3 * CW]
            H8 = AR4[:, 3 * CW:4 * CW]
            pr.add("V", lambda e: e.memset(DA8.rearrange("p (a q) -> p a q", a=KT)[:, :, 0:1], 0.0),
                   waits=[("T", W("T"))], inc="V")
            bx_done = W("V")
            sc_done = W("V")
            ch_done = W("V")
            bc_consumed = {0: 0, 1: 0}
            cc_consumed = {0: 0, 1: 0}
            for c in range(NCH):
                cs = c * T
                for k in range(KT):
                    pr.add("T", lambda e, k=k, cs=cs: e.matmul(
                        pdel[:], wdt_sb[:, k * 128:(k + 1) * 128], AR2[0:64, cs:cs + T],
                        start=True, stop=True),
                        waits=[("G", dbcG), ("A", W("A")), ("V", W("V"))], inc="T")
                    tW = W("T")
                    tmp = AR6[:, 7168:7680]
                    pr.add("A", lambda e, tmp=tmp, k=k: e.activation(
                        tmp, pdel[:], ACT.Exp, bias=vecs[:, 40 + k:41 + k]),
                        waits=[("T", tW)], inc="A")
                    dsl = AR6[:, k * T:(k + 1) * T]
                    pr.add("A", lambda e, dsl=dsl, tmp=tmp: e.activation(dsl, tmp, ACT.Ln, bias=1.0), inc="A")
                    aW = W("A")
                    xck = AR6[:, 4096 + (k % 2) * T:4096 + (k % 2) * T + T]
                    pr.add("A", lambda e, xck=xck, k=k, cs=cs: e.dma_start(
                        out=xck, in_=xc_h[k * 128:(k + 1) * 128, cs:cs + T]),
                        waits=[("V", W("V"))], inc="AD", amt=16)
                    adW = W("AD")
                    pr.add("V", lambda e, k=k, dsl=dsl, xck=xck: e.tensor_tensor(
                        U8[:, k * SL + 1:k * SL + 1 + T], dsl, xck, AOP.mult),
                        waits=[("A", aW), ("AD", adW), ("T", W("T"))], inc="V")
                    pr.add("V", lambda e, k=k, xck=xck: e.tensor_scalar_mul(
                        AR1[:, k * SL + 1:k * SL + 1 + T], xck, vecs[:, 48 + k:49 + k]), inc="V")
                for n in range(NST):
                    bslot = bcst[:, (n % 2) * T:(n % 2) * T + T]
                    cslot = bcst[:, 1024 + (n % 2) * T:1024 + (n % 2) * T + T]
                    pr.add("S", lambda e, n=n, cs=cs, bslot=bslot: e.dma_start(
                        out=bslot, in_=ccd_o[64 + n:65 + n, cs:cs + T].to_broadcast([128, T])),
                        waits=[("V", bc_consumed[n % 2]), ("G", dbcG)], inc="S", amt=16)
                    sWb = W("S")
                    pr.add("G", lambda e, n=n, cs=cs, cslot=cslot: e.dma_start(
                        out=cslot, in_=ccd_o[80 + n:81 + n, cs:cs + T].to_broadcast([128, T])),
                        waits=[("V", cc_consumed[n % 2])], inc="G", amt=16)
                    gWc = W("G")
                    for k in range(KT):
                        pr.add("A", lambda e, k=k, n=n: e.activation(
                            DA8[:, k * SL + 1:k * SL + 1 + T], AR6[:, k * T:(k + 1) * T], ACT.Exp,
                            scale=acol[:, k * NST + n:k * NST + n + 1]),
                            waits=([("V", sc_done)] if k == 0 else []), inc="A")
                    aW = W("A")
                    pr.add("V", lambda e, n=n: e.tensor_copy(
                        BX8.rearrange("p (a q) -> p a q", a=KT)[:, :, 0],
                        state[:, n * KT:(n + 1) * KT]), waits=[("A", aW), ("S", sWb)], inc="V")
                    for k in range(KT):
                        pr.add("V", lambda e, k=k, bslot=bslot: e.tensor_tensor(
                            BX8[:, k * SL + 1:k * SL + 1 + T], U8[:, k * SL + 1:k * SL + 1 + T],
                            bslot, AOP.mult), inc="V")
                    bc_consumed[n % 2] = W("V")
                    bx_done = W("V")
                    pr.add("V", lambda e: e.tensor_tensor_scan(H8[:], DA8[:], BX8[:], 0.0, AOP.mult, AOP.add),
                           inc="V")
                    sc_done = W("V")
                    for k in range(KT):
                        pr.add("V", lambda e, k=k, cslot=cslot: e.tensor_tensor(
                            BX8[:, k * SL + 1:k * SL + 1 + T], H8[:, k * SL + 1:k * SL + 1 + T],
                            cslot, AOP.mult), waits=([("G", gWc)] if k == 0 else []), inc="V")
                    cc_consumed[n % 2] = W("V")
                    ch_done = W("V")
                    pr.add("V", lambda e: e.tensor_tensor(
                        AR1.rearrange("p (a q) -> p a q", a=KT)[:, :, 1:SL],
                        AR1.rearrange("p (a q) -> p a q", a=KT)[:, :, 1:SL],
                        BX8.rearrange("p (a q) -> p a q", a=KT)[:, :, 1:SL], AOP.add), inc="V")
                    pr.add("V", lambda e, n=n: e.tensor_copy(
                        state[:, n * KT:(n + 1) * KT],
                        H8.rearrange("p (a q) -> p a q", a=KT)[:, :, SL - 1]), inc="V")
                for k in range(KT):
                    szk = AR6[:, 5120 + (k % 2) * T:5120 + (k % 2) * T + T]
                    pr.add("S", lambda e, szk=szk, k=k, cs=cs: e.dma_start(
                        out=szk, in_=zs_h[k * 128:(k + 1) * 128, cs:cs + T]),
                        waits=[("V", W("V"))], inc="S", amt=16)
                    sW = W("S")
                    pr.add("V", lambda e, k=k, szk=szk: e.tensor_tensor(
                        AR1[:, k * SL + 1:k * SL + 1 + T], AR1[:, k * SL + 1:k * SL + 1 + T],
                        szk, AOP.mult), waits=[("S", sW)], inc="V")
                gV = W("V")
                for m2 in range(8):
                    bank = pwo0 if m2 % 2 == 0 else pwo1
                    for k in range(KT):
                        pr.add("T", lambda e, m2=m2, k=k, bank=bank: e.matmul(
                            bank[:], AR5[:, k * 1024 + m2 * 128:k * 1024 + (m2 + 1) * 128],
                            AR1[:, k * SL + 1:k * SL + 1 + T], start=(k == 0), stop=(k == KT - 1)),
                            waits=([("V", gV), ("S", woutS), ("A", W("A"))] if k == 0 else []), inc="T")
                    tW = W("T")
                    osl = AR6[:, 6144 + (m2 % 2) * T:6144 + (m2 % 2) * T + T]
                    pr.add("A", lambda e, osl=osl, bank=bank: e.copy(osl, bank[:]),
                           waits=[("T", tW), ("G", W("G"))], inc="A")
                    aW = W("A")
                    pr.add("G", lambda e, osl=osl, m2=m2, cs=cs: e.dma_start(
                        out=ccw_i[m2 * 128:(m2 + 1) * 128, cs:cs + T], in_=osl),
                        waits=[("A", aW)], inc="G", amt=16)
            gW = W("G")
            pr.add("G", lambda e: e.collective_compute("AllReduce", AOP.add, replica_groups=RG,
                                                       ins=[ccw_i], outs=[ccw_o]),
                   waits=[("G", gW)], inc="G", amt=1)
            woG = W("G")
            if dbg:
                pr.add("G", lambda e: e.dma_start(out=dbg_d, in_=ccd_o), inc="G", amt=16)
                pr.add("G", lambda e: e.dma_start(out=dbg_w, in_=ccw_o), inc="G", amt=16)
            upd_v = {0: None, 1: None}
            upd_g = {0: 0, 1: 0}
            for c in range(NCH):
                cs = c * T
                for h2 in range(2):
                    hs = cs + h2 * 256
                    slot = (c * 2 + h2) % 2
                    off = slot * 2048
                    w0 = ([("V", upd_v[slot]), ("G", upd_g[slot])] if upd_v[slot] else [
                        ("V", W("V")), ("T", W("T")), ("A", W("A")), ("G", W("G"))])
                    pr.add("S", lambda e, hs=hs, off=off: e.dma_start(
                        out=AR1[:, off:off + 2048].rearrange("p (a t) -> p a t", a=8),
                        in_=xm_h[:, hs:hs + 256].rearrange("(a p) t -> p a t", p=128)),
                        waits=w0, inc="S", amt=16)
                    pr.add("S", lambda e, hs=hs, off=off: e.dma_start(
                        out=AR2[:, off:off + 2048].rearrange("p (a t) -> p a t", a=8),
                        in_=ccw_o[:, hs:hs + 256].rearrange("(a p) t -> p a t", p=128)),
                        waits=[("G", woG)] if (c == 0 and h2 == 0) else [], inc="S", amt=16)
                    sW = W("S")
                    pr.add("V", lambda e, off=off: e.tensor_tensor(
                        AR1[:, off:off + 2048], AR1[:, off:off + 2048], AR2[:, off:off + 2048], AOP.add),
                        waits=[("S", sW)], inc="V")
                    vW = W("V")
                    upd_v[slot] = vW
                    pr.add("G", lambda e, hs=hs, off=off: e.dma_start(
                        out=xm_h[:, hs:hs + 256].rearrange("(a p) t -> p a t", p=128),
                        in_=AR1[:, off:off + 2048].rearrange("p (a t) -> p a t", a=8)),
                        waits=[("V", vW)], inc="G", amt=16)
                    upd_g[slot] = W("G")

        xnh = AR4[:, 0:8192].bitcast(BF16)
        xnl = AR4[:, 8208:16400].bitcast(BF16)
        xmG = W("G")
        for tt in range(LH // 128):
            xsl = AR1[:, (tt % 2) * 1024:(tt % 2) * 1024 + 1024]
            pr.add("S", lambda e, xsl=xsl, tt=tt: e.dma_start(out=xsl, in_=x_half[tt * 128:(tt + 1) * 128, :]),
                   waits=[("V", W("V")), ("G", xmG), ("T", W("T")), ("A", W("A"))], inc="S", amt=16)
            sW = W("S")
            ln_pass_s(xsl, tt, [("S", sW)])
        aW_ln = ln_pass_m()
        for tt in range(LH // 128):
            xsl = AR1[:, (tt % 2) * 1024:(tt % 2) * 1024 + 1024]
            pr.add("S", lambda e, xsl=xsl, tt=tt: e.dma_start(out=xsl, in_=x_half[tt * 128:(tt + 1) * 128, :]),
                   waits=[("V", W("V"))], inc="S", amt=16)
            sW = W("S")
            xnsl = AR2[:, 0:1024]
            ln_pass_n(xsl, xnsl, tt, [("S", sW), ("A", aW_ln), ("T", W("T"))])
            vW = W("V")
            for a in range(8):
                bank = pdel if a % 2 == 0 else pbc
                pr.add("T", lambda e, a=a, bank=bank, xnsl=xnsl: e.transpose(
                    bank[:, 0:128], xnsl[:, a * 128:(a + 1) * 128], eye_sb[:]),
                    waits=[("V", vW), ("A", W("A"))], inc="T")
                tW = W("T")
                pr.add("A", lambda e, a=a, tt=tt, bank=bank: e.copy(
                    xnh[:, a * LH + tt * 128:a * LH + (tt + 1) * 128], bank[:, 0:128]),
                    waits=[("T", tW)], inc="A")
                aW2 = W("A")
                pr.add("V", lambda e, a=a, tt=tt, bank=bank: e.tensor_tensor(
                    xnl[:, a * LH + tt * 128:a * LH + (tt + 1) * 128], bank[:, 0:128],
                    xnh[:, a * LH + tt * 128:a * LH + (tt + 1) * 128], AOP.subtract),
                    waits=[("A", aW2)], inc="V")

        h1hi = AR2[:, 2048:3072].bitcast(BF16)
        h1lo = AR2[:, 3072:4096].bitcast(BF16)
        for m in range(32):
            slot = m % 2
            wsl = AR5[:, slot * 4096:slot * 4096 + 1024]
            pr.add("S", lambda e, wsl=wsl, m=m: e.dma_start(
                out=wsl.rearrange("p (a q) -> p a q", a=8),
                in_=w1T[:, m * 128:(m + 1) * 128].rearrange("(a p) q -> p a q", p=128)),
                waits=[("V", W("V"))], inc="S", amt=16)
            sW = W("S")
            wqs = wq[:, 2048 + slot * 1024:2048 + slot * 1024 + 1024]
            wqb = wq[:, slot * 512:slot * 512 + 512].bitcast(BF16)
            pr.add("V", lambda e, wsl=wsl, wqs=wqs: e.tensor_scalar_mul(wqs, wsl, gq[:, 0:1]),
                   waits=[("S", sW), ("A", g1A), ("T", W("T"))], inc="V")
            pr.add("V", lambda e, wqs=wqs, wsl=wsl: e.tensor_single_scalar(wsl, wqs, 0.5, AOP.is_gt), inc="V")
            pr.add("V", lambda e, wqs=wqs: e.tensor_single_scalar(wqs, wqs, -0.5, AOP.is_lt), inc="V")
            pr.add("V", lambda e, wqs=wqs, wsl=wsl, wqb=wqb: e.tensor_sub(wqb, wsl, wqs), inc="V")
            vW = W("V")
            for cg in range(4):
                bank = pA0 if cg % 2 == 0 else pA1
                for k in range(8):
                    for hl, xsrc in ((0, xnh), (1, xnl)):
                        pr.add("T", lambda e, k=k, cg=cg, bank=bank, wqb=wqb, xsrc=xsrc: e.matmul(
                            bank[:], wqb[:, k * 128:(k + 1) * 128],
                            xsrc[:, k * LH + cg * T:k * LH + (cg + 1) * T],
                            start=(k == 0 and xsrc is xnh), stop=(k == 7 and xsrc is xnl)),
                            waits=([("V", vW), ("A", W("A"))] if (k == 0 and hl == 0) else []), inc="T")
                tW = W("T")
                pr.add("A", lambda e, cg=cg, bank=bank, m=m: e.activation(
                    AR2[:, cg * T:(cg + 1) * T], bank[:], ACT.Relu, bias=b1_sb[:, m:m + 1]),
                    waits=[("T", tW), ("G", W("G"))], inc="A")
            aW = W("A")
            for cg in range(4):
                pr.add("A", lambda e, cg=cg: e.copy(h1hi[:, cg * T:(cg + 1) * T], AR2[:, cg * T:(cg + 1) * T]),
                       inc="A")
                aW2 = W("A")
                pr.add("V", lambda e, cg=cg: e.tensor_tensor(
                    h1lo[:, cg * T:(cg + 1) * T], AR2[:, cg * T:(cg + 1) * T],
                    h1hi[:, cg * T:(cg + 1) * T], AOP.subtract), waits=[("A", aW2)], inc="V")
            vW2 = W("V")
            pr.add("G", lambda e, m=m: e.dma_start(out=h1_h[m * 128:(m + 1) * 128, 0:LH], in_=h1hi[:]),
                   waits=[("V", vW2)], inc="G", amt=16)
            pr.add("G", lambda e, m=m: e.dma_start(out=h1_h[m * 128:(m + 1) * 128, LH:2 * LH], in_=h1lo[:]),
                   inc="G", amt=16)

        for m2 in range(8):
            w2sl = AR5[:, 4096:8192]
            pr.add("S", lambda e, w2sl=w2sl, m2=m2: e.dma_start(
                out=w2sl.rearrange("p (a q) -> p a q", a=32),
                in_=w2T[:, m2 * 128:(m2 + 1) * 128].rearrange("(a p) q -> p a q", p=128)),
                waits=[("V", W("V")), ("T", W("T"))], inc="S", amt=16)
            sW = W("S")
            wqs = AR3[:, 0:4096]
            wq2b = wq[:, 0:2048].bitcast(BF16)
            pr.add("V", lambda e, w2sl=w2sl, wqs=wqs: e.tensor_scalar_mul(wqs, w2sl, gq[:, 1:2]),
                   waits=[("S", sW), ("A", g2A), ("T", W("T"))], inc="V")
            pr.add("V", lambda e, wqs=wqs, w2sl=w2sl: e.tensor_single_scalar(w2sl[:, 0:4096], wqs, 0.5, AOP.is_gt), inc="V")
            pr.add("V", lambda e, wqs=wqs: e.tensor_single_scalar(wqs, wqs, -0.5, AOP.is_lt), inc="V")
            pr.add("V", lambda e, wqs=wqs, w2sl=w2sl, wq2b=wq2b: e.tensor_sub(wq2b, w2sl[:, 0:4096], wqs), inc="V")
            vW = W("V")
            hread = {0: 0, 1: 0}
            for kf in range(32):
                hslot = kf % 2
                hhi = AR5[:, hslot * 2048:hslot * 2048 + 1024].bitcast(BF16)
                hlo = AR5[:, hslot * 2048 + 1024:hslot * 2048 + 2048].bitcast(BF16)
                pr.add("S", lambda e, hhi=hhi, kf=kf: e.dma_start(
                    out=hhi, in_=h1_h[kf * 128:(kf + 1) * 128, 0:LH]),
                    waits=[("T", hread[hslot]), ("G", W("G")), ("V", vW)], inc="S", amt=16)
                pr.add("S", lambda e, hlo=hlo, kf=kf: e.dma_start(
                    out=hlo, in_=h1_h[kf * 128:(kf + 1) * 128, LH:2 * LH]),
                    inc="S", amt=16)
                sW = W("S")
                for cg in range(4):
                    bank = [pA0, pA1, pwo0, pwo1][cg]
                    for hl, hsrc in ((0, hhi), (1, hlo)):
                        pr.add("T", lambda e, kf=kf, cg=cg, bank=bank, hsrc=hsrc, wq2b=wq2b, hl=hl: e.matmul(
                            bank[:], wq2b[:, kf * 128:(kf + 1) * 128], hsrc[:, cg * T:(cg + 1) * T],
                            start=(kf == 0 and hl == 0), stop=(kf == 31 and hl == 1)),
                            waits=([("S", sW)] + ([("V", vW), ("A", W("A"))] if (kf == 0 and hl == 0) else [])), inc="T")
                hread[hslot] = W("T")
            tW = W("T")
            for cg in range(4):
                bank = [pA0, pA1, pwo0, pwo1][cg]
                pr.add("A", lambda e, m2=m2, cg=cg, bank=bank: e.activation(
                    AR4[:, m2 * LH + cg * T:m2 * LH + (cg + 1) * T], bank[:], ACT.Silu,
                    bias=b2_sb[:, m2:m2 + 1]), waits=([("T", tW)] if cg == 0 else []), inc="A")
        gA = W("A")

        outG = []
        for tt in range(LH // 128):
            xsl = AR1[:, (tt % 2) * 1024:(tt % 2) * 1024 + 1024]
            pr.add("S", lambda e, xsl=xsl, tt=tt: e.dma_start(out=xsl, in_=x_half[tt * 128:(tt + 1) * 128, :]),
                   waits=[("V", W("V")), ("T", W("T"))], inc="S", amt=16)
            pr.add("S", lambda e, tt=tt: e.dma_start(
                out=AR2[:, 0:1024].rearrange("p (a t) -> p a t", a=8),
                in_=xm_h[:, tt * 128:(tt + 1) * 128].rearrange("(a p) t -> p a t", p=128)),
                waits=[("V", W("V")), ("G", W("G"))], inc="S", amt=16)
            pr.add("S", lambda e, tt=tt: e.dma_start(
                out=AR2[:, 1024:2048].rearrange("p (a t) -> p a t", a=8),
                in_=xm_h[:, LH + tt * 128:LH + (tt + 1) * 128].rearrange("(a p) t -> p a t", p=128)),
                inc="S", amt=16)
            sW = W("S")
            pr.add("V", lambda e: e.tensor_scalar_mul(AR2[:, 2048:3072], AR2[:, 1024:2048], m1_sb[:, 0:1]),
                   waits=[("S", sW), ("A", gA)], inc="V")
            pr.add("V", lambda e: e.scalar_tensor_tensor(
                AR2[:, 3072:4096], AR2[:, 0:1024], m0_sb[:, 0:1], AR2[:, 2048:3072],
                AOP.mult, AOP.add), inc="V")
            for a in range(8):
                pr.add("V", lambda e, a=a, tt=tt: e.tensor_tensor(
                    AR2[:, a * 128:(a + 1) * 128], AR2[:, 3072 + a * 128:3072 + (a + 1) * 128],
                    AR4[:, a * LH + tt * 128:a * LH + (tt + 1) * 128], AOP.mult), inc="V")
            vW = W("V")
            for a in range(8):
                bank = pdel if a % 2 == 0 else pbc
                pr.add("T", lambda e, a=a, bank=bank: e.transpose(
                    bank[:, 0:128], AR2[:, a * 128:(a + 1) * 128], eye_sb[:]),
                    waits=[("V", vW), ("A", W("A"))], inc="T")
                tW = W("T")
                pr.add("A", lambda e, a=a, bank=bank, tt=tt: e.copy(
                    AR3[:, 2048 + (tt % 2) * 1024 + a * 128:2048 + (tt % 2) * 1024 + (a + 1) * 128],
                    bank[:, 0:128]),
                    waits=[("T", tW)] + ([("G", outG[tt - 2])] if tt >= 2 and a == 0 else []), inc="A")
            aW = W("A")
            osl = AR3[:, 2048 + (tt % 2) * 1024:2048 + (tt % 2) * 1024 + 1024]
            pr.add("V", lambda e, osl=osl, xsl=xsl: e.tensor_tensor(osl, osl, xsl, AOP.add),
                   waits=[("A", aW)], inc="V")
            vW = W("V")
            pr.add("G", lambda e, osl=osl, tt=tt: e.dma_start(out=out_t[tt * 128:(tt + 1) * 128, :], in_=osl),
                   waits=[("V", vW)], inc="G", amt=16)
            outG.append(W("G"))

        pr.add("A", lambda e: e.copy(sc1[:, 0:1], sc1[:, 0:1]),
               waits=[("G", W("G")), ("S", W("S"))], inc="A")

        cross_waits = {s: set() for s in SEMS}
        for (eng, fn, waits, inc, amt) in pr.steps:
            for (s, v) in waits:
                cross_waits[s].add((v, eng))
        cnt_replay = dict.fromkeys(SEMS, 0)
        needs_drain = []
        for (eng, fn, waits, inc, amt) in pr.steps:
            nd = False
            if inc is not None:
                lo = cnt_replay[inc]
                cnt_replay[inc] += amt
                hi = cnt_replay[inc]
                if eng in ("V", "A"):
                    for (v, weng) in cross_waits[inc]:
                        if weng != eng and lo < v <= hi:
                            nd = True
                            break
            needs_drain.append(nd)

        def emitter(name):
            def go(E):
                for i, (eng, fn, waits, inc, amt) in enumerate(pr.steps):
                    if eng != name:
                        continue
                    for (s, v) in waits:
                        E.wait_ge(sems[s], v)
                    inst = fn(E)
                    if inc is not None:
                        inst.then_inc(sems[inc], amt)
            return go

        blk.sync(emitter("S"))
        blk.vector(emitter("V"))
        blk.scalar(emitter("A"))
        blk.tensor(emitter("T"))
        blk.gpsimd(emitter("G"))

    return nc


def kernel(L0_attn_x, L0_attn_norm_w, L0_attn_W_in, L0_attn_conv_w, L0_attn_conv_b,
           L0_attn_W_x, L0_attn_W_dt, L0_attn_b_dt, L0_attn_A_log, L0_attn_D,
           L0_attn_W_out, L0_ffn_W1, L0_ffn_b1, L0_ffn_W2, L0_ffn_b2):
    f32 = np.float32
    x = np.ascontiguousarray(np.asarray(L0_attn_x, dtype=f32))

    def tile128(a):
        dl, k = a.shape
        return np.ascontiguousarray(a.reshape(KT, 128, k).transpose(1, 0, 2).reshape(128, KT * k))

    if "prog" not in _NC_CACHE:
        _NC_CACHE["prog"] = build_nc()
    nc = _NC_CACHE["prog"]

    in_maps = []
    for c in range(8):
        b, half = c // 2, c % 2
        r0 = half * DL
        rows = slice(r0, r0 + DL)
        m = {
            "x_full": np.ascontiguousarray(x[b]),
            "x_half": np.ascontiguousarray(x[b, half * LH:(half + 1) * LH]),
            "w1T": np.ascontiguousarray(np.asarray(L0_ffn_W1, f32).T),
            "w2T": np.ascontiguousarray(np.asarray(L0_ffn_W2, f32).T),
            "b1c": np.ascontiguousarray(np.asarray(L0_ffn_b1, f32).reshape(32, 128).T),
            "b2c": np.ascontiguousarray(np.asarray(L0_ffn_b2, f32).reshape(8, 128).T),
            "ones1": np.ones((128, 128), f32),
            "onescol": np.ones((128, 1), f32),
            "eye": np.eye(128, dtype=f32),
            "m0c": np.full((128, 1), 1.0 if half == 0 else 0.0, f32),
            "m1c": np.full((128, 1), 1.0 if half == 1 else 0.0, f32),
            "cvals": np.tile(np.array([[EPS, 1.0]], f32), (128, 1)),
        }
        W_in = np.asarray(L0_attn_W_in, f32)
        W_x = np.asarray(L0_attn_W_x, f32)
        W_dt = np.asarray(L0_attn_W_dt, f32)
        W_out = np.asarray(L0_attn_W_out, f32)
        for i in range(NL):
            sel = np.concatenate([W_in[i][r0:r0 + DL], W_in[i][2048 + r0:2048 + r0 + DL]], axis=0)
            m[f"winT{i}"] = np.ascontiguousarray(sel.T)
            m[f"wxT{i}"] = np.ascontiguousarray(W_x[i][:, rows].T)
            m[f"wdtT{i}"] = np.ascontiguousarray(W_dt[i][rows].T)
            m[f"woutT{i}"] = np.ascontiguousarray(W_out[i][:, rows].T)
            m[f"convw{i}"] = tile128(np.asarray(L0_attn_conv_w, f32)[i][rows])
            m[f"convb{i}"] = tile128(np.asarray(L0_attn_conv_b, f32)[i][rows, None])
            m[f"bdt{i}"] = tile128(np.asarray(L0_attn_b_dt, f32)[i][rows, None])
            m[f"alog{i}"] = tile128(np.asarray(L0_attn_A_log, f32)[i][rows])
            m[f"dvec{i}"] = tile128(np.asarray(L0_attn_D, f32)[i][rows, None])
            m[f"normw{i}"] = np.ascontiguousarray(
                np.asarray(L0_attn_norm_w, f32)[i].reshape(8, 128).T)
        in_maps.append(m)

    res = run_bass_kernel_spmd(nc, in_maps, core_ids=list(range(8)))
    out = np.empty((4, L, DM), f32)
    for c in range(8):
        b, half = c // 2, c % 2
        out[b, half * LH:(half + 1) * LH] = res.results[c]["out"]
    return out


# version 17 (final)
# speedup vs baseline: 1.0340x; 1.0063x over previous
import numpy as np
import concourse.bass as bass
import concourse.mybir as mybir
from concourse.bass_utils import run_bass_kernel_spmd

F32 = mybir.dt.float32
BF16 = mybir.dt.bfloat16
AOP = mybir.AluOpType
ACT = mybir.ActivationFunctionType
AXX = mybir.AxisListType.X

L = 4096
LH = 2048
DM = 1024
DL = 1024
KT = 8
NST = 16
RK = 64
T = 512
NCH = L // T
SL = T + 1
CW = KT * SL
NL = 4
EPS = 1e-5

SEMS = ["S", "G", "T", "V", "A", "AD"]

_NC_CACHE = {}


class Prog:
    def __init__(self):
        self.steps = []
        self.cnt = dict.fromkeys(SEMS, 0)

    def add(self, eng, fn, waits=None, inc=None, amt=1):
        w = [(s, v) for (s, v) in (waits or []) if v > 0]
        if inc is not None:
            self.cnt[inc] += amt
        self.steps.append((eng, fn, w, inc, amt))

    def c(self, s):
        return self.cnt[s]


def build_nc(dbg=False, nl=NL):
    nc = bass.Bass("TRN2", target_bir_lowering=False, debug=False)
    KIND = "ExternalOutput" if dbg else "Internal"

    x_full = nc.dram_tensor("x_full", [L, DM], F32, kind="ExternalInput").ap()
    x_half = nc.dram_tensor("x_half", [LH, DM], F32, kind="ExternalInput").ap()
    winT, wxT, wdtT, woutT = [], [], [], []
    convw, convb, bdt, alog, dvec, normw = [], [], [], [], [], []
    for i in range(NL):
        winT.append(nc.dram_tensor(f"winT{i}", [DM, 2 * DL], F32, kind="ExternalInput").ap())
        wxT.append(nc.dram_tensor(f"wxT{i}", [DL, 96], F32, kind="ExternalInput").ap())
        wdtT.append(nc.dram_tensor(f"wdtT{i}", [RK, DL], F32, kind="ExternalInput").ap())
        woutT.append(nc.dram_tensor(f"woutT{i}", [DL, DM], F32, kind="ExternalInput").ap())
        convw.append(nc.dram_tensor(f"convw{i}", [128, 32], F32, kind="ExternalInput").ap())
        convb.append(nc.dram_tensor(f"convb{i}", [128, 8], F32, kind="ExternalInput").ap())
        bdt.append(nc.dram_tensor(f"bdt{i}", [128, 8], F32, kind="ExternalInput").ap())
        alog.append(nc.dram_tensor(f"alog{i}", [128, 128], F32, kind="ExternalInput").ap())
        dvec.append(nc.dram_tensor(f"dvec{i}", [128, 8], F32, kind="ExternalInput").ap())
        normw.append(nc.dram_tensor(f"normw{i}", [128, 8], F32, kind="ExternalInput").ap())
    w1T = nc.dram_tensor("w1T", [DM, 4096], F32, kind="ExternalInput").ap()
    w2T = nc.dram_tensor("w2T", [4096, DM], F32, kind="ExternalInput").ap()
    b1c = nc.dram_tensor("b1c", [128, 32], F32, kind="ExternalInput").ap()
    b2c = nc.dram_tensor("b2c", [128, 8], F32, kind="ExternalInput").ap()
    ones1 = nc.dram_tensor("ones1", [128, 128], F32, kind="ExternalInput").ap()
    onescol = nc.dram_tensor("onescol", [128, 1], F32, kind="ExternalInput").ap()
    eye_in = nc.dram_tensor("eye", [128, 128], F32, kind="ExternalInput").ap()
    m0_in = nc.dram_tensor("m0c", [128, 1], F32, kind="ExternalInput").ap()
    m1_in = nc.dram_tensor("m1c", [128, 1], F32, kind="ExternalInput").ap()
    cv_in = nc.dram_tensor("cvals", [128, 2], F32, kind="ExternalInput").ap()

    out_t = nc.dram_tensor("out", [LH, DM], F32, kind="ExternalOutput").ap()

    xm_h = nc.dram_tensor("xm_h", [DM, L], F32, kind=KIND).ap()
    xc_h = nc.dram_tensor("xc_h", [DL, L], F32, kind=KIND).ap()
    zs_h = nc.dram_tensor("zs_h", [DL, L], F32, kind=KIND).ap()
    h1_h = nc.dram_tensor("h1_h", [4096, 2 * LH], BF16, kind="Internal").ap()
    ccd_i = nc.dram_tensor("ccd_i", [96, L], F32, kind="Internal").ap()
    ccd_o = nc.dram_tensor("ccd_o", [96, L], F32, kind="Internal").ap()
    ccw_i = nc.dram_tensor("ccw_i", [DM, L], F32, kind="Internal").ap()
    ccw_o = nc.dram_tensor("ccw_o", [DM, L], F32, kind="Internal").ap()
    if dbg:
        dbg_d = nc.dram_tensor("dbg_d", [96, L], F32, kind="ExternalOutput").ap()
        dbg_w = nc.dram_tensor("dbg_w", [DM, L], F32, kind="ExternalOutput").ap()
        dbg_s = nc.dram_tensor("dbg_s", [128, 512], F32, kind="ExternalOutput").ap()
    RG = [[0, 1], [2, 3], [4, 5], [6, 7]]

    import contextlib
    ctx = contextlib.ExitStack()
    with ctx:
        AR1 = ctx.enter_context(nc.sbuf_tensor("AR1", [128, CW], F32))
        AR2 = ctx.enter_context(nc.sbuf_tensor("AR2", [128, 4096], F32))
        AR3 = ctx.enter_context(nc.sbuf_tensor("AR3", [128, KT * 516], F32))
        AR4 = ctx.enter_context(nc.sbuf_tensor("AR4", [128, 4 * CW], F32))
        AR5 = ctx.enter_context(nc.sbuf_tensor("AR5", [128, 8192], F32))
        AR6 = ctx.enter_context(nc.sbuf_tensor("AR6", [128, 7680], F32))
        AR7 = ctx.enter_context(nc.sbuf_tensor("AR7", [128, CW], F32))
        state = ctx.enter_context(nc.sbuf_tensor("state", [128, NST * KT], F32))
        vecs = ctx.enter_context(nc.sbuf_tensor("vecs", [128, 64], F32))
        acol = ctx.enter_context(nc.sbuf_tensor("acol", [128, 128], F32))
        atmp = ctx.enter_context(nc.sbuf_tensor("atmp", [128, 128], F32))
        wdt_sb = ctx.enter_context(nc.sbuf_tensor("wdt_sb", [64, DL], F32))
        wx_sb = ctx.enter_context(nc.sbuf_tensor("wx_sb", [128, KT * 96], F32))
        b1_sb = ctx.enter_context(nc.sbuf_tensor("b1_sb", [128, 32], F32))
        b2_sb = ctx.enter_context(nc.sbuf_tensor("b2_sb", [128, 8], F32))
        ones1_sb = ctx.enter_context(nc.sbuf_tensor("ones1_sb", [128, 128], F32))
        onesc_sb = ctx.enter_context(nc.sbuf_tensor("onesc_sb", [128, 1], F32))
        eye_sb = ctx.enter_context(nc.sbuf_tensor("eye_sb", [128, 128], F32))
        m0_sb = ctx.enter_context(nc.sbuf_tensor("m0_sb", [128, 1], F32))
        m1_sb = ctx.enter_context(nc.sbuf_tensor("m1_sb", [128, 1], F32))
        cvals = ctx.enter_context(nc.sbuf_tensor("cvals_sb", [128, 2], F32))
        nc.const_aps.register(F32, EPS, cvals[:, 0:1])
        nc.const_aps.register(F32, 1.0, cvals[:, 1:2])
        rs1 = ctx.enter_context(nc.sbuf_tensor("rs1", [1, T], F32))
        rs2 = rs1
        sc1 = ctx.enter_context(nc.sbuf_tensor("sc1", [128, 8], F32))
        gq = ctx.enter_context(nc.sbuf_tensor("gq", [128, 4], F32))

        wq = AR1

        pA0 = ctx.enter_context(nc.psum_tensor("pA0", [128, T], F32))
        pA1 = ctx.enter_context(nc.psum_tensor("pA1", [128, T], F32))
        pdbc = ctx.enter_context(nc.psum_tensor("pdbc", [128, T], F32))
        pbc = ctx.enter_context(nc.psum_tensor("pbc", [128, T], F32))
        pdel = ctx.enter_context(nc.psum_tensor("pdel", [128, T], F32))
        pwo0 = ctx.enter_context(nc.psum_tensor("pwo0", [128, T], F32))
        pwo1 = ctx.enter_context(nc.psum_tensor("pwo1", [128, T], F32))

        sems = {s: ctx.enter_context(nc.semaphore(s)) for s in SEMS}
        blk = ctx.enter_context(nc.Block())

        scarr = AR6[:, 0:768]
        bcst = AR7[:, 0:2048]

        pr = Prog()
        W = pr.c

        def ln_pass_s(xsl, tt, w0):
            pr.add("V", lambda e, xsl=xsl, tt=tt: e.tensor_reduce(
                scarr[:, tt:tt + 1], xsl, AXX, AOP.add), waits=w0, inc="V")
            pr.add("V", lambda e, xsl=xsl: e.tensor_tensor(AR2[:, 0:1024], xsl, xsl, AOP.mult),
                   waits=[("T", W("T")), ("A", W("A"))], inc="V")
            pr.add("V", lambda e, tt=tt: e.tensor_reduce(
                scarr[:, 128 + tt:129 + tt], AR2[:, 0:1024], AXX, AOP.add), inc="V")

        def ln_pass_m():
            pr.add("V", lambda e: e.tensor_scalar_mul(scarr[:, 256:384], scarr[:, 0:128], 1.0 / DM), inc="V")
            pr.add("V", lambda e: e.tensor_tensor(scarr[:, 384:512], scarr[:, 256:384],
                                                  scarr[:, 256:384], AOP.mult), inc="V")
            pr.add("V", lambda e: e.tensor_scalar_mul(scarr[:, 512:640], scarr[:, 128:256], 1.0 / DM), inc="V")
            pr.add("V", lambda e: e.tensor_sub(scarr[:, 512:640], scarr[:, 512:640],
                                               scarr[:, 384:512]), inc="V")
            vW = W("V")
            pr.add("A", lambda e: e.activation(scarr[:, 640:768], scarr[:, 512:640], ACT.Ln, bias=EPS),
                   waits=[("V", vW)], inc="A")
            pr.add("A", lambda e: e.activation(scarr[:, 640:768], scarr[:, 640:768], ACT.Exp, scale=-0.5),
                   inc="A")
            return W("A")

        def ln_pass_n(xsl, xnsl, tt, w0):
            pr.add("V", lambda e, xsl=xsl, xnsl=xnsl, tt=tt: e.tensor_scalar(
                xnsl, xsl, scarr[:, 256 + tt:257 + tt], scarr[:, 640 + tt:641 + tt],
                AOP.subtract, AOP.mult), waits=w0, inc="V")
            return W("V")

        for dst, src in [(cvals[:], cv_in), (ones1_sb[:], ones1), (onesc_sb[:], onescol),
                         (eye_sb[:], eye_in), (b1_sb[:], b1c), (b2_sb[:], b2c),
                         (m0_sb[:], m0_in), (m1_sb[:], m1_in)]:
            pr.add("S", lambda e, dst=dst, src=src: e.dma_start(out=dst, in_=src), inc="S", amt=16)

        for tt in range(L // 128):
            xsl = AR1[:, (tt % 4) * 1024:(tt % 4) * 1024 + 1024]
            pr.add("S", lambda e, xsl=xsl, tt=tt: e.dma_start(out=xsl, in_=x_full[tt * 128:(tt + 1) * 128, :]),
                   waits=[("V", W("V"))], inc="S", amt=16)
            sW = W("S")
            ln_pass_s(xsl, tt, [("S", sW)])
        aW_ln = ln_pass_m()
        for tt in range(L // 128):
            xsl = AR1[:, (tt % 4) * 1024:(tt % 4) * 1024 + 1024]
            pr.add("S", lambda e, xsl=xsl, tt=tt: e.dma_start(out=xsl, in_=x_full[tt * 128:(tt + 1) * 128, :]),
                   waits=[("V", W("V"))], inc="S", amt=16)
            sW = W("S")
            xnsl = AR2[:, 0:1024]
            ln_pass_n(xsl, xnsl, tt, [("S", sW), ("A", aW_ln), ("T", W("T"))])
            vW = W("V")
            prevG = W("G")
            for a in range(8):
                bank = pdel if a % 2 == 0 else pbc
                pr.add("T", lambda e, bank=bank, xnsl=xnsl, a=a: e.transpose(
                    bank[:, 0:128], xnsl[:, a * 128:(a + 1) * 128], eye_sb[:]),
                    waits=[("V", vW), ("A", W("A")), ("G", prevG)], inc="T")
                tW = W("T")
                pr.add("A", lambda e, a=a, bank=bank: e.copy(AR3[:, a * 128:(a + 1) * 128], bank[:, 0:128]),
                       waits=[("T", tW)], inc="A")
            aW = W("A")
            pr.add("G", lambda e, tt=tt: e.dma_start(
                out=xm_h[:, tt * 128:(tt + 1) * 128].rearrange("(a p) t -> p a t", p=128),
                in_=AR3[:, 0:1024].rearrange("p (a t) -> p a t", a=8)),
                waits=[("A", aW)], inc="G", amt=16)

        def spacer():
            pr.add("V", lambda e: e.tensor_copy(AR2[:, 1024:1536], AR2[:, 1536:2048]), inc="V")

        gam_cons = {0: W("A"), 1: W("A")}
        for gi, (wt, n_t, ldfn) in enumerate([
            (w1T, 8, lambda e, t, wsl: e.dma_start(out=wsl, in_=w1T[t * 128:(t + 1) * 128, :])),
            (w2T, 8, lambda e, t, wsl: e.dma_start(
                out=wsl.rearrange("p (a m) -> p a m", a=4),
                in_=w2T[t * 512:(t + 1) * 512, :].rearrange("(a p) m -> p a m", p=128))),
        ]):
            for t in range(n_t):
                wsl = AR5[:, (t % 2) * 4096:(t % 2) * 4096 + 4096]
                pr.add("S", lambda e, t=t, wsl=wsl, ldfn=ldfn: ldfn(e, t, wsl),
                       waits=[("A", gam_cons[t % 2])], inc="S", amt=16)
                sW = W("S")
                pr.add("A", lambda e, wsl=wsl: e.activation(AR2[:, 0:4096], wsl, ACT.Abs),
                       waits=[("S", sW), ("V", W("V"))], inc="A")
                gam_cons[t % 2] = W("A")
                aW = W("A")
                pr.add("V", lambda e, t=t, gi=gi: e.tensor_reduce(
                    atmp[:, 64 + gi * 8 + t:64 + gi * 8 + t + 1], AR2[:, 0:4096], AXX, AOP.add),
                    waits=[("A", aW)], inc="V")
            spacer()
            pr.add("V", lambda e, gi=gi: e.tensor_reduce(
                gq[:, gi:gi + 1], atmp[:, 64 + gi * 8:64 + gi * 8 + 8], AXX, AOP.add), inc="V")
            spacer()
            vW = W("V")
            pr.add("T", lambda e, gi=gi: e.matmul(pbc[0:1, 0:1], onesc_sb[:], gq[:, gi:gi + 1],
                                                  start=True, stop=True), waits=[("V", vW)], inc="T")
            tW = W("T")
            pr.add("V", lambda e: e.tensor_scalar(gq[0:1, 3:4], pbc[0:1, 0:1], 1.0 / (4096.0 * DM), EPS,
                                                  AOP.mult, AOP.add), waits=[("T", tW)], inc="V")
            spacer()
            pr.add("V", lambda e: e.reciprocal(gq[0:1, 2:3], gq[0:1, 3:4]), inc="V")
            spacer()
            vW = W("V")
            pr.add("T", lambda e: e.matmul(pbc[:, 0:1], ones1_sb[0:1, :], gq[0:1, 2:3],
                                           start=True, stop=True), waits=[("V", vW)], inc="T")
            tW = W("T")
            pr.add("A", lambda e, gi=gi: e.copy(gq[:, gi:gi + 1], pbc[:, 0:1]), waits=[("T", tW)], inc="A")
        g1A = g2A = W("A")


        for li in range(nl):
            pr.add("S", lambda e, li=li: e.dma_start(out=vecs[:, 0:32], in_=convw[li]),
                   waits=[("V", W("V")), ("A", W("A"))], inc="S", amt=16)
            pr.add("S", lambda e, li=li: e.dma_start(out=vecs[:, 32:40], in_=convb[li]), inc="S", amt=16)
            pr.add("S", lambda e, li=li: e.dma_start(out=vecs[:, 40:48], in_=bdt[li]), inc="S", amt=16)
            pr.add("S", lambda e, li=li: e.dma_start(out=vecs[:, 48:56], in_=dvec[li]), inc="S", amt=16)
            pr.add("S", lambda e, li=li: e.dma_start(out=vecs[:, 56:64], in_=normw[li]), inc="S", amt=16)
            pr.add("S", lambda e, li=li: e.dma_start(out=atmp[:], in_=alog[li]), inc="S", amt=16)
            pr.add("S", lambda e, li=li: e.dma_start(out=wdt_sb[:], in_=wdtT[li]),
                   waits=[("T", W("T"))], inc="S", amt=16)
            pr.add("S", lambda e, li=li: e.dma_start(
                out=wx_sb[:].rearrange("p (a m) -> p a m", a=KT),
                in_=wxT[li].rearrange("(a p) m -> p a m", p=128)), inc="S", amt=16)
            winh = AR4[:, 0:8192].bitcast(BF16)
            winl = AR4[:, 8208:16400].bitcast(BF16)
            wsp_v = {0: W("V"), 1: W("V")}
            for a in range(8):
                wscr = AR6[:, (a % 2) * 2048:(a % 2) * 2048 + 2048]
                pr.add("S", lambda e, li=li, a=a, wscr=wscr: e.dma_start(
                    out=wscr, in_=winT[li][a * 128:(a + 1) * 128, :]),
                    waits=[("V", wsp_v[a % 2]), ("A", W("A"))], inc="S", amt=16)
                sWp = W("S")
                pr.add("A", lambda e, a=a, wscr=wscr: e.copy(
                    winh[:, a * 2048:(a + 1) * 2048], wscr), waits=[("S", sWp)], inc="A")
                aWp = W("A")
                pr.add("V", lambda e, a=a, wscr=wscr: e.tensor_tensor(
                    winl[:, a * 2048:(a + 1) * 2048], wscr,
                    winh[:, a * 2048:(a + 1) * 2048], AOP.subtract),
                    waits=[("A", aWp)], inc="V")
                wsp_v[a % 2] = W("V")
            wsplit_v = W("V")
            wsplit_a = W("A")
            sWL = W("S")
            pr.add("A", lambda e: e.activation(acol[:], atmp[:], ACT.Exp), waits=[("S", sWL)], inc="A")
            pr.add("V", lambda e: e.tensor_scalar_mul(acol[:], acol[:], -1.0),
                   waits=[("A", W("A"))], inc="V")
            pr.add("V", lambda e: e.memset(state[:], 0.0), inc="V")
            pr.add("V", lambda e: e.memset(
                AR3[:].rearrange("p (a q) -> p a q", a=KT)[:, :, 0:3], 0.0),
                waits=[("A", W("A")), ("G", W("G"))], inc="V")

            evac_cnt = {0: 0, 1: 0}
            for c in range(NCH):
                cs = c * T
                pr.add("S", lambda e, cs=cs: e.dma_start(
                    out=AR1[:, 0:4096].rearrange("p (a t) -> p a t", a=8),
                    in_=xm_h[:, cs:cs + T].rearrange("(a p) t -> p a t", p=128)),
                    waits=[("V", W("V")), ("T", W("T")), ("G", W("G"))], inc="S", amt=16)
                sW = W("S")
                pr.add("A", lambda e: e.activation(AR2[:, 0:4096], AR1[:, 0:4096], ACT.Square),
                       waits=[("S", sW), ("T", W("T")), ("V", W("V"))], inc="A")
                aW = W("A")
                for j in range(8):
                    pr.add("T", lambda e, j=j: e.matmul(pbc[0:1, :], onesc_sb[:], AR2[:, j * T:(j + 1) * T],
                                                        start=(j == 0), stop=(j == 7)),
                           waits=([("A", aW)] if j == 0 else []), inc="T")
                tW = W("T")
                pr.add("A", lambda e: e.activation(rs1[:], pbc[0:1, :], ACT.Ln, scale=1.0 / DM, bias=EPS),
                       waits=[("T", tW)], inc="A")
                pr.add("A", lambda e: e.activation(rs2[:], rs1[:], ACT.Exp, scale=-0.5), inc="A")
                aW = W("A")
                pr.add("T", lambda e: e.matmul(pdel[:], ones1_sb[0:1, :], rs2[:], start=True, stop=True),
                       waits=[("A", aW)], inc="T")
                tW = W("T")
                for j in range(8):
                    pr.add("V", lambda e, j=j: e.scalar_tensor_tensor(
                        AR2[:, j * T:(j + 1) * T], AR1[:, j * T:(j + 1) * T], vecs[:, 56 + j:57 + j],
                        pdel[:], AOP.mult, AOP.mult), waits=([("T", tW)] if j == 0 else []), inc="V")
                hWs = W("V")
                hh = AR1[:, 0:2048].bitcast(BF16)
                hl = AR1[:, 2048:4096].bitcast(BF16)
                pr.add("A", lambda e, hh=hh: e.copy(hh, AR2[:, 0:4096]), waits=[("V", hWs)], inc="A")
                aWh = W("A")
                pr.add("V", lambda e, hh=hh, hl=hl: e.tensor_tensor(hl, AR2[:, 0:4096], hh, AOP.subtract),
                       waits=[("A", aWh)], inc="V")
                hW = W("V")
                silu_done = {}
                for m in range(16):
                    bank = pA0 if m % 2 == 0 else pA1
                    for k in range(8):
                        passes = (
                            (winh, hh), (winh, hl), (winl, hh))
                        for pi, (wsrc, hsrc) in enumerate(passes):
                            pr.add("T", lambda e, m=m, k=k, bank=bank, wsrc=wsrc, hsrc=hsrc,
                                   pi=pi: e.matmul(
                                bank[:], wsrc[:, k * 2048 + m * 128:k * 2048 + (m + 1) * 128],
                                hsrc[:, k * T:(k + 1) * T],
                                start=(k == 0 and pi == 0), stop=(k == 7 and pi == 2)),
                                waits=([("V", hW), ("A", evac_cnt[m % 2] if evac_cnt[m % 2] else wsplit_a)]
                                       if (k == 0 and pi == 0) else []), inc="T")
                    tW = W("T")
                    if m < 8:
                        base = m * 516
                        xsl = AR3[:, base + 3:base + 515]
                        w_extra = [("G", W("G"))] if (li == 0 and c == 0 and m == 0) else []
                        pr.add("A", lambda e, xsl=xsl, bank=bank: e.copy(xsl, bank[:]),
                               waits=[("T", tW), ("V", W("V"))] + w_extra, inc="A")
                        evac_cnt[m % 2] = W("A")
                        aW = W("A")
                        xcm = AR6[:, 4096 + (m % 2) * T:4096 + (m % 2) * T + T]
                        pr.add("V", lambda e, xcm=xcm, base=base, m=m: e.tensor_scalar_mul(
                            xcm, AR3[:, base:base + T], vecs[:, m * 4:m * 4 + 1]),
                            waits=[("A", aW)], inc="V")
                        for s in range(1, 4):
                            pr.add("V", lambda e, xcm=xcm, base=base, m=m, s=s: e.scalar_tensor_tensor(
                                xcm, AR3[:, base + s:base + s + T], vecs[:, m * 4 + s:m * 4 + s + 1],
                                xcm, AOP.mult, AOP.add), inc="V")
                        vW = W("V")
                        xcs = AR6[:, m * T:(m + 1) * T]
                        pr.add("A", lambda e, xcs=xcs, xcm=xcm, m=m: e.activation(
                            xcs, xcm, ACT.Silu, bias=vecs[:, 32 + m:33 + m]),
                            waits=[("V", vW), ("G", W("G")), ("T", W("T"))], inc="A")
                        silu_done[m] = W("A")
                        aW = W("A")
                        pr.add("G", lambda e, xcs=xcs, m=m, cs=cs: e.dma_start(
                            out=xc_h[m * 128:(m + 1) * 128, cs:cs + T], in_=xcs),
                            waits=[("A", aW)], inc="G", amt=16)
                        pr.add("V", lambda e, base=base: e.tensor_copy(
                            AR3[:, base:base + 3], AR3[:, base + 512:base + 515]), inc="V")
                    else:
                        mm = m - 8
                        szs = AR6[:, 4096 + (m % 2) * T:4096 + (m % 2) * T + T]
                        pr.add("A", lambda e, szs=szs, bank=bank: e.activation(szs, bank[:], ACT.Silu),
                               waits=[("T", tW), ("G", W("G"))], inc="A")
                        evac_cnt[m % 2] = W("A")
                        aW = W("A")
                        pr.add("G", lambda e, szs=szs, mm=mm, cs=cs: e.dma_start(
                            out=zs_h[mm * 128:(mm + 1) * 128, cs:cs + T], in_=szs),
                            waits=[("A", aW)], inc="G", amt=16)
                for m in range(8):
                    pr.add("T", lambda e, m=m: e.matmul(
                        pdbc[0:96, :], wx_sb[:, m * 96:(m + 1) * 96], AR6[:, m * T:(m + 1) * T],
                        start=(m == 0), stop=(m == 7)),
                        waits=([("A", silu_done[7])] if m == 0 else []), inc="T")
                tW = W("T")
                pr.add("A", lambda e, cs=cs: e.copy(AR7[0:96, cs:cs + T], pdbc[0:96, :]),
                       waits=[("T", tW)], inc="A")
            aW = W("A")
            pr.add("G", lambda e: e.dma_start(out=ccd_i, in_=AR7[0:96, 0:4096]),
                   waits=[("A", aW)], inc="G", amt=16)
            gW = W("G")
            pr.add("G", lambda e: e.collective_compute("AllReduce", AOP.add, replica_groups=RG,
                                                       ins=[ccd_i], outs=[ccd_o]),
                   waits=[("G", gW)], inc="G", amt=1)
            gW = W("G")
            pr.add("G", lambda e: e.dma_start(out=AR2[0:96, 0:4096], in_=ccd_o),
                   waits=[("G", gW), ("V", W("V")), ("T", W("T")), ("A", W("A"))], inc="G", amt=16)
            dbcG = W("G")
            wouth = AR5[:, 0:4096].bitcast(BF16)
            woutl = AR5[:, 4096:8192].bitcast(BF16)
            wosp = {0: W("V"), 1: W("V")}
            for a in range(8):
                wscr2 = AR6[:, (a % 2) * 1024:(a % 2) * 1024 + 1024]
                pr.add("S", lambda e, li=li, a=a, wscr2=wscr2: e.dma_start(
                    out=wscr2, in_=woutT[li][a * 128:(a + 1) * 128, :]),
                    waits=[("A", W("A")), ("T", W("T")), ("V", wosp[a % 2])], inc="S", amt=16)
                sWp = W("S")
                pr.add("A", lambda e, a=a, wscr2=wscr2: e.copy(
                    wouth[:, a * 1024:(a + 1) * 1024], wscr2), waits=[("S", sWp)], inc="A")
                aWp = W("A")
                pr.add("V", lambda e, a=a, wscr2=wscr2: e.tensor_tensor(
                    woutl[:, a * 1024:(a + 1) * 1024], wscr2,
                    wouth[:, a * 1024:(a + 1) * 1024], AOP.subtract),
                    waits=[("A", aWp)], inc="V")
                wosp[a % 2] = W("V")
            woutS = W("S")
            wout_split_v = W("V")

            U8 = AR4[:, 0 * CW:1 * CW]
            DA8 = AR4[:, 1 * CW:2 * CW]
            BX8 = AR4[:, 2 * CW:3 * CW]
            H8 = AR4[:, 3 * CW:4 * CW]
            pr.add("V", lambda e: e.memset(DA8.rearrange("p (a q) -> p a q", a=KT)[:, :, 0:1], 0.0),
                   waits=[("T", W("T"))], inc="V")
            bx_done = W("V")
            sc_done = W("V")
            ch_done = W("V")
            bc_consumed = {0: 0, 1: 0}
            cc_consumed = {0: 0, 1: 0}
            for c in range(NCH):
                cs = c * T
                for k in range(KT):
                    pr.add("T", lambda e, k=k, cs=cs: e.matmul(
                        pdel[:], wdt_sb[:, k * 128:(k + 1) * 128], AR2[0:64, cs:cs + T],
                        start=True, stop=True),
                        waits=[("G", dbcG), ("A", W("A")), ("V", W("V"))], inc="T")
                    tW = W("T")
                    tmp = AR6[:, 7168:7680]
                    pr.add("A", lambda e, tmp=tmp, k=k: e.activation(
                        tmp, pdel[:], ACT.Exp, bias=vecs[:, 40 + k:41 + k]),
                        waits=[("T", tW)], inc="A")
                    dsl = AR6[:, k * T:(k + 1) * T]
                    pr.add("A", lambda e, dsl=dsl, tmp=tmp: e.activation(dsl, tmp, ACT.Ln, bias=1.0), inc="A")
                    aW = W("A")
                    xck = AR6[:, 4096 + (k % 2) * T:4096 + (k % 2) * T + T]
                    pr.add("A", lambda e, xck=xck, k=k, cs=cs: e.dma_start(
                        out=xck, in_=xc_h[k * 128:(k + 1) * 128, cs:cs + T]),
                        waits=[("V", W("V"))], inc="AD", amt=16)
                    adW = W("AD")
                    pr.add("V", lambda e, k=k, dsl=dsl, xck=xck: e.tensor_tensor(
                        U8[:, k * SL + 1:k * SL + 1 + T], dsl, xck, AOP.mult),
                        waits=[("A", aW), ("AD", adW), ("T", W("T"))], inc="V")
                    pr.add("V", lambda e, k=k, xck=xck: e.tensor_scalar_mul(
                        AR1[:, k * SL + 1:k * SL + 1 + T], xck, vecs[:, 48 + k:49 + k]), inc="V")
                for n in range(NST):
                    bslot = bcst[:, (n % 2) * T:(n % 2) * T + T]
                    cslot = bcst[:, 1024 + (n % 2) * T:1024 + (n % 2) * T + T]
                    pr.add("S", lambda e, n=n, cs=cs, bslot=bslot: e.dma_start(
                        out=bslot, in_=ccd_o[64 + n:65 + n, cs:cs + T].to_broadcast([128, T])),
                        waits=[("V", bc_consumed[n % 2]), ("G", dbcG)], inc="S", amt=16)
                    sWb = W("S")
                    pr.add("G", lambda e, n=n, cs=cs, cslot=cslot: e.dma_start(
                        out=cslot, in_=ccd_o[80 + n:81 + n, cs:cs + T].to_broadcast([128, T])),
                        waits=[("V", cc_consumed[n % 2])], inc="G", amt=16)
                    gWc = W("G")
                    for k in range(KT):
                        pr.add("A", lambda e, k=k, n=n: e.activation(
                            DA8[:, k * SL + 1:k * SL + 1 + T], AR6[:, k * T:(k + 1) * T], ACT.Exp,
                            scale=acol[:, k * NST + n:k * NST + n + 1]),
                            waits=([("V", sc_done)] if k == 0 else []), inc="A")
                    aW = W("A")
                    pr.add("V", lambda e, n=n: e.tensor_copy(
                        BX8.rearrange("p (a q) -> p a q", a=KT)[:, :, 0],
                        state[:, n * KT:(n + 1) * KT]), waits=[("A", aW), ("S", sWb)], inc="V")
                    for k in range(KT):
                        pr.add("V", lambda e, k=k, bslot=bslot: e.tensor_tensor(
                            BX8[:, k * SL + 1:k * SL + 1 + T], U8[:, k * SL + 1:k * SL + 1 + T],
                            bslot, AOP.mult), inc="V")
                    bc_consumed[n % 2] = W("V")
                    bx_done = W("V")
                    pr.add("V", lambda e: e.tensor_tensor_scan(H8[:], DA8[:], BX8[:], 0.0, AOP.mult, AOP.add),
                           inc="V")
                    sc_done = W("V")
                    for k in range(KT):
                        pr.add("V", lambda e, k=k, cslot=cslot: e.tensor_tensor(
                            BX8[:, k * SL + 1:k * SL + 1 + T], H8[:, k * SL + 1:k * SL + 1 + T],
                            cslot, AOP.mult), waits=([("G", gWc)] if k == 0 else []), inc="V")
                    cc_consumed[n % 2] = W("V")
                    ch_done = W("V")
                    pr.add("V", lambda e: e.tensor_tensor(
                        AR1.rearrange("p (a q) -> p a q", a=KT)[:, :, 1:SL],
                        AR1.rearrange("p (a q) -> p a q", a=KT)[:, :, 1:SL],
                        BX8.rearrange("p (a q) -> p a q", a=KT)[:, :, 1:SL], AOP.add), inc="V")
                    pr.add("V", lambda e, n=n: e.tensor_copy(
                        state[:, n * KT:(n + 1) * KT],
                        H8.rearrange("p (a q) -> p a q", a=KT)[:, :, SL - 1]), inc="V")
                for k in range(KT):
                    szk = AR6[:, 5120 + (k % 2) * T:5120 + (k % 2) * T + T]
                    pr.add("S", lambda e, szk=szk, k=k, cs=cs: e.dma_start(
                        out=szk, in_=zs_h[k * 128:(k + 1) * 128, cs:cs + T]),
                        waits=[("V", W("V"))], inc="S", amt=16)
                    sW = W("S")
                    pr.add("V", lambda e, k=k, szk=szk: e.tensor_tensor(
                        AR1[:, k * SL + 1:k * SL + 1 + T], AR1[:, k * SL + 1:k * SL + 1 + T],
                        szk, AOP.mult), waits=[("S", sW)], inc="V")
                gV = W("V")
                gh = AR3[:, 0:2048].bitcast(BF16)
                gl = AR3[:, 2048:4096].bitcast(BF16)
                for k in range(KT):
                    pr.add("A", lambda e, k=k, gh=gh: e.copy(
                        gh[:, k * T:(k + 1) * T], AR1[:, k * SL + 1:k * SL + 1 + T]),
                        waits=([("V", gV)] if k == 0 else []), inc="A")
                gaW = W("A")
                for k in range(KT):
                    pr.add("V", lambda e, k=k, gh=gh, gl=gl: e.tensor_tensor(
                        gl[:, k * T:(k + 1) * T], AR1[:, k * SL + 1:k * SL + 1 + T],
                        gh[:, k * T:(k + 1) * T], AOP.subtract),
                        waits=([("A", gaW)] if k == 0 else []), inc="V")
                gVs = W("V")
                for m2 in range(8):
                    bank = pwo0 if m2 % 2 == 0 else pwo1
                    for pi, (wsrc, gsrc) in enumerate(((wouth, gh), (wouth, gl), (woutl, gh))):
                        for k in range(KT):
                            pr.add("T", lambda e, m2=m2, k=k, bank=bank, wsrc=wsrc, gsrc=gsrc,
                                   pi=pi: e.matmul(
                                bank[:], wsrc[:, k * 1024 + m2 * 128:k * 1024 + (m2 + 1) * 128],
                                gsrc[:, k * T:(k + 1) * T],
                                start=(k == 0 and pi == 0), stop=(k == KT - 1 and pi == 2)),
                                waits=([("V", gVs), ("S", woutS), ("A", W("A"))]
                                       if (k == 0 and pi == 0) else []), inc="T")
                    tW = W("T")
                    osl = AR6[:, 6144 + (m2 % 2) * T:6144 + (m2 % 2) * T + T]
                    pr.add("A", lambda e, osl=osl, bank=bank: e.copy(osl, bank[:]),
                           waits=[("T", tW), ("G", W("G"))], inc="A")
                    aW = W("A")
                    pr.add("G", lambda e, osl=osl, m2=m2, cs=cs: e.dma_start(
                        out=ccw_i[m2 * 128:(m2 + 1) * 128, cs:cs + T], in_=osl),
                        waits=[("A", aW)], inc="G", amt=16)
            gW = W("G")
            pr.add("G", lambda e: e.collective_compute("AllReduce", AOP.add, replica_groups=RG,
                                                       ins=[ccw_i], outs=[ccw_o]),
                   waits=[("G", gW)], inc="G", amt=1)
            woG = W("G")
            if dbg:
                pr.add("G", lambda e: e.dma_start(out=dbg_d, in_=ccd_o), inc="G", amt=16)
                pr.add("G", lambda e: e.dma_start(out=dbg_w, in_=ccw_o), inc="G", amt=16)
            upd_v = {0: None, 1: None}
            upd_g = {0: 0, 1: 0}
            for c in range(NCH):
                cs = c * T
                for h2 in range(2):
                    hs = cs + h2 * 256
                    slot = (c * 2 + h2) % 2
                    off = slot * 2048
                    w0 = ([("V", upd_v[slot]), ("G", upd_g[slot])] if upd_v[slot] else [
                        ("V", W("V")), ("T", W("T")), ("A", W("A")), ("G", W("G"))])
                    pr.add("S", lambda e, hs=hs, off=off: e.dma_start(
                        out=AR1[:, off:off + 2048].rearrange("p (a t) -> p a t", a=8),
                        in_=xm_h[:, hs:hs + 256].rearrange("(a p) t -> p a t", p=128)),
                        waits=w0, inc="S", amt=16)
                    pr.add("S", lambda e, hs=hs, off=off: e.dma_start(
                        out=AR2[:, off:off + 2048].rearrange("p (a t) -> p a t", a=8),
                        in_=ccw_o[:, hs:hs + 256].rearrange("(a p) t -> p a t", p=128)),
                        waits=[("G", woG)] if (c == 0 and h2 == 0) else [], inc="S", amt=16)
                    sW = W("S")
                    pr.add("V", lambda e, off=off: e.tensor_tensor(
                        AR1[:, off:off + 2048], AR1[:, off:off + 2048], AR2[:, off:off + 2048], AOP.add),
                        waits=[("S", sW)], inc="V")
                    vW = W("V")
                    upd_v[slot] = vW
                    pr.add("G", lambda e, hs=hs, off=off: e.dma_start(
                        out=xm_h[:, hs:hs + 256].rearrange("(a p) t -> p a t", p=128),
                        in_=AR1[:, off:off + 2048].rearrange("p (a t) -> p a t", a=8)),
                        waits=[("V", vW)], inc="G", amt=16)
                    upd_g[slot] = W("G")

        xnh = AR4[:, 0:8192].bitcast(BF16)
        xnl = AR4[:, 8208:16400].bitcast(BF16)
        xmG = W("G")
        for tt in range(LH // 128):
            xsl = AR1[:, (tt % 2) * 1024:(tt % 2) * 1024 + 1024]
            pr.add("S", lambda e, xsl=xsl, tt=tt: e.dma_start(out=xsl, in_=x_half[tt * 128:(tt + 1) * 128, :]),
                   waits=[("V", W("V")), ("G", xmG), ("T", W("T")), ("A", W("A"))], inc="S", amt=16)
            sW = W("S")
            ln_pass_s(xsl, tt, [("S", sW)])
        aW_ln = ln_pass_m()
        for tt in range(LH // 128):
            xsl = AR1[:, (tt % 2) * 1024:(tt % 2) * 1024 + 1024]
            pr.add("S", lambda e, xsl=xsl, tt=tt: e.dma_start(out=xsl, in_=x_half[tt * 128:(tt + 1) * 128, :]),
                   waits=[("V", W("V"))], inc="S", amt=16)
            sW = W("S")
            xnsl = AR2[:, 0:1024]
            ln_pass_n(xsl, xnsl, tt, [("S", sW), ("A", aW_ln), ("T", W("T"))])
            vW = W("V")
            for a in range(8):
                bank = pdel if a % 2 == 0 else pbc
                pr.add("T", lambda e, a=a, bank=bank, xnsl=xnsl: e.transpose(
                    bank[:, 0:128], xnsl[:, a * 128:(a + 1) * 128], eye_sb[:]),
                    waits=[("V", vW), ("A", W("A"))], inc="T")
                tW = W("T")
                pr.add("A", lambda e, a=a, tt=tt, bank=bank: e.copy(
                    xnh[:, a * LH + tt * 128:a * LH + (tt + 1) * 128], bank[:, 0:128]),
                    waits=[("T", tW)], inc="A")
                aW2 = W("A")
                pr.add("V", lambda e, a=a, tt=tt, bank=bank: e.tensor_tensor(
                    xnl[:, a * LH + tt * 128:a * LH + (tt + 1) * 128], bank[:, 0:128],
                    xnh[:, a * LH + tt * 128:a * LH + (tt + 1) * 128], AOP.subtract),
                    waits=[("A", aW2)], inc="V")

        h1hi = AR2[:, 2048:3072].bitcast(BF16)
        h1lo = AR2[:, 3072:4096].bitcast(BF16)
        for m in range(32):
            slot = m % 2
            wsl = AR5[:, slot * 4096:slot * 4096 + 1024]
            pr.add("S", lambda e, wsl=wsl, m=m: e.dma_start(
                out=wsl.rearrange("p (a q) -> p a q", a=8),
                in_=w1T[:, m * 128:(m + 1) * 128].rearrange("(a p) q -> p a q", p=128)),
                waits=[("V", W("V"))], inc="S", amt=16)
            sW = W("S")
            wqs = wq[:, 2048 + slot * 1024:2048 + slot * 1024 + 1024]
            wqb = wq[:, slot * 512:slot * 512 + 512].bitcast(BF16)
            pr.add("V", lambda e, wsl=wsl, wqs=wqs: e.tensor_scalar_mul(wqs, wsl, gq[:, 0:1]),
                   waits=[("S", sW), ("A", g1A), ("T", W("T"))], inc="V")
            pr.add("V", lambda e, wqs=wqs, wsl=wsl: e.tensor_single_scalar(wsl, wqs, 0.5, AOP.is_gt), inc="V")
            pr.add("V", lambda e, wqs=wqs: e.tensor_single_scalar(wqs, wqs, -0.5, AOP.is_lt), inc="V")
            pr.add("V", lambda e, wqs=wqs, wsl=wsl, wqb=wqb: e.tensor_sub(wqb, wsl, wqs), inc="V")
            vW = W("V")
            for cg in range(4):
                bank = pA0 if cg % 2 == 0 else pA1
                for k in range(8):
                    for hl, xsrc in ((0, xnh), (1, xnl)):
                        pr.add("T", lambda e, k=k, cg=cg, bank=bank, wqb=wqb, xsrc=xsrc: e.matmul(
                            bank[:], wqb[:, k * 128:(k + 1) * 128],
                            xsrc[:, k * LH + cg * T:k * LH + (cg + 1) * T],
                            start=(k == 0 and xsrc is xnh), stop=(k == 7 and xsrc is xnl)),
                            waits=([("V", vW), ("A", W("A"))] if (k == 0 and hl == 0) else []), inc="T")
                tW = W("T")
                pr.add("A", lambda e, cg=cg, bank=bank, m=m: e.activation(
                    AR2[:, cg * T:(cg + 1) * T], bank[:], ACT.Relu, bias=b1_sb[:, m:m + 1]),
                    waits=[("T", tW), ("G", W("G"))], inc="A")
            aW = W("A")
            for cg in range(4):
                pr.add("A", lambda e, cg=cg: e.copy(h1hi[:, cg * T:(cg + 1) * T], AR2[:, cg * T:(cg + 1) * T]),
                       inc="A")
                aW2 = W("A")
                pr.add("V", lambda e, cg=cg: e.tensor_tensor(
                    h1lo[:, cg * T:(cg + 1) * T], AR2[:, cg * T:(cg + 1) * T],
                    h1hi[:, cg * T:(cg + 1) * T], AOP.subtract), waits=[("A", aW2)], inc="V")
            vW2 = W("V")
            pr.add("G", lambda e, m=m: e.dma_start(out=h1_h[m * 128:(m + 1) * 128, 0:LH], in_=h1hi[:]),
                   waits=[("V", vW2)], inc="G", amt=16)
            pr.add("G", lambda e, m=m: e.dma_start(out=h1_h[m * 128:(m + 1) * 128, LH:2 * LH], in_=h1lo[:]),
                   inc="G", amt=16)

        for m2 in range(8):
            w2sl = AR5[:, 4096:8192]
            pr.add("S", lambda e, w2sl=w2sl, m2=m2: e.dma_start(
                out=w2sl.rearrange("p (a q) -> p a q", a=32),
                in_=w2T[:, m2 * 128:(m2 + 1) * 128].rearrange("(a p) q -> p a q", p=128)),
                waits=[("V", W("V")), ("T", W("T"))], inc="S", amt=16)
            sW = W("S")
            wqs = AR3[:, 0:4096]
            wq2b = wq[:, 0:2048].bitcast(BF16)
            pr.add("V", lambda e, w2sl=w2sl, wqs=wqs: e.tensor_scalar_mul(wqs, w2sl, gq[:, 1:2]),
                   waits=[("S", sW), ("A", g2A), ("T", W("T"))], inc="V")
            pr.add("V", lambda e, wqs=wqs, w2sl=w2sl: e.tensor_single_scalar(w2sl[:, 0:4096], wqs, 0.5, AOP.is_gt), inc="V")
            pr.add("V", lambda e, wqs=wqs: e.tensor_single_scalar(wqs, wqs, -0.5, AOP.is_lt), inc="V")
            pr.add("V", lambda e, wqs=wqs, w2sl=w2sl, wq2b=wq2b: e.tensor_sub(wq2b, w2sl[:, 0:4096], wqs), inc="V")
            vW = W("V")
            hread = {0: 0, 1: 0}
            for kf in range(32):
                hslot = kf % 2
                hhi = AR5[:, hslot * 2048:hslot * 2048 + 1024].bitcast(BF16)
                hlo = AR5[:, hslot * 2048 + 1024:hslot * 2048 + 2048].bitcast(BF16)
                pr.add("S", lambda e, hhi=hhi, kf=kf: e.dma_start(
                    out=hhi, in_=h1_h[kf * 128:(kf + 1) * 128, 0:LH]),
                    waits=[("T", hread[hslot]), ("G", W("G")), ("V", vW)], inc="S", amt=16)
                pr.add("S", lambda e, hlo=hlo, kf=kf: e.dma_start(
                    out=hlo, in_=h1_h[kf * 128:(kf + 1) * 128, LH:2 * LH]),
                    inc="S", amt=16)
                sW = W("S")
                for cg in range(4):
                    bank = [pA0, pA1, pwo0, pwo1][cg]
                    for hl, hsrc in ((0, hhi), (1, hlo)):
                        pr.add("T", lambda e, kf=kf, cg=cg, bank=bank, hsrc=hsrc, wq2b=wq2b, hl=hl: e.matmul(
                            bank[:], wq2b[:, kf * 128:(kf + 1) * 128], hsrc[:, cg * T:(cg + 1) * T],
                            start=(kf == 0 and hl == 0), stop=(kf == 31 and hl == 1)),
                            waits=([("S", sW)] + ([("V", vW), ("A", W("A"))] if (kf == 0 and hl == 0) else [])), inc="T")
                hread[hslot] = W("T")
            tW = W("T")
            for cg in range(4):
                bank = [pA0, pA1, pwo0, pwo1][cg]
                pr.add("A", lambda e, m2=m2, cg=cg, bank=bank: e.activation(
                    AR4[:, m2 * LH + cg * T:m2 * LH + (cg + 1) * T], bank[:], ACT.Silu,
                    bias=b2_sb[:, m2:m2 + 1]), waits=([("T", tW)] if cg == 0 else []), inc="A")
        gA = W("A")

        outG = []
        for tt in range(LH // 128):
            xsl = AR1[:, (tt % 2) * 1024:(tt % 2) * 1024 + 1024]
            pr.add("S", lambda e, xsl=xsl, tt=tt: e.dma_start(out=xsl, in_=x_half[tt * 128:(tt + 1) * 128, :]),
                   waits=[("V", W("V")), ("T", W("T"))], inc="S", amt=16)
            pr.add("S", lambda e, tt=tt: e.dma_start(
                out=AR2[:, 0:1024].rearrange("p (a t) -> p a t", a=8),
                in_=xm_h[:, tt * 128:(tt + 1) * 128].rearrange("(a p) t -> p a t", p=128)),
                waits=[("V", W("V")), ("G", W("G"))], inc="S", amt=16)
            pr.add("S", lambda e, tt=tt: e.dma_start(
                out=AR2[:, 1024:2048].rearrange("p (a t) -> p a t", a=8),
                in_=xm_h[:, LH + tt * 128:LH + (tt + 1) * 128].rearrange("(a p) t -> p a t", p=128)),
                inc="S", amt=16)
            sW = W("S")
            pr.add("V", lambda e: e.tensor_scalar_mul(AR2[:, 2048:3072], AR2[:, 1024:2048], m1_sb[:, 0:1]),
                   waits=[("S", sW), ("A", gA)], inc="V")
            pr.add("V", lambda e: e.scalar_tensor_tensor(
                AR2[:, 3072:4096], AR2[:, 0:1024], m0_sb[:, 0:1], AR2[:, 2048:3072],
                AOP.mult, AOP.add), inc="V")
            for a in range(8):
                pr.add("V", lambda e, a=a, tt=tt: e.tensor_tensor(
                    AR2[:, a * 128:(a + 1) * 128], AR2[:, 3072 + a * 128:3072 + (a + 1) * 128],
                    AR4[:, a * LH + tt * 128:a * LH + (tt + 1) * 128], AOP.mult), inc="V")
            vW = W("V")
            for a in range(8):
                bank = pdel if a % 2 == 0 else pbc
                pr.add("T", lambda e, a=a, bank=bank: e.transpose(
                    bank[:, 0:128], AR2[:, a * 128:(a + 1) * 128], eye_sb[:]),
                    waits=[("V", vW), ("A", W("A"))], inc="T")
                tW = W("T")
                pr.add("A", lambda e, a=a, bank=bank, tt=tt: e.copy(
                    AR3[:, 2048 + (tt % 2) * 1024 + a * 128:2048 + (tt % 2) * 1024 + (a + 1) * 128],
                    bank[:, 0:128]),
                    waits=[("T", tW)] + ([("G", outG[tt - 2])] if tt >= 2 and a == 0 else []), inc="A")
            aW = W("A")
            osl = AR3[:, 2048 + (tt % 2) * 1024:2048 + (tt % 2) * 1024 + 1024]
            pr.add("V", lambda e, osl=osl, xsl=xsl: e.tensor_tensor(osl, osl, xsl, AOP.add),
                   waits=[("A", aW)], inc="V")
            vW = W("V")
            pr.add("G", lambda e, osl=osl, tt=tt: e.dma_start(out=out_t[tt * 128:(tt + 1) * 128, :], in_=osl),
                   waits=[("V", vW)], inc="G", amt=16)
            outG.append(W("G"))

        pr.add("A", lambda e: e.copy(sc1[:, 0:1], sc1[:, 0:1]),
               waits=[("G", W("G")), ("S", W("S"))], inc="A")

        cross_waits = {s: set() for s in SEMS}
        for (eng, fn, waits, inc, amt) in pr.steps:
            for (s, v) in waits:
                cross_waits[s].add((v, eng))
        cnt_replay = dict.fromkeys(SEMS, 0)
        needs_drain = []
        for (eng, fn, waits, inc, amt) in pr.steps:
            nd = False
            if inc is not None:
                lo = cnt_replay[inc]
                cnt_replay[inc] += amt
                hi = cnt_replay[inc]
                if eng in ("V", "A"):
                    for (v, weng) in cross_waits[inc]:
                        if weng != eng and lo < v <= hi:
                            nd = True
                            break
            needs_drain.append(nd)

        def emitter(name):
            def go(E):
                for i, (eng, fn, waits, inc, amt) in enumerate(pr.steps):
                    if eng != name:
                        continue
                    for (s, v) in waits:
                        E.wait_ge(sems[s], v)
                    inst = fn(E)
                    if inc is not None:
                        inst.then_inc(sems[inc], amt)
            return go

        blk.sync(emitter("S"))
        blk.vector(emitter("V"))
        blk.scalar(emitter("A"))
        blk.tensor(emitter("T"))
        blk.gpsimd(emitter("G"))

    return nc


def kernel(L0_attn_x, L0_attn_norm_w, L0_attn_W_in, L0_attn_conv_w, L0_attn_conv_b,
           L0_attn_W_x, L0_attn_W_dt, L0_attn_b_dt, L0_attn_A_log, L0_attn_D,
           L0_attn_W_out, L0_ffn_W1, L0_ffn_b1, L0_ffn_W2, L0_ffn_b2):
    f32 = np.float32
    x = np.ascontiguousarray(np.asarray(L0_attn_x, dtype=f32))

    def tile128(a):
        dl, k = a.shape
        return np.ascontiguousarray(a.reshape(KT, 128, k).transpose(1, 0, 2).reshape(128, KT * k))

    if "prog" not in _NC_CACHE:
        _NC_CACHE["prog"] = build_nc()
    nc = _NC_CACHE["prog"]

    in_maps = []
    for c in range(8):
        b, half = c // 2, c % 2
        r0 = half * DL
        rows = slice(r0, r0 + DL)
        m = {
            "x_full": np.ascontiguousarray(x[b]),
            "x_half": np.ascontiguousarray(x[b, half * LH:(half + 1) * LH]),
            "w1T": np.ascontiguousarray(np.asarray(L0_ffn_W1, f32).T),
            "w2T": np.ascontiguousarray(np.asarray(L0_ffn_W2, f32).T),
            "b1c": np.ascontiguousarray(np.asarray(L0_ffn_b1, f32).reshape(32, 128).T),
            "b2c": np.ascontiguousarray(np.asarray(L0_ffn_b2, f32).reshape(8, 128).T),
            "ones1": np.ones((128, 128), f32),
            "onescol": np.ones((128, 1), f32),
            "eye": np.eye(128, dtype=f32),
            "m0c": np.full((128, 1), 1.0 if half == 0 else 0.0, f32),
            "m1c": np.full((128, 1), 1.0 if half == 1 else 0.0, f32),
            "cvals": np.tile(np.array([[EPS, 1.0]], f32), (128, 1)),
        }
        W_in = np.asarray(L0_attn_W_in, f32)
        W_x = np.asarray(L0_attn_W_x, f32)
        W_dt = np.asarray(L0_attn_W_dt, f32)
        W_out = np.asarray(L0_attn_W_out, f32)
        for i in range(NL):
            sel = np.concatenate([W_in[i][r0:r0 + DL], W_in[i][2048 + r0:2048 + r0 + DL]], axis=0)
            m[f"winT{i}"] = np.ascontiguousarray(sel.T)
            m[f"wxT{i}"] = np.ascontiguousarray(W_x[i][:, rows].T)
            m[f"wdtT{i}"] = np.ascontiguousarray(W_dt[i][rows].T)
            m[f"woutT{i}"] = np.ascontiguousarray(W_out[i][:, rows].T)
            m[f"convw{i}"] = tile128(np.asarray(L0_attn_conv_w, f32)[i][rows])
            m[f"convb{i}"] = tile128(np.asarray(L0_attn_conv_b, f32)[i][rows, None])
            m[f"bdt{i}"] = tile128(np.asarray(L0_attn_b_dt, f32)[i][rows, None])
            m[f"alog{i}"] = tile128(np.asarray(L0_attn_A_log, f32)[i][rows])
            m[f"dvec{i}"] = tile128(np.asarray(L0_attn_D, f32)[i][rows, None])
            m[f"normw{i}"] = np.ascontiguousarray(
                np.asarray(L0_attn_norm_w, f32)[i].reshape(8, 128).T)
        in_maps.append(m)

    res = run_bass_kernel_spmd(nc, in_maps, core_ids=list(range(8)))
    out = np.empty((4, L, DM), f32)
    for c in range(8):
        b, half = c // 2, c % 2
        out[b, half * LH:(half + 1) * LH] = res.results[c]["out"]
    return out
